# version 08
# baseline (speedup 1.0000x reference)
import numpy as np
import ml_dtypes
import concourse.bass as bass
import concourse.mybir as mybir
from concourse.bass_utils import run_bass_kernel_spmd

F32 = mybir.dt.float32
BF16 = mybir.dt.bfloat16
FP8 = mybir.dt.float8e4
U8 = mybir.dt.uint8
AF = mybir.ActivationFunctionType
ALU = mybir.AluOpType
AX = mybir.AxisListType

B, L, DM, FF, NS = 32, 2048, 1024, 4096, 64
NCORES = 8
BC = B // NCORES
NTILE = BC * (L // 128)
NCH = DM
NCK = L // 128
EPS = 1e-5
MAGIC = 12582912.0
MAGICM128 = 12582784.0
SHP = 257
SHLEN = 128 * SHP + 256

ENG = {"v": "vector", "a": "scalar", "t": "tensor", "g": "gpsimd",
       "gq": "gpsimd", "sq": "sync", "cq": "scalar", "tq": "gpsimd"}
ALIASES = {"v": ("v",), "a": ("a", "cq"), "t": ("t",),
           "g": ("g", "gq", "tq"), "sq": ("sq",)}


class Ctx:

    def __init__(self, nc, sems, who, marks):
        self.nc = nc
        self.sems = sems
        self.who = who
        self.aliases = set() if who is None else set(ALIASES[who])
        self.record = who is None
        self.marks = marks
        self.cnt = {k: 0 for k in sems}

    def dma(self, queue, sem, out, in_, **kw):
        self.cnt[sem] += 16
        if queue in self.aliases:
            getattr(self.nc, ENG[queue]).dma_start(out=out, in_=in_, **kw) \
                .then_inc(self.sems[sem], 16)
        return self.cnt[sem]

    def dmat(self, sem, out, in_):
        self.cnt[sem] += 16
        if "sq" in self.aliases:
            self.nc.sync.dma_start_transpose(out=out, in_=in_).then_inc(
                self.sems[sem], 16)
        return self.cnt[sem]

    def op(self, engine, fn, inc=True):
        if inc:
            self.cnt[engine] += 1
        if engine in self.aliases:
            ins = fn(getattr(self.nc, ENG[engine]))
            if inc:
                ins.then_inc(self.sems[engine], 1)
        return self.cnt[engine]

    def mark(self, label):
        if self.record:
            self.marks[label] = dict(self.cnt)

    def val(self, label, sem):
        m = self.marks.get(label)
        if m is None:
            if not self.record:
                raise KeyError(f"mark {label} missing")
            return 0
        return m[sem]

    def wait(self, engine, sem, label_or_val):
        v = (self.val(label_or_val, sem) if isinstance(label_or_val, str)
             else label_or_val)
        if v <= 0:
            return
        if engine in self.aliases:
            getattr(self.nc, ENG[engine]).wait_ge(self.sems[sem], v)

    def barrier(self, label):
        self.mark("bar_" + label)
        snap = (dict(self.cnt) if self.record else self.marks["bar_" + label])
        if self.who is not None:
            e = getattr(self.nc, ENG[self.who])
            for sem, v in snap.items():
                if v > 0:
                    e.wait_ge(self.sems[sem], v)


def build_program(ntile=NTILE, ngrp=32, k16g=8, debug=False, phases='all'):
    nc = bass.Bass("TRN2", target_bir_lowering=False,
                   detect_race_conditions=False)

    u = nc.dram_tensor("u", [BC, L, DM], F32, kind="ExternalInput").ap()
    w1 = nc.dram_tensor("w1", [FF, DM], F32, kind="ExternalInput").ap()
    b1 = nc.dram_tensor("b1", [FF], F32, kind="ExternalInput").ap()
    w2 = nc.dram_tensor("w2", [DM, FF], F32, kind="ExternalInput").ap()
    b2 = nc.dram_tensor("b2", [DM], F32, kind="ExternalInput").ap()
    a_log = nc.dram_tensor("a_log", [NS], F32, kind="ExternalInput").ap()
    d_vec = nc.dram_tensor("d_vec", [DM], F32, kind="ExternalInput").ap()
    c_re = nc.dram_tensor("c_re", [DM, NS], F32, kind="ExternalInput").ap()
    log_dt = nc.dram_tensor("log_dt", [DM], F32, kind="ExternalInput").ap()
    outh = nc.dram_tensor("out", [BC, L, DM], F32, kind="ExternalOutput").ap()

    dkind = {"kind": "ExternalOutput"} if debug else {}
    xsp = nc.dram_tensor("xsp", [NTILE, 128, DM], F32, **dkind).ap()
    ulnT = nc.dram_tensor("ulnT", [DM, BC, L], BF16, **dkind).ap()
    ysp = nc.dram_tensor("ysp", [NCH // 32, BC, NCK, 128, 32], F32, **dkind).ap()
    k16h = nc.dram_tensor("k16h", [NCH, 256], BF16, **dkind).ap()
    shb = nc.dram_tensor("shb", [NCH, SHLEN], BF16, **dkind).ap()
    dtAh = nc.dram_tensor("dtAh", [NCH, NS], F32, **dkind).ap()
    tmp64 = nc.dram_tensor("tmp64", [NS], F32).ap()
    tmp1 = nc.dram_tensor("tmp1", [4], F32).ap()

    T = nc.alloc_sbuf_tensor
    w1T = T("w1T", [128, 8, FF], FP8)
    w2T = T("w2T", [128, 32, DM], FP8)
    b2hrep = T("b2hrep", [128, DM], F32)
    dtA_sb = T("dtA_sb", [128, 8, NS], F32)
    cre_sb = T("cre_sb", [128, 8, NS], F32)
    d_sb = T("d_sb", [128, 8], F32)
    dtAT_sb = T("dtAT_sb", [64, DM], F32)
    rho_sb = T("rho_sb", [64, DM], F32)
    iotaL = T("iotaL", [128, 128], F32)
    iotaL1_64 = T("iotaL1_64", [64, 128], F32)
    revI = T("revI", [128, 1], F32)
    iotaP = T("iotaP", [128, 1], F32)
    ones_c = T("ones_c", [128, 1], F32)
    id128 = T("id128", [128, 128], F32)
    zt16 = T("zt16", [128, 128], BF16)
    invsw1 = T("invsw1", [128, 1], F32)
    invsw2 = T("invsw2", [128, 1], F32)
    ldt_sb = T("ldt_sb", [128, 8], F32)
    al_sb = T("al_sb", [1, NS], F32)
    narep = T("narep", [128, NS], F32)

    xt = [T(f"xt{i}", [128, DM], F32) for i in range(2)]
    sqdX = T("sqdX", [128, DM], F32)
    w1s = T("w1s", [128, 1], F32)
    B4h = T("B4h", [2, FF], BF16)
    B4l = T("B4l", [2, FF], BF16)
    id16 = T("id16", [128, 128], BF16)
    sxp = T("sxp", [128, 16], BF16)
    sxT = T("sxT", [2, 4, 128], BF16)
    qu8x = T("qu8x", [128, DM], U8)
    qx16 = T("qx16", [128, DM], BF16)
    qxT = [T(f"qxT{i}", [128, 8, 128], BF16) for i in range(2)]
    hb = T("hb", [128, 1024], F32)
    sqd = T("sqd", [128, 1024], F32)
    t2b = T("t2b", [128, DM], F32)
    gbuf = T("gbuf", [128, FF], F32)
    qu8 = T("qu8", [128, FF], U8)
    qg16 = T("qg16", [128, FF], BF16)
    qgT = [T(f"qgT{i}", [128, 32, 128], BF16) for i in range(2)]
    xnew = [T(f"xnew{i}", [128, DM], F32) for i in range(2)]
    uln16 = T("uln16", [128, DM], BF16)
    ulnTt = [T(f"ulnTt{i}", [128, 8, 128], BF16) for i in range(2)]
    cols = [T(f"cols{i}", [128, 64], F32) for i in range(4)]

    hstash = T("hstash", [64, 2176], F32)
    ppk = [qgT[i][:].rearrange("p a b -> p (a b)")[:, 0:512].bitcast(BF16)
           if False else qgT[i][:, 0:4, :].rearrange("p a b -> p (a b)")
           for i in range(2)]
    usl = [xnew[i][:].bitcast(BF16).rearrange("p (a b) -> p a b", b=64)
           for i in range(2)]
    t0t = [ulnTt[0][:, j, :] for j in range(8)]
    rrep = t2b[0:64, 128:256]
    scantmp = t2b[0:64, 0:128]
    e2b = [qu8[:].bitcast(BF16)[0:64, 128 * j:128 * (j + 1)] for j in range(8)]
    pexp = sqd[0:128, 0:512]
    dtarep = [xt[0][:, 0:512], xt[0][:, 512:1024]]
    crerep = [xt[1][:, 0:512], xt[1][:, 512:1024]]
    sstash = gbuf[0:64, 0:2048]
    hst16 = qg16[0:64, 0:2176]
    ystrip2 = gbuf[:, 2048:4096]
    k16t = hb[:, 0:128]
    en2 = hb[:, 128:256]
    en3 = hb[:, 256:384]
    k16b = uln16[:, 0:128]

    pAll = nc.alloc_psum_tensor("pAll", [128, 4096], F32)
    pq = [pAll[:, 0:1024], pAll[:, 1024:2048]]
    p2 = pAll[:, 2048:3072]
    pst = pAll[0:64, 0:512]
    ptr = pAll[0:64, 3072:3584]
    pyb = pAll[:, 2048:4096]
    pmean = pAll[0:1, 3072:3073]

    if debug:
        dbg_w1T = nc.dram_tensor("dbg_w1T", [128, 8, FF], U8,
                                 kind="ExternalOutput").ap()
        dbg_w2T = nc.dram_tensor("dbg_w2T", [128, 32, DM], U8,
                                 kind="ExternalOutput").ap()
        dbg_qx = nc.dram_tensor("dbg_qx", [128, DM], BF16,
                                kind="ExternalOutput").ap()
        dbg_qg = nc.dram_tensor("dbg_qg", [128, FF], BF16,
                                kind="ExternalOutput").ap()
        dbg_g = nc.dram_tensor("dbg_g", [128, FF], F32,
                               kind="ExternalOutput").ap()
        dbg_cols = nc.dram_tensor("dbg_cols", [128, 96], F32,
                                  kind="ExternalOutput").ap()
        dbg_xt0 = nc.dram_tensor("dbg_xt0", [128, DM], F32,
                                 kind="ExternalOutput").ap()
        dbg_xt1 = nc.dram_tensor("dbg_xt1", [128, DM], F32,
                                 kind="ExternalOutput").ap()
    sems = {k: nc.alloc_semaphore("sem_" + k)
            for k in ("v", "a", "t", "g",
                      "xt0", "xt1", "os0", "os1", "ul0", "ul1", "k16s",
                      "shw", "ys", "tx", "tg", "tu", "us0", "us1",
                      "t0a", "t0b", "t0c", "t0d", "t0e", "t0f", "t0g", "t0h", "rp0", "rp1", "p0l",
                      "gb", "cqm")}

    CN = ["s1", "s2", "mx", "mn", "m", "ms", "msq", "var", "vpe", "rec", "inv",
          "ra", "ra2", "rec2", "rmsc", "d1", "d2", "mdev", "mr", "mr2", "mrc",
          "recm", "sx", "al", "tm", "beta", "rsx", "c1", "g2s", "gmax", "rme",
          "rec3", "rmsi", "gm2", "gmc", "recg", "sh", "alh", "rsh", "c2",
          "c2h", "sc0", "sc1", "sc2", "sc3", "acc", "tmpc", "mean",
          "s1u", "s2u", "mu", "msu", "msqu", "varu", "vpeu", "recu", "invu"]
    COL = {n: i for i, n in enumerate(CN)}
    assert len(CN) <= 64

    def program(E: Ctx):
        SD = AX.X

        def ts(en, o, i0, s1, s2, o0, o1=None, acc=None, inc=True):
            kw = {}
            if o1 is not None:
                kw["op1"] = o1
            if acc is not None:
                kw["accum_out"] = acc
            return E.op(en, lambda e: e.tensor_scalar(
                out=o, in0=i0, scalar1=s1, scalar2=s2, op0=o0, **kw),
                inc=True)

        def tt(en, o, i0, i1, o0, inc=True):
            return E.op(en, lambda e: e.tensor_tensor(
                out=o, in0=i0, in1=i1, op=o0), inc=True)

        def VD():
            E.op("v", lambda e: e.tensor_scalar(
                out=cols[0][:, 63:64], in0=ones_c[:], scalar1=0.0,
                scalar2=None, op0=ALU.mult), inc=True)

        def act(o, i0, fn, scale=None, bias=None, acc=None, inc=True):
            kw = {}
            if scale is not None:
                kw["scale"] = scale
            if bias is not None:
                kw["bias"] = bias
            if acc is not None:
                kw["accum_out"] = acc
            return E.op("a", lambda e: e.activation(
                out=o, in_=i0, func=fn, **kw), inc=inc)

        def cp(en, o, i0, inc=True):
            return E.op(en, lambda e: e.tensor_copy(out=o, in_=i0), inc=True)

        def rcp(o, i0, inc=True):
            return E.op("v", lambda e: e.reciprocal(out=o, in_=i0), inc=True)

        def mm(o, lhsT, rhs, start, stop, inc=False):
            return E.op("t", lambda e: e.matmul(
                o, lhsT, rhs, start=start, stop=stop, skip_group_check=True),
                inc=inc)

        def col(name, buf=0):
            return cols[buf][:, COL[name]:COL[name] + 1]

        E.op("g", lambda e: e.iota(iotaL[:], pattern=[[1, 128]], base=0,
                                   channel_multiplier=0,
                                   allow_small_or_imprecise_dtypes=True))
        E.op("g", lambda e: e.iota(iotaL1_64[:], pattern=[[1, 128]], base=1,
                                   channel_multiplier=0,
                                   allow_small_or_imprecise_dtypes=True))
        E.op("g", lambda e: e.iota(revI[:], pattern=[[0, 1]], base=127,
                                   channel_multiplier=-1,
                                   allow_small_or_imprecise_dtypes=True))
        E.op("g", lambda e: e.iota(iotaP[:], pattern=[[0, 1]], base=0,
                                   channel_multiplier=1,
                                   allow_small_or_imprecise_dtypes=True))
        E.op("g", lambda e: e.iota(ones_c[:], pattern=[[0, 1]], base=1,
                                   channel_multiplier=0,
                                   allow_small_or_imprecise_dtypes=True))
        E.mark("p0_iota")

        E.dma("gq", "p0l", xt[0][:], b2.unsqueeze(0).broadcast_to((128, DM)))
        E.dma("gq", "p0l", ldt_sb[:], log_dt.rearrange("(g p) -> p g", p=128), allow_slow_non_contiguous=True)
        E.dma("gq", "p0l", al_sb[:], a_log.unsqueeze(0))
        E.dma("gq", "p0l", d_sb[:], d_vec.rearrange("(g p) -> p g", p=128), allow_slow_non_contiguous=True)
        E.dma("gq", "p0l", cre_sb[:], c_re.rearrange("(g p) n -> p g n", p=128))
        E.mark("p0_loads")

        E.wait("v", "g", "p0_iota")
        E.wait("v", "p0l", "p0_loads")
        ts("v", id128[:], iotaL[:], iotaP[:], None, ALU.is_equal)
        VD()
        cp("v", id16[:], id128[:])
        ts("v", b2hrep[:], xt[0][:], 0.5, None, ALU.mult)
        E.op("v", lambda e: e.memset(zt16[:], 0.0))
        E.mark("p0_v1")
        E.wait("a", "p0l", "p0_loads")
        dtl = act(ldt_sb[:], ldt_sb[:], AF.Exp)
        aexp = act(al_sb[:], al_sb[:], AF.Exp)
        E.wait("v", "a", aexp)
        ts("v", al_sb[:], al_sb[:], -1.0, None, ALU.mult)
        E.mark("p0_na")
        E.wait("cq", "v", "p0_na")
        w64 = E.dma("cq", "cqm", tmp64.unsqueeze(0), al_sb[:])
        E.wait("cq", "cqm", w64)
        E.dma("cq", "cqm", narep[:], tmp64.unsqueeze(0).broadcast_to((128, NS)))
        E.mark("p0_narep")
        E.wait("v", "cqm", "p0_narep")
        E.wait("v", "a", dtl)
        for gi in range(8):
            ts("v", dtA_sb[:, gi, :], narep[:], ldt_sb[:, gi:gi + 1], None,
               ALU.mult, inc=(gi == 7))
        E.mark("p0_dta")
        E.wait("cq", "v", "p0_dta")
        E.dma("cq", "cqm", dtAh.rearrange("(g p) n -> p g n", p=128), dtA_sb[:])
        E.wait("t", "v", "p0_dta")
        for gi in range(8):
            if gi > 0:
                E.wait("t", "v", f"p0_dtat_{gi - 1}")
            tr = E.op("t", lambda e, gi=gi: e.transpose(
                ptr[0:64, 0:128], dtA_sb[:, gi, :], id128[:]), inc=True)
            E.wait("v", "t", tr)
            cp("v", dtAT_sb[:, gi * 128:(gi + 1) * 128], ptr[0:64, 0:128])
            E.mark(f"p0_dtat_{gi}")
        E.mark("p0_dtat")
        E.wait("a", "v", "p0_dtat")
        act(rho_sb[:], dtAT_sb[:], AF.Exp, scale=128.0)
        E.mark("p0_rho")

        def wprep(wap, nslab, sldm, wT, invsw, label):
            bufs = ([gbuf, gbuf] if sldm == 4096 else xt)
            qb = qg16[:, 0:sldm] if sldm == 4096 else uln16[:]
            stg = qgT[0] if sldm == 4096 else ulnTt[0]
            acc_col = col("acc", 1)
            tmpc = col("tmpc", 1)
            E.wait("gq", "v", "p0_v1")
            if label == "w2p":
                E.wait("gq", "v", "b4_done")
            ts("v", acc_col, ones_c[:], 0.0, None, ALU.mult)
            for sl in range(nslab):
                buf = bufs[sl % 2][:, 0:sldm]
                bsem = "gb" if sldm == 4096 else f"xt{sl % 2}"
                back = 1 if sldm == 4096 else 2
                if sl >= back:
                    E.wait("gq", "v", f"{label}_rd_{sl - back}")
                ld = E.dma("gq", bsem, buf, wap[sl * 128:(sl + 1) * 128, :])
                E.wait("v", bsem, ld)
                E.op("v", lambda e, b=buf: e.reduce_sum(
                    out=tmpc, in_=b, axis=SD, apply_absolute_value=True),
                    inc=False)
                VD()
                tt("v", acc_col, acc_col, tmpc, ALU.add)
                E.mark(f"{label}_rd_{sl}")
            E.wait("t", "v", f"{label}_rd_{nslab - 1}")
            mmv = E.op("t", lambda e: e.matmul(
                pmean, ones_c[:], acc_col, start=True, stop=True), inc=True)
            E.wait("v", "t", mmv)
            ts("v", cols[1][0:1, COL["mean"]:COL["mean"] + 1], pmean,
               1.0 / (FF * DM), 1e-5, ALU.mult, ALU.max)
            E.mark(f"{label}_mcl")
            E.wait("cq", "v", f"{label}_mcl")
            wt1 = E.dma("cq", "cqm", tmp1[0:1].unsqueeze(0),
                        cols[1][0:1, COL["mean"]:COL["mean"] + 1])
            E.wait("cq", "cqm", wt1)
            rb = E.dma("cq", "cqm", invsw[:],
                       tmp1[0:1].unsqueeze(0).broadcast_to((128, 1)))
            E.wait("v", "cqm", rb)
            rcp(tmpc, invsw[:])
            VD()
            if label == "w1p":
                cp("v", w1s[:], tmpc)
            E.mark(f"{label}_scale")
            for sl in range(nslab):
                buf = bufs[sl % 2][:, 0:sldm]
                bsem = "gb" if sldm == 4096 else f"xt{sl % 2}"
                if sl > 0:
                    E.wait("gq", "v", f"{label}_q_{sl - 1}")
                ld = E.dma("gq", bsem, buf, wap[sl * 128:(sl + 1) * 128, :])
                E.wait("v", bsem, ld)
                if sl > 0:
                    E.wait("v", "tx", f"{label}_tp_{sl - 1}")
                ts("v", buf, buf, tmpc, 1.4, ALU.mult, ALU.min, inc=False)
                ts("v", buf, buf, -1.4, MAGIC, ALU.max, ALU.add, inc=False)
                ts("v", qu8[:, 0:sldm], buf, MAGICM128, None, ALU.subtract,
                   inc=False)
                ts("v", qb, qu8[:, 0:sldm], -128.0, None, ALU.add)
                E.mark(f"{label}_q_{sl}")
                E.wait("sq", "v", f"{label}_q_{sl}")
                if sl > 0:
                    E.wait("sq", "v", f"{label}_cast_{sl - 1}")
                E.dmat("tx", stg[:], qb)
                E.mark(f"{label}_tp_{sl}")
                E.wait("v", "tx", f"{label}_tp_{sl}")
                cp("v", wT[:, :, sl * 128:(sl + 1) * 128], stg[:])
                E.mark(f"{label}_cast_{sl}")

        wprep(w1, 32, 1024, w1T, invsw1, "w1p")
        stage2 = gbuf[0:2, 0:4096]
        b1ld = E.dma("gq", "p0l", stage2,
                     b1.unsqueeze(0).broadcast_to((2, FF)))
        E.wait("v", "p0l", b1ld)
        ts("v", stage2, stage2, w1s[0:2, 0:1], None, ALU.mult)
        VD()
        cp("v", B4h[:], stage2)
        VD()
        tt("v", stage2, stage2, B4h[:], ALU.subtract)
        VD()
        cp("v", B4l[:], stage2)
        E.mark("b4_done")
        wprep(w2, 8, 4096, w2T, invsw2, "w2p")
        E.barrier("p0")

        def ffn_front(ph, i):
            bse, cse = i // NCK, i % NCK
            pf = f"{ph}_{i}"
            cI = lambda n: col(n, i % 4)
            if i >= 4:
                E.wait("v", "a", f"{ph}_ev2_{i - 4}")
            if i >= 2:
                E.wait("gq", "v", f"{ph}_xtfree_{i - 2}")
            xsem = f"xt{i % 2}"
            if ph == "f1":
                ld = E.dma("gq", xsem, xt[i % 2][:],
                           u[bse, cse * 128:(cse + 1) * 128, :])
            else:
                pl = E.dma("gq", xsem, xt[i % 2][:], xsp[i, :, :])
                E.wait("gq", xsem, pl)
                ld = E.dma("gq", xsem,
                           xt[i % 2][:].rearrange("p (g ch) -> p g ch", ch=32),
                           ysp.transpose((3, 0, 4, 1, 2))[:, :, :, bse, cse],
                           accum_op=ALU.add)
            E.mark(f"{pf}_xt")
            E.wait("v", xsem, ld)
            E.op("v", lambda e: e.reduce_sum(out=cI("s1"), in_=xt[i % 2][:],
                                             axis=SD), inc=False)
            E.op("v", lambda e: e.reduce_max(out=cI("mx"), in_=xt[i % 2][:],
                                             axis=SD), inc=False)
            E.op("v", lambda e: e.tensor_reduce(
                out=cI("mn"), in_=xt[i % 2][:], axis=SD, op=ALU.min))
            E.mark(f"{pf}_vst")
            E.wait("a", xsem, ld)
            if i >= 1:
                E.wait("a", "v", f"{ph}_sqdXfree_{i - 1}")
            act(sqdX[:], xt[i % 2][:], AF.Square, acc=cI("s2"))
            E.mark(f"{pf}_sqa")
            E.wait("v", "a", f"{pf}_sqa")
            ts("v", cI("m"), cI("s1"), 1.0 / DM, None, ALU.mult, inc=False)
            ts("v", cI("ms"), cI("s2"), 1.0 / DM, None, ALU.mult, inc=False)
            tt("v", cI("msq"), cI("m"), cI("m"), ALU.mult, inc=False)
            VD()
            tt("v", cI("var"), cI("ms"), cI("msq"), ALU.subtract, inc=False)
            VD()
            ts("v", cI("vpe"), cI("var"), EPS, None, ALU.add, inc=False)
            VD()
            rcp(cI("rec"), cI("vpe"))
            E.mark(f"{pf}_r1")
            VD()
            tt("v", cI("ra"), cI("var"), cI("rec"), ALU.mult, inc=False)
            VD()
            ts("v", cI("ra2"), cI("ra"), EPS, None, ALU.add, inc=False)
            VD()
            rcp(cI("rec2"), cI("ra2"))
            E.mark(f"{pf}_r2")
            E.wait("a", "v", f"{pf}_r2")
            act(cI("inv"), cI("rec"), AF.Sqrt, inc=False)
            act(cI("rmsc"), cI("rec2"), AF.Sqrt)
            E.mark(f"{pf}_sq2")
            E.wait("v", "a", f"{pf}_sq2")
            tt("v", cI("d1"), cI("mx"), cI("m"), ALU.subtract, inc=False)
            tt("v", cI("d2"), cI("m"), cI("mn"), ALU.subtract, inc=False)
            VD()
            tt("v", cI("mdev"), cI("d1"), cI("d2"), ALU.max, inc=False)
            VD()
            tt("v", cI("mr"), cI("mdev"), cI("inv"), ALU.mult, inc=False)
            VD()
            tt("v", cI("mr2"), cI("mr"), cI("rmsc"), ALU.mult, inc=False)
            VD()
            ts("v", cI("mrc"), cI("mr2"), 1e-5, None, ALU.max, inc=False)
            VD()
            rcp(cI("recm"), cI("mrc"), inc=False)
            VD()
            ts("v", cI("sx"), cI("recm"), 127.0, None, ALU.mult, inc=False)
            VD()
            tt("v", cI("al"), cI("inv"), cI("rmsc"), ALU.mult, inc=False)
            VD()
            tt("v", cI("al"), cI("al"), cI("sx"), ALU.mult, inc=False)
            VD()
            tt("v", cI("tm"), cI("m"), cI("al"), ALU.mult, inc=False)
            VD()
            ts("v", cI("beta"), cI("tm"), -1.0, None, ALU.mult, inc=False)
            rcp(cI("rsx"), cI("sx"), inc=False)
            VD()
            tt("v", cI("c1"), cI("rsx"), invsw1[:], ALU.mult)
            sl0 = (i % 4) * 4
            cp("v", sxp[:, sl0:sl0 + 1], cI("sx"))
            VD()
            tt("v", cI("tm"), cI("sx"), sxp[:, sl0:sl0 + 1], ALU.subtract)
            VD()
            cp("v", sxp[:, sl0 + 1:sl0 + 2], cI("tm"))
            E.mark(f"{pf}_cols")
            E.wait("t", "v", f"{pf}_cols")
            if i >= 1:
                E.wait("t", "v", f"{ph}_sxT_{i - 1}")
            E.op("t", lambda e: e.transpose(
                pAll[0:2, 3072:3136].bitcast(BF16),
                sxp[:, sl0:sl0 + 2], id16[:]), inc=True)
            E.mark(f"{pf}_sxtr")
            E.wait("v", "t", f"{pf}_sxtr")
            cp("v", sxT[:, i % 4, :],
               pAll[0:2, 3072:3136].bitcast(BF16)[:, 0:128])
            E.mark(f"{ph}_sxT_{i}")
            if i >= 1:
                E.wait("v", "tx", f"{ph}_qxT_{i - 1}")
            ts("v", sqdX[:], xt[i % 2][:], cI("al"), cI("beta"), ALU.mult,
               ALU.add, inc=False)
            ts("v", qu8x[:], sqdX[:], MAGIC, MAGICM128, ALU.add,
               ALU.subtract, inc=False)
            ts("v", qx16[:], qu8x[:], -128.0, None, ALU.add)
            E.mark(f"{ph}_sqdXfree_{i}")
            E.wait("sq", "v", f"{ph}_sqdXfree_{i}")
            if i >= 2:
                E.wait("sq", "t", f"{ph}_mm1d_{i - 2}")
            E.dmat("tx", qxT[i % 2][:], qx16[:])
            E.mark(f"{ph}_qxT_{i}")

        def ffn_back1(ph, i):
            bse, cse = i // NCK, i % NCK
            pf = f"{ph}_{i}"
            cI = lambda n: col(n, i % 4)
            E.wait("t", "tx", f"{ph}_qxT_{i}")
            E.wait("t", "v", f"{pf}_cols")
            E.wait("t", "v", f"{ph}_sxT_{i}")
            for q in range(4):
                prev = (f"{ph}_ev1_{i}_{q - 2}" if q >= 2 else
                        (f"{ph}_ev1_{i - 1}_{q + 2}" if i >= 1 else None))
                if prev is not None:
                    E.wait("t", "a", prev)
                for dt_ in range(8):
                    for hh in range(2):
                        mm(pq[q % 2][:, hh * 512:(hh + 1) * 512],
                           qxT[i % 2][:, dt_, :],
                           w1T[:, dt_, q * 1024 + hh * 512:
                               q * 1024 + (hh + 1) * 512],
                           start=(dt_ == 0), stop=False, inc=False)
                for hh in range(2):
                    mm(pq[q % 2][:, hh * 512:(hh + 1) * 512],
                       sxT[:, i % 4, :],
                       B4h[:, q * 1024 + hh * 512:q * 1024 + (hh + 1) * 512],
                       start=False, stop=False, inc=False)
                    mm(pq[q % 2][:, hh * 512:(hh + 1) * 512],
                       sxT[:, i % 4, :],
                       B4l[:, q * 1024 + hh * 512:q * 1024 + (hh + 1) * 512],
                       start=False, stop=True, inc=(hh == 1))
                E.mark(f"{ph}_mm1_{i}_{q}")
                E.wait("a", "t", f"{ph}_mm1_{i}_{q}")
                E.wait("a", "v", f"{pf}_cols")
                if q == 0 and i >= 1:
                    E.wait("a", "v", f"{ph}_qgq_{i - 1}")
                act(gbuf[:, q * 1024:(q + 1) * 1024], pq[q % 2][:], AF.Gelu,
                    scale=cI("c1"))
                E.mark(f"{ph}_ev1_{i}_{q}")
                act(sqd[:], gbuf[:, q * 1024:(q + 1) * 1024], AF.Square,
                    acc=cI("sc%d" % q))
                E.mark(f"{ph}_gsq_{i}_{q}")
                if q == 3:
                    E.mark(f"{ph}_mm1d_{i}")
            E.mark(f"{ph}_back1_{i}")

        def ffn_back2(ph, i):
            bse, cse = i // NCK, i % NCK
            pf = f"{ph}_{i}"
            cI = lambda n: col(n, i % 4)
            E.wait("v", "a", f"{ph}_gsq_{i}_3")
            tt("v", cI("g2s"), cI("sc0"), cI("sc1"), ALU.add, inc=False)
            tt("v", cI("tm"), cI("sc2"), cI("sc3"), ALU.add, inc=False)
            VD()
            tt("v", cI("g2s"), cI("g2s"), cI("tm"), ALU.add, inc=False)
            E.op("v", lambda e: e.reduce_max(
                out=cI("gmax"), in_=gbuf[:], axis=SD,
                apply_absolute_value=True), inc=False)
            ts("v", cI("rme"), cI("g2s"), 1.0 / FF, EPS, ALU.mult, ALU.add,
               inc=False)
            VD()
            rcp(cI("rec3"), cI("rme"))
            E.mark(f"{pf}_r3")
            E.wait("a", "v", f"{pf}_r3")
            act(cI("rmsi"), cI("rec3"), AF.Sqrt)
            E.mark(f"{pf}_s3")
            E.wait("v", "a", f"{pf}_s3")
            tt("v", cI("gm2"), cI("gmax"), cI("rmsi"), ALU.mult, inc=False)
            VD()
            ts("v", cI("gmc"), cI("gm2"), 1e-5, None, ALU.max, inc=False)
            VD()
            rcp(cI("recg"), cI("gmc"), inc=False)
            VD()
            ts("v", cI("sh"), cI("recg"), 127.0, None, ALU.mult, inc=False)
            VD()
            tt("v", cI("alh"), cI("rmsi"), cI("sh"), ALU.mult, inc=False)
            rcp(cI("rsh"), cI("sh"), inc=False)
            VD()
            tt("v", cI("c2"), cI("rsh"), invsw2[:], ALU.mult, inc=False)
            VD()
            ts("v", cI("c2h"), cI("c2"), 0.5, None, ALU.mult)
            E.mark(f"{pf}_cols2")
            if i >= 1:
                E.wait("v", "tg", f"{ph}_qgT_{i - 1}")
            ts("v", gbuf[:], gbuf[:], cI("alh"), MAGIC, ALU.mult, ALU.add,
               inc=False)
            ts("v", qu8[:], gbuf[:], MAGICM128, None, ALU.subtract,
               inc=False)
            ts("v", qg16[:], qu8[:], -128.0, None, ALU.add)
            E.mark(f"{ph}_qgq_{i}")
            E.wait("sq", "v", f"{ph}_qgq_{i}")
            if i >= 2:
                E.wait("sq", "t", f"{ph}_mm2_{i - 2}")
            E.dmat("tg", qgT[i % 2][:], qg16[:])
            E.mark(f"{ph}_qgT_{i}")

        def ffn_back2b(ph, i):
            bse, cse = i // NCK, i % NCK
            pf = f"{ph}_{i}"
            cI = lambda n: col(n, i % 4)
            E.wait("t", "tg", f"{ph}_qgT_{i}")
            if i >= 1:
                E.wait("t", "a", f"{ph}_ev2_{i - 1}")
            for ft in range(32):
                for dc in range(2):
                    mm(p2[:, dc * 512:(dc + 1) * 512],
                       qgT[i % 2][:, ft, :],
                       w2T[:, ft, dc * 512:(dc + 1) * 512],
                       start=(ft == 0), stop=(ft == 31),
                       inc=(ft == 31 and dc == 1))
            E.mark(f"{ph}_mm2_{i}")
            E.wait("a", "t", f"{ph}_mm2_{i}")
            if i >= 1:
                E.wait("a", "v", f"{ph}_t3_{i - 1}")
            act(t2b[:], p2[:], AF.Identity, scale=cI("c2h"))
            E.mark(f"{ph}_ev2_{i}")
            E.wait("v", "a", f"{ph}_ev2_{i}")
            tt("v", t2b[:], t2b[:], b2hrep[:], ALU.add)
            E.mark(f"{ph}_t3_{i}")
            if i >= 2:
                E.wait("v", f"os{i % 2}", f"{ph}_ost_{i - 2}")
            tt("v", xnew[i % 2][:], xt[i % 2][:], t2b[:], ALU.add)
            E.mark(f"{ph}_xn_{i}")
            E.mark(f"{ph}_xtfree_{i}")
            E.wait("gq", "v", f"{ph}_xn_{i}")
            if ph == "f1":
                E.dma("gq", f"os{i % 2}", xsp[i, :, :], xnew[i % 2][:])
            else:
                E.dma("gq", f"os{i % 2}",
                      outh[bse, cse * 128:(cse + 1) * 128, :],
                      xnew[i % 2][:])
            E.mark(f"{ph}_ost_{i}")
            if ph == "f1":
                E.op("v", lambda e: e.reduce_sum(
                    out=cI("s1u"), in_=xnew[i % 2][:], axis=SD), inc=False)
                E.wait("a", "v", f"{ph}_xn_{i}")
                act(sqd[:], xnew[i % 2][:], AF.Square, acc=cI("s2u"))
                E.mark(f"{pf}_sqa2")
                E.wait("v", "a", f"{pf}_sqa2")
                ts("v", cI("mu"), cI("s1u"), 1.0 / DM, None, ALU.mult,
                   inc=False)
                ts("v", cI("msu"), cI("s2u"), 1.0 / DM, None, ALU.mult,
                   inc=False)
                tt("v", cI("msqu"), cI("mu"), cI("mu"), ALU.mult, inc=False)
                VD()
                tt("v", cI("varu"), cI("msu"), cI("msqu"), ALU.subtract,
                   inc=False)
                VD()
                ts("v", cI("vpeu"), cI("varu"), EPS, None, ALU.add, inc=False)
                VD()
                rcp(cI("recu"), cI("vpeu"))
                E.mark(f"{pf}_r4")
                E.wait("a", "v", f"{pf}_r4")
                act(cI("invu"), cI("recu"), AF.Sqrt)
                E.mark(f"{pf}_s4")
                E.wait("v", "a", f"{pf}_s4")
                if i >= 1:
                    E.wait("v", "tu", f"{ph}_ulT_{i - 1}")
                ts("v", uln16[:], xnew[i % 2][:], cI("mu"), cI("invu"),
                   ALU.subtract, ALU.mult)
                E.mark(f"{pf}_uln")
                E.wait("sq", "v", f"{pf}_uln")
                if i >= 2:
                    E.wait("sq", f"ul{i % 2}", f"{ph}_ulst_{i - 2}")
                E.dmat("tu", ulnTt[i % 2][:], uln16[:])
                E.mark(f"{ph}_ulT_{i}")
                E.wait("gq", "tu", f"{ph}_ulT_{i}")
                E.dma("gq", f"ul{i % 2}",
                      ulnT.rearrange("(dt dp) b (c s) -> dp dt b c s",
                                     dp=128, s=128)[:, :, bse, cse, :],
                      ulnTt[i % 2][:])
                E.mark(f"{ph}_ulst_{i}")
            E.mark(f"{ph}_done_{i}")

        for i in range(ntile + 2):
            if i >= 2:
                ffn_back2("f1", i - 2)
            if 1 <= i <= ntile:
                ffn_back1("f1", i - 1)
            if i >= 2:
                ffn_back2b("f1", i - 2)
            if i < ntile:
                ffn_front("f1", i)
        E.barrier("f1")
        if phases == "f1":
            if debug:
                E.dma("gq", "ys", dbg_w1T.bitcast(FP8), w1T[:])
                E.dma("gq", "ys", dbg_w2T.bitcast(FP8), w2T[:])
                E.dma("gq", "ys", dbg_qx, qx16[:])
                E.dma("gq", "ys", dbg_qg, qg16[:])
                E.dma("gq", "ys", dbg_g, gbuf[:])
                E.dma("gq", "ys", dbg_cols[:, 0:64], cols[0][:])
                E.dma("gq", "ys", dbg_cols[:, 64:96], cols[1][:, 0:32])
                E.dma("gq", "ys", dbg_xt0, xt[0][:])
                E.dma("gq", "ys", dbg_xt1, xt[1][:])
            E.wait("g", "ys", E.cnt["ys"])
            E.wait("g", "os0", E.cnt["os0"])
            E.wait("g", "os1", E.cnt["os1"])
            return

        def k16_group(gi):
            E.op("v", lambda e: e.memset(k16t[:], 0.0))
            E.mark(f"k16_z_{gi}")
            E.wait("a", "v", f"k16_z_{gi}")
            en3s = [en3, hb[:, 384:512]]
            for n in range(NS):
                act(en2[:], iotaL[:], AF.Exp, scale=dtA_sb[:, gi, n:n + 1],
                    inc=True)
                E.mark(f"k16_e_{gi}_{n}")
                E.wait("v", "a", f"k16_e_{gi}_{n}")
                ts("v", en3s[n % 2][:], en2[:], cre_sb[:, gi, n:n + 1], None,
                   ALU.mult, inc=True)
                E.mark(f"k16_m_{gi}_{n}")
                if n >= 1:
                    tt("v", k16t[:], k16t[:], en3s[(n - 1) % 2][:], ALU.add,
                       inc=False)
                if n + 1 < NS:
                    E.wait("a", "v", f"k16_m_{gi}_{n}")
            tt("v", k16t[:], k16t[:], en3s[(NS - 1) % 2][:], ALU.add,
               inc=False)
            VD()
            tt("v", k16t[:, 0:1], k16t[:, 0:1], d_sb[:, gi:gi + 1], ALU.add,
               inc=False)
            VD()
            VD()
            cp("v", k16b[:], k16t[:])
            E.mark(f"k16_b_{gi}")
            E.wait("gq", "v", f"k16_b_{gi}")
            E.dma("gq", "k16s", k16h[gi * 128:(gi + 1) * 128, 128:256],
                  k16b[:])
            E.dma("gq", "k16s", k16h[gi * 128:(gi + 1) * 128, 0:128],
                  zt16[:])
            E.mark(f"k16_st_{gi}")
            E.wait("cq", "k16s", f"k16_st_{gi}")
            for sub in range(8):
                m0 = gi * 128 + sub * 16
                wv = shb[m0:m0 + 16, 0:128 * SHP].rearrange(
                    "ch (s i) -> ch s i", i=SHP)[:, :, 0:256]
                E.dma("cq", "shw", wv, k16h[m0:m0 + 16, :].unsqueeze(1)
                      .broadcast_to((16, 128, 256)))
            E.mark(f"k16_sh_{gi}")

        NGRP = ngrp

        def load_uslices(grp):
            if grp >= 2:
                E.wait("sq", "t", f"ssm_tdone_{grp - 2}")
            m0 = grp * 32
            E.dmat(f"us{grp % 2}",
                   usl[grp % 2][:].rearrange("p a b -> p (a b)"),
                   ulnT[m0:m0 + 32].rearrange("ch b (c s) -> (ch b c) s",
                                              s=128))
            E.mark(f"ssm_usl_{grp}")

        k16_group(0)
        if k16g > 1:
            k16_group(1)
        load_uslices(0)
        if NGRP > 1:
            load_uslices(1)

        def t0view(mch):
            return shb[mch, 128:128 + 128 * 256].rearrange(
                "(s l) -> s l", l=256)[:, 0:128]

        hs4 = hstash.rearrange("p (blk b sl) -> p blk b sl", b=4, sl=17)
        h16_4 = hst16.rearrange("p (blk b sl) -> p blk b sl", b=4, sl=17)
        ss4 = sstash.rearrange("p (blk b c) -> p blk b c", b=4, c=16)
        rrep3 = rrep.rearrange("p (blk b) -> p blk b", b=4)
        stmp3 = scantmp.rearrange("p (blk b) -> p blk b", b=4)

        for grp in range(NGRP):
            m0 = grp * 32
            if grp % 4 == 0 and grp // 4 + 2 < k16g:
                k16_group(grp // 4 + 2)
            for pk in range(4):
                if pk % 2 == 0:
                    pp = f"ssm_pexp_{grp}_{pk - 1}" if pk > 0 else \
                        (f"ssm_pexp_{grp - 1}_3" if grp > 0 else None)
                    if pp is not None:
                        E.wait("cq", "a", pp)
                        E.wait("cq", "v", f"ssm_ppk_{grp}_{pk - 1}" if pk > 0
                               else f"ssm_ppk_{grp - 1}_3")
                    E.dma("cq", "rp0",
                          xt[0][:, 0:1024],
                          dtAh[m0 + pk * 8:m0 + pk * 8 + 16, :]
                          .rearrange("a n -> (a n)").unsqueeze(0)
                          .broadcast_to((128, 1024)))
                    E.dma("cq", "rp1",
                          xt[1][:, 0:1024],
                          c_re[m0 + pk * 8:m0 + pk * 8 + 16, :]
                          .rearrange("a n -> (a n)").unsqueeze(0)
                          .broadcast_to((128, 1024)))
                    E.mark(f"ssm_rep_{grp}_{pk}")
                E.wait("a", "rp0", f"ssm_rep_{grp}_{pk - pk % 2}")
                ppv = f"ssm_ppk_{grp}_{pk - 1}" if pk > 0 else \
                    (f"ssm_ppk_{grp - 1}_3" if grp > 0 else None)
                if ppv is not None:
                    E.wait("a", "v", ppv)
                act(pexp[:], dtarep[pk % 2][:], AF.Exp, scale=revI[:])
                E.mark(f"ssm_pexp_{grp}_{pk}")
                E.wait("v", "a", f"ssm_pexp_{grp}_{pk}")
                smmp = f"ssm_smm_{grp}_{pk - 2}" if pk >= 2 else \
                    (f"ssm_smm_{grp - 1}_{pk + 2}" if grp > 0 else None)
                if smmp is not None:
                    E.wait("v", "t", smmp)
                E.wait("v", "rp1", f"ssm_rep_{grp}_{pk - pk % 2}")
                tt("v", ppk[pk % 2][:], pexp[:], crerep[pk % 2][:], ALU.mult)
                E.mark(f"ssm_ppk_{grp}_{pk}")
                E.wait("t", "v", f"ssm_ppk_{grp}_{pk}")
                E.wait("t", f"us{grp % 2}", f"ssm_usl_{grp}")
                scp = f"ssm_scp_{grp}_{pk - 1}" if pk > 0 else \
                    (f"ssm_scp_{grp - 1}_3" if grp > 0 else None)
                if scp is not None:
                    E.wait("t", "v", scp)
                for ch in range(8):
                    q = pk * 8 + ch
                    mm(pst[:, ch * 64:(ch + 1) * 64],
                       ppk[pk % 2][:, ch * 64:(ch + 1) * 64],
                       usl[grp % 2][:, q, :], start=True, stop=True,
                       inc=(ch == 7))
                E.mark(f"ssm_smm_{grp}_{pk}")
                E.wait("v", "t", f"ssm_smm_{grp}_{pk}")
                cp("v", sstash[:, pk * 512:(pk + 1) * 512], pst[:])
                E.mark(f"ssm_scp_{grp}_{pk}")
            if grp > 0:
                E.wait("v", "t", f"ssm_emm_{grp - 1}_31")
            E.op("v", lambda e: e.memset(hstash[:], 0.0), inc=False)
            for bb in range(4):
                cp("v", rrep3[:, :, bb], rho_sb[:, m0:m0 + 32],
                   inc=(bb == 3))
            VD()
            E.mark(f"ssm_rrep_{grp}")
            for c in range(NCK):
                tt("v", stmp3[:, 0:16, :], hs4[:, 0:16, :, c],
                   rrep3[:, 0:16, :], ALU.mult, inc=False)
                tt("v", stmp3[:, 16:32, :], hs4[:, 16:32, :, c],
                   rrep3[:, 16:32, :], ALU.mult, inc=False)
                tt("v", hs4[:, 0:16, :, c + 1], stmp3[:, 0:16, :],
                   ss4[:, 0:16, :, c], ALU.add, inc=False)
                tt("v", hs4[:, 16:32, :, c + 1], stmp3[:, 16:32, :],
                   ss4[:, 16:32, :, c], ALU.add, inc=(c == NCK - 1))
            VD()
            cp("v", hst16[:], hstash[:])
            E.mark(f"ssm_scan_{grp}")
            for sub in range(4):
                t0sem = ["t0a", "t0b"][sub % 2]
                t0prev = (f"ssm_t0c_{grp}_{(sub - 2) * 8 + 7}" if sub >= 2
                          else (f"ssm_t0c_{grp - 1}_{(sub + 2) * 8 + 7}"
                                if grp > 0 else None))
                if t0prev is not None:
                    E.wait("gq", "t", t0prev)
                E.wait("gq", "shw", f"k16_sh_{(m0 + sub * 8) // 128}")
                E.dma("gq", t0sem, ulnTt[sub % 2][:],
                      shb[m0 + sub * 8:m0 + sub * 8 + 8,
                          128:128 + 128 * 256]
                      .rearrange("ch (s l) -> s ch l", l=256)[:, :, 0:128])
                E.mark(f"ssm_t0_{grp}_{sub}")
                for j in range(8):
                    ch = sub * 8 + j
                    mch = m0 + ch
                    e2p = (f"ssm_emm_{grp}_{ch - 8}" if ch >= 8 else
                           (f"ssm_emm_{grp - 1}_{ch + 24}" if grp > 0
                            else None))
                    if e2p is not None:
                        E.wait("a", "t", e2p)
                    act(e2b[ch % 8][:], iotaL1_64[:], AF.Exp,
                        scale=dtAT_sb[:, mch:mch + 1])
                    E.mark(f"ssm_e2_{grp}_{ch}")
                for j in range(8):
                    ch = sub * 8 + j
                    E.wait("t", t0sem, f"ssm_t0_{grp}_{sub}")
                    if sub == 0 and j == 0:
                        E.wait("t", "v", f"ssm_scan_{grp}")
                        yev = (f"ssm_yev_{grp - 1}" if grp > 0 else None)
                        if yev is not None:
                            E.wait("t", "v", yev)
                    mm(pyb[:, ch * 64:(ch + 1) * 64],
                       ulnTt[sub % 2][:, j, :], usl[grp % 2][:, ch, :],
                       start=(j == 0), stop=False, inc=True)
                    E.mark(f"ssm_t0c_{grp}_{ch}")
                for j in range(8):
                    ch = sub * 8 + j
                    E.wait("t", "a", f"ssm_e2_{grp}_{ch}")
                    rhsH = h16_4[:, ch, :, 0:16]
                    mm(pyb[:, ch * 64:(ch + 1) * 64],
                       e2b[ch % 8][:], rhsH, start=False, stop=(j == 7),
                       inc=True)
                    E.mark(f"ssm_emm_{grp}_{ch}")
            E.wait("v", "t", f"ssm_emm_{grp}_31")
            if grp > 0:
                E.wait("v", "ys", f"ssm_yst_{grp - 1}")
            E.op("v", lambda e: e.tensor_copy(
                out=ystrip2.rearrange("p (j ch) -> p ch j", ch=32)
                if False else
                ystrip2.rearrange("p (j ch) -> p j ch", ch=32)
                .transpose((0, 2, 1)),
                in_=pyb.rearrange("p (ch j) -> p ch j", j=64)))
            E.mark(f"ssm_yev_{grp}")
            E.wait("gq", "v", f"ssm_yev_{grp}")
            E.dma("gq", "ys", ysp[grp].transpose((2, 0, 1, 3))
                  .rearrange("l b c ch -> l (b c) ch"),
                  ystrip2.rearrange("p (j ch) -> p j ch", ch=32))
            E.mark(f"ssm_yst_{grp}")
            E.mark(f"ssm_tdone_{grp}")
            if grp + 2 < NGRP:
                load_uslices(grp + 2)
        E.barrier("ssm")

        for i in range(ntile + 2):
            if i >= 2:
                ffn_back2("f2", i - 2)
            if 1 <= i <= ntile:
                ffn_back1("f2", i - 1)
            if i >= 2:
                ffn_back2b("f2", i - 2)
            if i < ntile:
                ffn_front("f2", i)
        E.wait("g", "os0", E.cnt["os0"])
        E.wait("g", "os1", E.cnt["os1"])
        E.wait("sq", "os0", E.cnt["os0"])
        E.wait("sq", "os1", E.cnt["os1"])

    marks = {}
    program(Ctx(nc, sems, None, marks))

    with nc.Block() as block:
        @block.sync
        def _(_e):
            program(Ctx(nc, sems, "sq", marks))

        @block.vector
        def _(_e):
            program(Ctx(nc, sems, "v", marks))

        @block.scalar
        def _(_e):
            program(Ctx(nc, sems, "a", marks))

        @block.tensor
        def _(_e):
            program(Ctx(nc, sems, "t", marks))

        @block.gpsimd
        def _(_e):
            program(Ctx(nc, sems, "g", marks))

    return nc


_NC_CACHE = []


def kernel(**inputs):
    ub = np.ascontiguousarray(inputs["L0_attn_u"], dtype=np.float32)
    shards = [ub[i * BC:(i + 1) * BC] for i in range(NCORES)]
    common = {
        "w1": np.ascontiguousarray(inputs["L0_ffn_w1"], dtype=np.float32),
        "b1": np.ascontiguousarray(inputs["L0_ffn_b1"], dtype=np.float32),
        "w2": np.ascontiguousarray(inputs["L0_ffn_w2"], dtype=np.float32),
        "b2": np.ascontiguousarray(inputs["L0_ffn_b2"], dtype=np.float32),
        "a_log": np.ascontiguousarray(inputs["L0_attn_A_log"],
                                      dtype=np.float32),
        "d_vec": np.ascontiguousarray(inputs["L0_attn_D"], dtype=np.float32),
        "c_re": np.ascontiguousarray(inputs["L0_attn_C_re"],
                                     dtype=np.float32),
        "log_dt": np.ascontiguousarray(inputs["L0_attn_log_dt"],
                                       dtype=np.float32),
    }
    if not _NC_CACHE:
        _NC_CACHE.append(build_program())
    nc = _NC_CACHE[0]
    in_maps = [dict(common, u=s) for s in shards]
    res = run_bass_kernel_spmd(nc, in_maps, core_ids=list(range(NCORES)))
    return np.concatenate([r["out"] for r in res.results], axis=0)


if __name__ == "__main__":
    nc = build_program()
    print("built ok")


# version 09
# speedup vs baseline: 1.1194x; 1.1194x over previous
import numpy as np
import ml_dtypes
import concourse.bass as bass
import concourse.mybir as mybir
from concourse.bass_utils import run_bass_kernel_spmd

F32 = mybir.dt.float32
BF16 = mybir.dt.bfloat16
FP8 = mybir.dt.float8e4
U8 = mybir.dt.uint8
AF = mybir.ActivationFunctionType
ALU = mybir.AluOpType
AX = mybir.AxisListType

B, L, DM, FF, NS = 32, 2048, 1024, 4096, 64
NCORES = 8
BC = B // NCORES
NTILE = BC * (L // 128)
NCH = DM
NCK = L // 128
EPS = 1e-5
MAGIC = 12582912.0
MAGICM128 = 12582784.0
SHP = 257
SHLEN = 128 * SHP + 256

ENG = {"v": "vector", "a": "scalar", "t": "tensor", "g": "gpsimd",
       "gq": "gpsimd", "sq": "sync", "cq": "scalar", "tq": "gpsimd"}
ALIASES = {"v": ("v",), "a": ("a", "cq"), "t": ("t",),
           "g": ("g", "gq", "tq"), "sq": ("sq",)}


class Ctx:

    def __init__(self, nc, sems, who, marks):
        self.nc = nc
        self.sems = sems
        self.who = who
        self.aliases = set() if who is None else set(ALIASES[who])
        self.record = who is None
        self.marks = marks
        self.cnt = {k: 0 for k in sems}

    def dma(self, queue, sem, out, in_, **kw):
        self.cnt[sem] += 16
        if queue in self.aliases:
            getattr(self.nc, ENG[queue]).dma_start(out=out, in_=in_, **kw) \
                .then_inc(self.sems[sem], 16)
        return self.cnt[sem]

    def dmat(self, sem, out, in_):
        self.cnt[sem] += 16
        if "sq" in self.aliases:
            self.nc.sync.dma_start_transpose(out=out, in_=in_).then_inc(
                self.sems[sem], 16)
        return self.cnt[sem]

    def op(self, engine, fn, inc=True):
        if inc:
            self.cnt[engine] += 1
        if engine in self.aliases:
            ins = fn(getattr(self.nc, ENG[engine]))
            if inc:
                ins.then_inc(self.sems[engine], 1)
        return self.cnt[engine]

    def mark(self, label):
        if self.record:
            self.marks[label] = dict(self.cnt)

    def val(self, label, sem):
        m = self.marks.get(label)
        if m is None:
            if not self.record:
                raise KeyError(f"mark {label} missing")
            return 0
        return m[sem]

    def wait(self, engine, sem, label_or_val):
        v = (self.val(label_or_val, sem) if isinstance(label_or_val, str)
             else label_or_val)
        if v <= 0:
            return
        if engine in self.aliases:
            getattr(self.nc, ENG[engine]).wait_ge(self.sems[sem], v)

    def barrier(self, label):
        self.mark("bar_" + label)
        snap = (dict(self.cnt) if self.record else self.marks["bar_" + label])
        if self.who is not None:
            e = getattr(self.nc, ENG[self.who])
            for sem, v in snap.items():
                if v > 0:
                    e.wait_ge(self.sems[sem], v)


def build_program(ntile=NTILE, ngrp=32, k16g=8, debug=False, phases='all'):
    nc = bass.Bass("TRN2", target_bir_lowering=False,
                   detect_race_conditions=False)

    u = nc.dram_tensor("u", [BC, L, DM], F32, kind="ExternalInput").ap()
    w1 = nc.dram_tensor("w1", [FF, DM], F32, kind="ExternalInput").ap()
    b1 = nc.dram_tensor("b1", [FF], F32, kind="ExternalInput").ap()
    w2 = nc.dram_tensor("w2", [DM, FF], F32, kind="ExternalInput").ap()
    b2 = nc.dram_tensor("b2", [DM], F32, kind="ExternalInput").ap()
    a_log = nc.dram_tensor("a_log", [NS], F32, kind="ExternalInput").ap()
    d_vec = nc.dram_tensor("d_vec", [DM], F32, kind="ExternalInput").ap()
    c_re = nc.dram_tensor("c_re", [DM, NS], F32, kind="ExternalInput").ap()
    log_dt = nc.dram_tensor("log_dt", [DM], F32, kind="ExternalInput").ap()
    outh = nc.dram_tensor("out", [BC, L, DM], F32, kind="ExternalOutput").ap()

    dkind = {"kind": "ExternalOutput"} if debug else {}
    xsp = nc.dram_tensor("xsp", [NTILE, 128, DM], F32, **dkind).ap()
    ulnT = nc.dram_tensor("ulnT", [DM, BC, L], BF16, **dkind).ap()
    ysp = nc.dram_tensor("ysp", [NCH // 32, BC, NCK, 128, 32], F32, **dkind).ap()
    k16h = nc.dram_tensor("k16h", [NCH, 256], BF16, **dkind).ap()
    shb = nc.dram_tensor("shb", [NCH, SHLEN], BF16, **dkind).ap()
    dtAh = nc.dram_tensor("dtAh", [NCH, NS], F32, **dkind).ap()
    tmp64 = nc.dram_tensor("tmp64", [NS], F32).ap()
    tmp1 = nc.dram_tensor("tmp1", [4], F32).ap()

    T = nc.alloc_sbuf_tensor
    w1T = T("w1T", [128, 8, FF], FP8)
    w2T = T("w2T", [128, 32, DM], FP8)
    b2hrep = T("b2hrep", [128, DM], F32)
    dtA_sb = T("dtA_sb", [128, 8, NS], F32)
    cre_sb = T("cre_sb", [128, 8, NS], F32)
    d_sb = T("d_sb", [128, 8], F32)
    dtAT_sb = T("dtAT_sb", [64, DM], F32)
    rho_sb = T("rho_sb", [64, DM], F32)
    iotaL = T("iotaL", [128, 128], F32)
    iotaL1_64 = T("iotaL1_64", [64, 128], F32)
    revI = T("revI", [128, 1], F32)
    iotaP = T("iotaP", [128, 1], F32)
    ones_c = T("ones_c", [128, 1], F32)
    id128 = T("id128", [128, 128], F32)
    zt16 = T("zt16", [128, 128], BF16)
    invsw1 = T("invsw1", [128, 1], F32)
    invsw2 = T("invsw2", [128, 1], F32)
    ldt_sb = T("ldt_sb", [128, 8], F32)
    al_sb = T("al_sb", [1, NS], F32)
    narep = T("narep", [128, NS], F32)

    xt = [T(f"xt{i}", [128, DM], F32) for i in range(4)]
    sqdX = T("sqdX", [128, DM], F32)
    w1s = T("w1s", [128, 1], F32)
    B4h = T("B4h", [2, FF], BF16)
    B4l = T("B4l", [2, FF], BF16)
    id16 = T("id16", [128, 128], BF16)
    sxp = T("sxp", [128, 16], BF16)
    sxT = T("sxT", [2, 4, 128], BF16)
    qu8x = T("qu8x", [128, DM], U8)
    qx16 = T("qx16", [128, DM], BF16)
    qxT = [T(f"qxT{i}", [128, 8, 128], BF16) for i in range(2)]
    hb = T("hb", [128, 1024], F32)
    sqd = T("sqd", [128, 1024], F32)
    t2b = T("t2b", [128, DM], F32)
    gbuf = T("gbuf", [128, FF], F32)
    qu8 = T("qu8", [128, FF], U8)
    qg16 = T("qg16", [128, FF], BF16)
    qgT = [T(f"qgT{i}", [128, 32, 128], BF16) for i in range(2)]
    xnew = [T(f"xnew{i}", [128, DM], F32) for i in range(2)]
    uln16 = T("uln16", [128, DM], BF16)
    ulnTt = [T(f"ulnTt{i}", [128, 8, 128], BF16) for i in range(2)]
    cols = [T(f"cols{i}", [128, 64], F32) for i in range(4)]

    hstash = T("hstash", [64, 2176], F32)
    ppk = [qgT[i][:].rearrange("p a b -> p (a b)")[:, 0:512].bitcast(BF16)
           if False else qgT[i][:, 0:4, :].rearrange("p a b -> p (a b)")
           for i in range(2)]
    usl = [xnew[i][:].bitcast(BF16).rearrange("p (a b) -> p a b", b=64)
           for i in range(2)]
    t0t = [ulnTt[0][:, j, :] for j in range(8)]
    rrep = t2b[0:64, 128:256]
    scantmp = t2b[0:64, 0:128]
    e2b = [qu8[:].bitcast(BF16)[0:64, 128 * j:128 * (j + 1)] for j in range(8)]
    pexp = sqd[0:128, 0:512]
    dtarep = [xt[0][:, 0:512], xt[0][:, 512:1024]]
    crerep = [xt[1][:, 0:512], xt[1][:, 512:1024]]
    sstash = gbuf[0:64, 0:2048]
    hst16 = qg16[0:64, 0:2176]
    ystrip2 = gbuf[:, 2048:4096]
    k16t = hb[:, 0:128]
    en2 = hb[:, 128:256]
    en3 = hb[:, 256:384]
    k16b = uln16[:, 0:128]

    pAll = nc.alloc_psum_tensor("pAll", [128, 4096], F32)
    pq = [pAll[:, 0:1024], pAll[:, 1024:2048]]
    p2 = pAll[:, 2048:3072]
    pst = pAll[0:64, 0:512]
    ptr = pAll[0:64, 3072:3584]
    pyb = pAll[:, 2048:4096]
    pmean = pAll[0:1, 3072:3073]

    if debug:
        dbg_w1T = nc.dram_tensor("dbg_w1T", [128, 8, FF], U8,
                                 kind="ExternalOutput").ap()
        dbg_w2T = nc.dram_tensor("dbg_w2T", [128, 32, DM], U8,
                                 kind="ExternalOutput").ap()
        dbg_qx = nc.dram_tensor("dbg_qx", [128, DM], BF16,
                                kind="ExternalOutput").ap()
        dbg_qg = nc.dram_tensor("dbg_qg", [128, FF], BF16,
                                kind="ExternalOutput").ap()
        dbg_g = nc.dram_tensor("dbg_g", [128, FF], F32,
                               kind="ExternalOutput").ap()
        dbg_cols = nc.dram_tensor("dbg_cols", [128, 96], F32,
                                  kind="ExternalOutput").ap()
        dbg_xt0 = nc.dram_tensor("dbg_xt0", [128, DM], F32,
                                 kind="ExternalOutput").ap()
        dbg_xt1 = nc.dram_tensor("dbg_xt1", [128, DM], F32,
                                 kind="ExternalOutput").ap()
    sems = {k: nc.alloc_semaphore("sem_" + k)
            for k in ("v", "a", "t", "g",
                      "xt0", "xt1", "os0", "os1", "ul0", "ul1", "k16s",
                      "shw", "ys", "tx", "tg", "tu", "us0", "us1",
                      "t0a", "t0b", "t0c", "t0d", "t0e", "t0f", "t0g", "t0h", "rp0", "rp1", "p0l",
                      "gb", "cqm")}

    CN = ["s1", "s2", "mx", "mn", "m", "ms", "msq", "var", "vpe", "rec", "inv",
          "ra", "ra2", "rec2", "rmsc", "d1", "d2", "mdev", "mr", "mr2", "mrc",
          "recm", "sx", "al", "tm", "beta", "rsx", "c1", "g2s", "gmax", "rme",
          "rec3", "rmsi", "gm2", "gmc", "recg", "sh", "alh", "rsh", "c2",
          "c2h", "sc0", "sc1", "sc2", "sc3", "acc", "tmpc", "mean",
          "s1u", "s2u", "mu", "msu", "msqu", "varu", "vpeu", "recu", "invu"]
    COL = {n: i for i, n in enumerate(CN)}
    assert len(CN) <= 64

    def program(E: Ctx):
        SD = AX.X

        def ts(en, o, i0, s1, s2, o0, o1=None, acc=None, inc=True):
            kw = {}
            if o1 is not None:
                kw["op1"] = o1
            if acc is not None:
                kw["accum_out"] = acc
            return E.op(en, lambda e: e.tensor_scalar(
                out=o, in0=i0, scalar1=s1, scalar2=s2, op0=o0, **kw),
                inc=True)

        def tt(en, o, i0, i1, o0, inc=True):
            return E.op(en, lambda e: e.tensor_tensor(
                out=o, in0=i0, in1=i1, op=o0), inc=True)

        def VD():
            E.op("v", lambda e: e.tensor_scalar(
                out=cols[0][:, 63:64], in0=ones_c[:], scalar1=0.0,
                scalar2=None, op0=ALU.mult), inc=True)

        def act(o, i0, fn, scale=None, bias=None, acc=None, inc=True):
            kw = {}
            if scale is not None:
                kw["scale"] = scale
            if bias is not None:
                kw["bias"] = bias
            if acc is not None:
                kw["accum_out"] = acc
            return E.op("a", lambda e: e.activation(
                out=o, in_=i0, func=fn, **kw), inc=inc)

        def cp(en, o, i0, inc=True):
            return E.op(en, lambda e: e.tensor_copy(out=o, in_=i0), inc=True)

        def rcp(o, i0, inc=True):
            return E.op("v", lambda e: e.reciprocal(out=o, in_=i0), inc=True)

        def mm(o, lhsT, rhs, start, stop, inc=False):
            return E.op("t", lambda e: e.matmul(
                o, lhsT, rhs, start=start, stop=stop, skip_group_check=True),
                inc=inc)

        def col(name, buf=0):
            return cols[buf][:, COL[name]:COL[name] + 1]

        E.op("g", lambda e: e.iota(iotaL[:], pattern=[[1, 128]], base=0,
                                   channel_multiplier=0,
                                   allow_small_or_imprecise_dtypes=True))
        E.op("g", lambda e: e.iota(iotaL1_64[:], pattern=[[1, 128]], base=1,
                                   channel_multiplier=0,
                                   allow_small_or_imprecise_dtypes=True))
        E.op("g", lambda e: e.iota(revI[:], pattern=[[0, 1]], base=127,
                                   channel_multiplier=-1,
                                   allow_small_or_imprecise_dtypes=True))
        E.op("g", lambda e: e.iota(iotaP[:], pattern=[[0, 1]], base=0,
                                   channel_multiplier=1,
                                   allow_small_or_imprecise_dtypes=True))
        E.op("g", lambda e: e.iota(ones_c[:], pattern=[[0, 1]], base=1,
                                   channel_multiplier=0,
                                   allow_small_or_imprecise_dtypes=True))
        E.mark("p0_iota")

        E.dma("gq", "p0l", xt[0][:], b2.unsqueeze(0).broadcast_to((128, DM)))
        E.dma("gq", "p0l", ldt_sb[:], log_dt.rearrange("(g p) -> p g", p=128), allow_slow_non_contiguous=True)
        E.dma("gq", "p0l", al_sb[:], a_log.unsqueeze(0))
        E.dma("gq", "p0l", d_sb[:], d_vec.rearrange("(g p) -> p g", p=128), allow_slow_non_contiguous=True)
        E.dma("gq", "p0l", cre_sb[:], c_re.rearrange("(g p) n -> p g n", p=128))
        E.mark("p0_loads")

        E.wait("v", "g", "p0_iota")
        E.wait("v", "p0l", "p0_loads")
        ts("v", id128[:], iotaL[:], iotaP[:], None, ALU.is_equal)
        VD()
        cp("v", id16[:], id128[:])
        ts("v", b2hrep[:], xt[0][:], 0.5, None, ALU.mult)
        E.op("v", lambda e: e.memset(zt16[:], 0.0))
        E.mark("p0_v1")
        E.wait("a", "p0l", "p0_loads")
        dtl = act(ldt_sb[:], ldt_sb[:], AF.Exp)
        aexp = act(al_sb[:], al_sb[:], AF.Exp)
        E.wait("v", "a", aexp)
        ts("v", al_sb[:], al_sb[:], -1.0, None, ALU.mult)
        E.mark("p0_na")
        E.wait("cq", "v", "p0_na")
        w64 = E.dma("cq", "cqm", tmp64.unsqueeze(0), al_sb[:])
        E.wait("cq", "cqm", w64)
        E.dma("cq", "cqm", narep[:], tmp64.unsqueeze(0).broadcast_to((128, NS)))
        E.mark("p0_narep")
        E.wait("v", "cqm", "p0_narep")
        E.wait("v", "a", dtl)
        for gi in range(8):
            ts("v", dtA_sb[:, gi, :], narep[:], ldt_sb[:, gi:gi + 1], None,
               ALU.mult, inc=(gi == 7))
        E.mark("p0_dta")
        E.wait("cq", "v", "p0_dta")
        E.dma("cq", "cqm", dtAh.rearrange("(g p) n -> p g n", p=128), dtA_sb[:])
        E.wait("t", "v", "p0_dta")
        for gi in range(8):
            if gi > 0:
                E.wait("t", "v", f"p0_dtat_{gi - 1}")
            tr = E.op("t", lambda e, gi=gi: e.transpose(
                ptr[0:64, 0:128], dtA_sb[:, gi, :], id128[:]), inc=True)
            E.wait("v", "t", tr)
            cp("v", dtAT_sb[:, gi * 128:(gi + 1) * 128], ptr[0:64, 0:128])
            E.mark(f"p0_dtat_{gi}")
        E.mark("p0_dtat")
        E.wait("a", "v", "p0_dtat")
        act(rho_sb[:], dtAT_sb[:], AF.Exp, scale=128.0)
        E.mark("p0_rho")

        def wprep(wap, nslab, sldm, wT, invsw, label):
            bufs = ([gbuf, gbuf] if sldm == 4096 else xt)
            qb = qg16[:, 0:sldm] if sldm == 4096 else uln16[:]
            stg = qgT[0] if sldm == 4096 else ulnTt[0]
            acc_col = col("acc", 1)
            tmpc = col("tmpc", 1)
            E.wait("gq", "v", "p0_v1")
            if label == "w2p":
                E.wait("gq", "v", "b4_done")
            ts("v", acc_col, ones_c[:], 0.0, None, ALU.mult)
            for sl in range(nslab):
                buf = bufs[sl % 2][:, 0:sldm]
                bsem = "gb" if sldm == 4096 else f"xt{sl % 2}"
                back = 1 if sldm == 4096 else 2
                if sl >= back:
                    E.wait("gq", "v", f"{label}_rd_{sl - back}")
                ld = E.dma("gq", bsem, buf, wap[sl * 128:(sl + 1) * 128, :])
                E.wait("v", bsem, ld)
                E.op("v", lambda e, b=buf: e.reduce_sum(
                    out=tmpc, in_=b, axis=SD, apply_absolute_value=True),
                    inc=False)
                VD()
                tt("v", acc_col, acc_col, tmpc, ALU.add)
                E.mark(f"{label}_rd_{sl}")
            E.wait("t", "v", f"{label}_rd_{nslab - 1}")
            mmv = E.op("t", lambda e: e.matmul(
                pmean, ones_c[:], acc_col, start=True, stop=True), inc=True)
            E.wait("v", "t", mmv)
            ts("v", cols[1][0:1, COL["mean"]:COL["mean"] + 1], pmean,
               1.0 / (FF * DM), 1e-5, ALU.mult, ALU.max)
            E.mark(f"{label}_mcl")
            E.wait("cq", "v", f"{label}_mcl")
            wt1 = E.dma("cq", "cqm", tmp1[0:1].unsqueeze(0),
                        cols[1][0:1, COL["mean"]:COL["mean"] + 1])
            E.wait("cq", "cqm", wt1)
            rb = E.dma("cq", "cqm", invsw[:],
                       tmp1[0:1].unsqueeze(0).broadcast_to((128, 1)))
            E.wait("v", "cqm", rb)
            rcp(tmpc, invsw[:])
            VD()
            if label == "w1p":
                cp("v", w1s[:], tmpc)
            E.mark(f"{label}_scale")
            for sl in range(nslab):
                buf = bufs[sl % 2][:, 0:sldm]
                bsem = "gb" if sldm == 4096 else f"xt{sl % 2}"
                if sl > 0:
                    E.wait("gq", "v", f"{label}_q_{sl - 1}")
                ld = E.dma("gq", bsem, buf, wap[sl * 128:(sl + 1) * 128, :])
                E.wait("v", bsem, ld)
                if sl > 0:
                    E.wait("v", "tx", f"{label}_tp_{sl - 1}")
                ts("v", buf, buf, tmpc, 1.4, ALU.mult, ALU.min, inc=False)
                ts("v", buf, buf, -1.4, MAGIC, ALU.max, ALU.add, inc=False)
                ts("v", qu8[:, 0:sldm], buf, MAGICM128, None, ALU.subtract,
                   inc=False)
                ts("v", qb, qu8[:, 0:sldm], -128.0, None, ALU.add)
                E.mark(f"{label}_q_{sl}")
                E.wait("sq", "v", f"{label}_q_{sl}")
                if sl > 0:
                    E.wait("sq", "v", f"{label}_cast_{sl - 1}")
                E.dmat("tx", stg[:], qb)
                E.mark(f"{label}_tp_{sl}")
                E.wait("v", "tx", f"{label}_tp_{sl}")
                cp("v", wT[:, :, sl * 128:(sl + 1) * 128], stg[:])
                E.mark(f"{label}_cast_{sl}")

        wprep(w1, 32, 1024, w1T, invsw1, "w1p")
        stage2 = gbuf[0:2, 0:4096]
        b1ld = E.dma("gq", "p0l", stage2,
                     b1.unsqueeze(0).broadcast_to((2, FF)))
        E.wait("v", "p0l", b1ld)
        ts("v", stage2, stage2, w1s[0:2, 0:1], None, ALU.mult)
        VD()
        cp("v", B4h[:], stage2)
        VD()
        tt("v", stage2, stage2, B4h[:], ALU.subtract)
        VD()
        cp("v", B4l[:], stage2)
        E.mark("b4_done")
        wprep(w2, 8, 4096, w2T, invsw2, "w2p")
        E.barrier("p0")

        def ffn_front(ph, i):
            bse, cse = i // NCK, i % NCK
            pf = f"{ph}_{i}"
            cI = lambda n: col(n, i % 4)
            if i >= 4:
                E.wait("v", "a", f"{ph}_ev2_{i - 4}")
            if i >= 4:
                E.wait("gq", "v", f"{ph}_xtfree_{i - 4}")
            xsem = f"xt{i % 2}"
            if ph == "f1":
                ld = E.dma("gq", xsem, xt[i % 4][:],
                           u[bse, cse * 128:(cse + 1) * 128, :])
            else:
                pl = E.dma("gq", xsem, xt[i % 4][:], xsp[i, :, :])
                E.wait("gq", xsem, pl)
                ld = E.dma("gq", xsem,
                           xt[i % 4][:].rearrange("p (g ch) -> p g ch", ch=32),
                           ysp.transpose((3, 0, 4, 1, 2))[:, :, :, bse, cse],
                           accum_op=ALU.add)
            E.mark(f"{pf}_xt")
            E.wait("v", xsem, ld)
            E.op("v", lambda e: e.reduce_sum(out=cI("s1"), in_=xt[i % 4][:],
                                             axis=SD), inc=False)
            E.op("v", lambda e: e.reduce_max(out=cI("mx"), in_=xt[i % 4][:],
                                             axis=SD), inc=False)
            E.op("v", lambda e: e.tensor_reduce(
                out=cI("mn"), in_=xt[i % 4][:], axis=SD, op=ALU.min))
            E.mark(f"{pf}_vst")
            E.wait("a", xsem, ld)
            if i >= 1:
                E.wait("a", "v", f"{ph}_sqdXfree_{i - 1}")
            act(sqdX[:], xt[i % 4][:], AF.Square, acc=cI("s2"))
            E.mark(f"{pf}_sqa")
            E.wait("v", "a", f"{pf}_sqa")
            ts("v", cI("m"), cI("s1"), 1.0 / DM, None, ALU.mult, inc=False)
            ts("v", cI("ms"), cI("s2"), 1.0 / DM, None, ALU.mult, inc=False)
            tt("v", cI("msq"), cI("m"), cI("m"), ALU.mult, inc=False)
            VD()
            tt("v", cI("var"), cI("ms"), cI("msq"), ALU.subtract, inc=False)
            VD()
            ts("v", cI("vpe"), cI("var"), EPS, None, ALU.add, inc=False)
            VD()
            rcp(cI("rec"), cI("vpe"))
            E.mark(f"{pf}_r1")
            VD()
            tt("v", cI("ra"), cI("var"), cI("rec"), ALU.mult, inc=False)
            VD()
            ts("v", cI("ra2"), cI("ra"), EPS, None, ALU.add, inc=False)
            VD()
            rcp(cI("rec2"), cI("ra2"))
            E.mark(f"{pf}_r2")
            E.wait("a", "v", f"{pf}_r2")
            act(cI("inv"), cI("rec"), AF.Sqrt, inc=False)
            act(cI("rmsc"), cI("rec2"), AF.Sqrt)
            E.mark(f"{pf}_sq2")
            E.wait("v", "a", f"{pf}_sq2")
            tt("v", cI("d1"), cI("mx"), cI("m"), ALU.subtract, inc=False)
            tt("v", cI("d2"), cI("m"), cI("mn"), ALU.subtract, inc=False)
            VD()
            tt("v", cI("mdev"), cI("d1"), cI("d2"), ALU.max, inc=False)
            VD()
            tt("v", cI("mr"), cI("mdev"), cI("inv"), ALU.mult, inc=False)
            VD()
            tt("v", cI("mr2"), cI("mr"), cI("rmsc"), ALU.mult, inc=False)
            VD()
            ts("v", cI("mrc"), cI("mr2"), 1e-5, None, ALU.max, inc=False)
            VD()
            rcp(cI("recm"), cI("mrc"), inc=False)
            VD()
            ts("v", cI("sx"), cI("recm"), 127.0, None, ALU.mult, inc=False)
            VD()
            tt("v", cI("al"), cI("inv"), cI("rmsc"), ALU.mult, inc=False)
            VD()
            tt("v", cI("al"), cI("al"), cI("sx"), ALU.mult, inc=False)
            VD()
            tt("v", cI("tm"), cI("m"), cI("al"), ALU.mult, inc=False)
            VD()
            ts("v", cI("beta"), cI("tm"), -1.0, None, ALU.mult, inc=False)
            rcp(cI("rsx"), cI("sx"), inc=False)
            VD()
            tt("v", cI("c1"), cI("rsx"), invsw1[:], ALU.mult)
            sl0 = (i % 4) * 4
            cp("v", sxp[:, sl0:sl0 + 1], cI("sx"))
            VD()
            tt("v", cI("tm"), cI("sx"), sxp[:, sl0:sl0 + 1], ALU.subtract)
            VD()
            cp("v", sxp[:, sl0 + 1:sl0 + 2], cI("tm"))
            E.mark(f"{pf}_cols")
            E.wait("t", "v", f"{pf}_cols")
            if i >= 1:
                E.wait("t", "v", f"{ph}_sxT_{i - 1}")
            E.op("t", lambda e: e.transpose(
                pAll[0:2, 3072:3136].bitcast(BF16),
                sxp[:, sl0:sl0 + 2], id16[:]), inc=True)
            E.mark(f"{pf}_sxtr")
            E.wait("v", "t", f"{pf}_sxtr")
            cp("v", sxT[:, i % 4, :],
               pAll[0:2, 3072:3136].bitcast(BF16)[:, 0:128])
            E.mark(f"{ph}_sxT_{i}")
            if i >= 1:
                E.wait("v", "tx", f"{ph}_qxT_{i - 1}")
            ts("v", sqdX[:], xt[i % 4][:], cI("al"), cI("beta"), ALU.mult,
               ALU.add, inc=False)
            ts("v", qu8x[:], sqdX[:], MAGIC, MAGICM128, ALU.add,
               ALU.subtract, inc=False)
            ts("v", qx16[:], qu8x[:], -128.0, None, ALU.add)
            E.mark(f"{ph}_sqdXfree_{i}")
            E.wait("sq", "v", f"{ph}_sqdXfree_{i}")
            if i >= 2:
                E.wait("sq", "t", f"{ph}_mm1d_{i - 2}")
            E.dmat("tx", qxT[i % 2][:], qx16[:])
            E.mark(f"{ph}_qxT_{i}")

        def ffn_back1(ph, i):
            bse, cse = i // NCK, i % NCK
            pf = f"{ph}_{i}"
            cI = lambda n: col(n, i % 4)
            E.wait("t", "tx", f"{ph}_qxT_{i}")
            E.wait("t", "v", f"{pf}_cols")
            E.wait("t", "v", f"{ph}_sxT_{i}")
            for q in range(4):
                prev = (f"{ph}_ev1_{i}_{q - 2}" if q >= 2 else
                        (f"{ph}_ev1_{i - 1}_{q + 2}" if i >= 1 else None))
                if prev is not None:
                    E.wait("t", "a", prev)
                for dt_ in range(8):
                    for hh in range(2):
                        mm(pq[q % 2][:, hh * 512:(hh + 1) * 512],
                           qxT[i % 2][:, dt_, :],
                           w1T[:, dt_, q * 1024 + hh * 512:
                               q * 1024 + (hh + 1) * 512],
                           start=(dt_ == 0), stop=False, inc=False)
                for hh in range(2):
                    mm(pq[q % 2][:, hh * 512:(hh + 1) * 512],
                       sxT[:, i % 4, :],
                       B4h[:, q * 1024 + hh * 512:q * 1024 + (hh + 1) * 512],
                       start=False, stop=False, inc=False)
                    mm(pq[q % 2][:, hh * 512:(hh + 1) * 512],
                       sxT[:, i % 4, :],
                       B4l[:, q * 1024 + hh * 512:q * 1024 + (hh + 1) * 512],
                       start=False, stop=True, inc=(hh == 1))
                E.mark(f"{ph}_mm1_{i}_{q}")
                E.wait("a", "t", f"{ph}_mm1_{i}_{q}")
                E.wait("a", "v", f"{pf}_cols")
                if q == 0 and i >= 1:
                    E.wait("a", "v", f"{ph}_qgq_{i - 1}")
                act(gbuf[:, q * 1024:(q + 1) * 1024], pq[q % 2][:], AF.Gelu,
                    scale=cI("c1"))
                E.mark(f"{ph}_ev1_{i}_{q}")
                act(sqd[:], gbuf[:, q * 1024:(q + 1) * 1024], AF.Square,
                    acc=cI("sc%d" % q))
                E.mark(f"{ph}_gsq_{i}_{q}")
                if q == 3:
                    E.mark(f"{ph}_mm1d_{i}")
            E.mark(f"{ph}_back1_{i}")

        def ffn_back2(ph, i):
            bse, cse = i // NCK, i % NCK
            pf = f"{ph}_{i}"
            cI = lambda n: col(n, i % 4)
            E.wait("v", "a", f"{ph}_gsq_{i}_3")
            tt("v", cI("g2s"), cI("sc0"), cI("sc1"), ALU.add, inc=False)
            tt("v", cI("tm"), cI("sc2"), cI("sc3"), ALU.add, inc=False)
            VD()
            tt("v", cI("g2s"), cI("g2s"), cI("tm"), ALU.add, inc=False)
            E.op("v", lambda e: e.reduce_max(
                out=cI("gmax"), in_=gbuf[:], axis=SD,
                apply_absolute_value=True), inc=False)
            ts("v", cI("rme"), cI("g2s"), 1.0 / FF, EPS, ALU.mult, ALU.add,
               inc=False)
            VD()
            rcp(cI("rec3"), cI("rme"))
            E.mark(f"{pf}_r3")
            E.wait("a", "v", f"{pf}_r3")
            act(cI("rmsi"), cI("rec3"), AF.Sqrt)
            E.mark(f"{pf}_s3")
            E.wait("v", "a", f"{pf}_s3")
            tt("v", cI("gm2"), cI("gmax"), cI("rmsi"), ALU.mult, inc=False)
            VD()
            ts("v", cI("gmc"), cI("gm2"), 1e-5, None, ALU.max, inc=False)
            VD()
            rcp(cI("recg"), cI("gmc"), inc=False)
            VD()
            ts("v", cI("sh"), cI("recg"), 127.0, None, ALU.mult, inc=False)
            VD()
            tt("v", cI("alh"), cI("rmsi"), cI("sh"), ALU.mult, inc=False)
            rcp(cI("rsh"), cI("sh"), inc=False)
            VD()
            tt("v", cI("c2"), cI("rsh"), invsw2[:], ALU.mult, inc=False)
            VD()
            ts("v", cI("c2h"), cI("c2"), 0.5, None, ALU.mult)
            E.mark(f"{pf}_cols2")
            if i >= 1:
                E.wait("v", "tg", f"{ph}_qgT_{i - 1}")
            ts("v", gbuf[:], gbuf[:], cI("alh"), MAGIC, ALU.mult, ALU.add,
               inc=False)
            ts("v", qu8[:], gbuf[:], MAGICM128, None, ALU.subtract,
               inc=False)
            ts("v", qg16[:], qu8[:], -128.0, None, ALU.add)
            E.mark(f"{ph}_qgq_{i}")
            E.wait("sq", "v", f"{ph}_qgq_{i}")
            if i >= 2:
                E.wait("sq", "t", f"{ph}_mm2_{i - 2}")
            E.dmat("tg", qgT[i % 2][:], qg16[:])
            E.mark(f"{ph}_qgT_{i}")

        def ffn_back2b(ph, i):
            bse, cse = i // NCK, i % NCK
            pf = f"{ph}_{i}"
            cI = lambda n: col(n, i % 4)
            E.wait("t", "tg", f"{ph}_qgT_{i}")
            if i >= 1:
                E.wait("t", "a", f"{ph}_ev2_{i - 1}")
            for ft in range(32):
                for dc in range(2):
                    mm(p2[:, dc * 512:(dc + 1) * 512],
                       qgT[i % 2][:, ft, :],
                       w2T[:, ft, dc * 512:(dc + 1) * 512],
                       start=(ft == 0), stop=(ft == 31),
                       inc=(ft == 31 and dc == 1))
            E.mark(f"{ph}_mm2_{i}")
            E.wait("a", "t", f"{ph}_mm2_{i}")
            if i >= 1:
                E.wait("a", "v", f"{ph}_t3_{i - 1}")
            act(t2b[:], p2[:], AF.Identity, scale=cI("c2h"))
            E.mark(f"{ph}_ev2_{i}")
            E.wait("v", "a", f"{ph}_ev2_{i}")
            tt("v", t2b[:], t2b[:], b2hrep[:], ALU.add)
            E.mark(f"{ph}_t3_{i}")
            if i >= 2:
                E.wait("v", f"os{i % 2}", f"{ph}_ost_{i - 2}")
            tt("v", xnew[i % 2][:], xt[i % 4][:], t2b[:], ALU.add)
            E.mark(f"{ph}_xn_{i}")
            E.mark(f"{ph}_xtfree_{i}")
            E.wait("gq", "v", f"{ph}_xn_{i}")
            if ph == "f1":
                E.dma("gq", f"os{i % 2}", xsp[i, :, :], xnew[i % 2][:])
            else:
                E.dma("gq", f"os{i % 2}",
                      outh[bse, cse * 128:(cse + 1) * 128, :],
                      xnew[i % 2][:])
            E.mark(f"{ph}_ost_{i}")
            if ph == "f1":
                E.op("v", lambda e: e.reduce_sum(
                    out=cI("s1u"), in_=xnew[i % 2][:], axis=SD), inc=False)
                E.wait("a", "v", f"{ph}_xn_{i}")
                act(sqd[:], xnew[i % 2][:], AF.Square, acc=cI("s2u"))
                E.mark(f"{pf}_sqa2")
                E.wait("v", "a", f"{pf}_sqa2")
                ts("v", cI("mu"), cI("s1u"), 1.0 / DM, None, ALU.mult,
                   inc=False)
                ts("v", cI("msu"), cI("s2u"), 1.0 / DM, None, ALU.mult,
                   inc=False)
                tt("v", cI("msqu"), cI("mu"), cI("mu"), ALU.mult, inc=False)
                VD()
                tt("v", cI("varu"), cI("msu"), cI("msqu"), ALU.subtract,
                   inc=False)
                VD()
                ts("v", cI("vpeu"), cI("varu"), EPS, None, ALU.add, inc=False)
                VD()
                rcp(cI("recu"), cI("vpeu"))
                E.mark(f"{pf}_r4")
                E.wait("a", "v", f"{pf}_r4")
                act(cI("invu"), cI("recu"), AF.Sqrt)
                E.mark(f"{pf}_s4")
                E.wait("v", "a", f"{pf}_s4")
                if i >= 1:
                    E.wait("v", "tu", f"{ph}_ulT_{i - 1}")
                ts("v", uln16[:], xnew[i % 2][:], cI("mu"), cI("invu"),
                   ALU.subtract, ALU.mult)
                E.mark(f"{pf}_uln")
                E.wait("sq", "v", f"{pf}_uln")
                if i >= 2:
                    E.wait("sq", f"ul{i % 2}", f"{ph}_ulst_{i - 2}")
                E.dmat("tu", ulnTt[i % 2][:], uln16[:])
                E.mark(f"{ph}_ulT_{i}")
                E.wait("gq", "tu", f"{ph}_ulT_{i}")
                E.dma("gq", f"ul{i % 2}",
                      ulnT.rearrange("(dt dp) b (c s) -> dp dt b c s",
                                     dp=128, s=128)[:, :, bse, cse, :],
                      ulnTt[i % 2][:])
                E.mark(f"{ph}_ulst_{i}")
            E.mark(f"{ph}_done_{i}")

        for i in range(ntile + 2):
            if i >= 2:
                ffn_back2("f1", i - 2)
            if 1 <= i <= ntile:
                ffn_back1("f1", i - 1)
            if i < ntile:
                ffn_front("f1", i)
            if i >= 2:
                ffn_back2b("f1", i - 2)
        E.barrier("f1")
        if phases == "f1":
            if debug:
                E.dma("gq", "ys", dbg_w1T.bitcast(FP8), w1T[:])
                E.dma("gq", "ys", dbg_w2T.bitcast(FP8), w2T[:])
                E.dma("gq", "ys", dbg_qx, qx16[:])
                E.dma("gq", "ys", dbg_qg, qg16[:])
                E.dma("gq", "ys", dbg_g, gbuf[:])
                E.dma("gq", "ys", dbg_cols[:, 0:64], cols[0][:])
                E.dma("gq", "ys", dbg_cols[:, 64:96], cols[1][:, 0:32])
                E.dma("gq", "ys", dbg_xt0, xt[0][:])
                E.dma("gq", "ys", dbg_xt1, xt[1][:])
            E.wait("g", "ys", E.cnt["ys"])
            E.wait("g", "os0", E.cnt["os0"])
            E.wait("g", "os1", E.cnt["os1"])
            return

        def k16_group(gi):
            E.op("v", lambda e: e.memset(k16t[:], 0.0))
            E.mark(f"k16_z_{gi}")
            E.wait("a", "v", f"k16_z_{gi}")
            en3s = [en3, hb[:, 384:512]]
            for n in range(NS):
                act(en2[:], iotaL[:], AF.Exp, scale=dtA_sb[:, gi, n:n + 1],
                    inc=True)
                E.mark(f"k16_e_{gi}_{n}")
                E.wait("v", "a", f"k16_e_{gi}_{n}")
                ts("v", en3s[n % 2][:], en2[:], cre_sb[:, gi, n:n + 1], None,
                   ALU.mult, inc=True)
                E.mark(f"k16_m_{gi}_{n}")
                if n >= 1:
                    tt("v", k16t[:], k16t[:], en3s[(n - 1) % 2][:], ALU.add,
                       inc=False)
                if n + 1 < NS:
                    E.wait("a", "v", f"k16_m_{gi}_{n}")
            tt("v", k16t[:], k16t[:], en3s[(NS - 1) % 2][:], ALU.add,
               inc=False)
            VD()
            tt("v", k16t[:, 0:1], k16t[:, 0:1], d_sb[:, gi:gi + 1], ALU.add,
               inc=False)
            VD()
            VD()
            cp("v", k16b[:], k16t[:])
            E.mark(f"k16_b_{gi}")
            E.wait("gq", "v", f"k16_b_{gi}")
            E.dma("gq", "k16s", k16h[gi * 128:(gi + 1) * 128, 128:256],
                  k16b[:])
            E.dma("gq", "k16s", k16h[gi * 128:(gi + 1) * 128, 0:128],
                  zt16[:])
            E.mark(f"k16_st_{gi}")
            E.wait("cq", "k16s", f"k16_st_{gi}")
            for sub in range(8):
                m0 = gi * 128 + sub * 16
                wv = shb[m0:m0 + 16, 0:128 * SHP].rearrange(
                    "ch (s i) -> ch s i", i=SHP)[:, :, 0:256]
                E.dma("cq", "shw", wv, k16h[m0:m0 + 16, :].unsqueeze(1)
                      .broadcast_to((16, 128, 256)))
            E.mark(f"k16_sh_{gi}")

        NGRP = ngrp

        def load_uslices(grp):
            if grp >= 2:
                E.wait("sq", "t", f"ssm_tdone_{grp - 2}")
            m0 = grp * 32
            E.dmat(f"us{grp % 2}",
                   usl[grp % 2][:].rearrange("p a b -> p (a b)"),
                   ulnT[m0:m0 + 32].rearrange("ch b (c s) -> (ch b c) s",
                                              s=128))
            E.mark(f"ssm_usl_{grp}")

        k16_group(0)
        if k16g > 1:
            k16_group(1)
        load_uslices(0)
        if NGRP > 1:
            load_uslices(1)

        def t0view(mch):
            return shb[mch, 128:128 + 128 * 256].rearrange(
                "(s l) -> s l", l=256)[:, 0:128]

        hs4 = hstash.rearrange("p (blk b sl) -> p blk b sl", b=4, sl=17)
        h16_4 = hst16.rearrange("p (blk b sl) -> p blk b sl", b=4, sl=17)
        ss4 = sstash.rearrange("p (blk b c) -> p blk b c", b=4, c=16)
        rrep3 = rrep.rearrange("p (blk b) -> p blk b", b=4)
        stmp3 = scantmp.rearrange("p (blk b) -> p blk b", b=4)

        for grp in range(NGRP):
            m0 = grp * 32
            if grp % 4 == 0 and grp // 4 + 2 < k16g:
                k16_group(grp // 4 + 2)
            for pk in range(4):
                if pk % 2 == 0:
                    pp = f"ssm_pexp_{grp}_{pk - 1}" if pk > 0 else \
                        (f"ssm_pexp_{grp - 1}_3" if grp > 0 else None)
                    if pp is not None:
                        E.wait("cq", "a", pp)
                        E.wait("cq", "v", f"ssm_ppk_{grp}_{pk - 1}" if pk > 0
                               else f"ssm_ppk_{grp - 1}_3")
                    E.dma("cq", "rp0",
                          xt[0][:, 0:1024],
                          dtAh[m0 + pk * 8:m0 + pk * 8 + 16, :]
                          .rearrange("a n -> (a n)").unsqueeze(0)
                          .broadcast_to((128, 1024)))
                    E.dma("cq", "rp1",
                          xt[1][:, 0:1024],
                          c_re[m0 + pk * 8:m0 + pk * 8 + 16, :]
                          .rearrange("a n -> (a n)").unsqueeze(0)
                          .broadcast_to((128, 1024)))
                    E.mark(f"ssm_rep_{grp}_{pk}")
                E.wait("a", "rp0", f"ssm_rep_{grp}_{pk - pk % 2}")
                ppv = f"ssm_ppk_{grp}_{pk - 1}" if pk > 0 else \
                    (f"ssm_ppk_{grp - 1}_3" if grp > 0 else None)
                if ppv is not None:
                    E.wait("a", "v", ppv)
                act(pexp[:], dtarep[pk % 2][:], AF.Exp, scale=revI[:])
                E.mark(f"ssm_pexp_{grp}_{pk}")
                E.wait("v", "a", f"ssm_pexp_{grp}_{pk}")
                smmp = f"ssm_smm_{grp}_{pk - 2}" if pk >= 2 else \
                    (f"ssm_smm_{grp - 1}_{pk + 2}" if grp > 0 else None)
                if smmp is not None:
                    E.wait("v", "t", smmp)
                E.wait("v", "rp1", f"ssm_rep_{grp}_{pk - pk % 2}")
                tt("v", ppk[pk % 2][:], pexp[:], crerep[pk % 2][:], ALU.mult)
                E.mark(f"ssm_ppk_{grp}_{pk}")
                E.wait("t", "v", f"ssm_ppk_{grp}_{pk}")
                E.wait("t", f"us{grp % 2}", f"ssm_usl_{grp}")
                scp = f"ssm_scp_{grp}_{pk - 1}" if pk > 0 else \
                    (f"ssm_scp_{grp - 1}_3" if grp > 0 else None)
                if scp is not None:
                    E.wait("t", "v", scp)
                for ch in range(8):
                    q = pk * 8 + ch
                    mm(pst[:, ch * 64:(ch + 1) * 64],
                       ppk[pk % 2][:, ch * 64:(ch + 1) * 64],
                       usl[grp % 2][:, q, :], start=True, stop=True,
                       inc=(ch == 7))
                E.mark(f"ssm_smm_{grp}_{pk}")
                E.wait("v", "t", f"ssm_smm_{grp}_{pk}")
                cp("v", sstash[:, pk * 512:(pk + 1) * 512], pst[:])
                E.mark(f"ssm_scp_{grp}_{pk}")
            if grp > 0:
                E.wait("v", "t", f"ssm_emm_{grp - 1}_31")
            E.op("v", lambda e: e.memset(hstash[:], 0.0), inc=False)
            for bb in range(4):
                cp("v", rrep3[:, :, bb], rho_sb[:, m0:m0 + 32],
                   inc=(bb == 3))
            VD()
            E.mark(f"ssm_rrep_{grp}")
            for c in range(NCK):
                tt("v", stmp3[:, 0:16, :], hs4[:, 0:16, :, c],
                   rrep3[:, 0:16, :], ALU.mult, inc=False)
                tt("v", stmp3[:, 16:32, :], hs4[:, 16:32, :, c],
                   rrep3[:, 16:32, :], ALU.mult, inc=False)
                tt("v", hs4[:, 0:16, :, c + 1], stmp3[:, 0:16, :],
                   ss4[:, 0:16, :, c], ALU.add, inc=False)
                tt("v", hs4[:, 16:32, :, c + 1], stmp3[:, 16:32, :],
                   ss4[:, 16:32, :, c], ALU.add, inc=(c == NCK - 1))
            VD()
            cp("v", hst16[:], hstash[:])
            E.mark(f"ssm_scan_{grp}")
            for sub in range(4):
                t0sem = ["t0a", "t0b"][sub % 2]
                t0prev = (f"ssm_t0c_{grp}_{(sub - 2) * 8 + 7}" if sub >= 2
                          else (f"ssm_t0c_{grp - 1}_{(sub + 2) * 8 + 7}"
                                if grp > 0 else None))
                if t0prev is not None:
                    E.wait("gq", "t", t0prev)
                E.wait("gq", "shw", f"k16_sh_{(m0 + sub * 8) // 128}")
                E.dma("gq", t0sem, ulnTt[sub % 2][:],
                      shb[m0 + sub * 8:m0 + sub * 8 + 8,
                          128:128 + 128 * 256]
                      .rearrange("ch (s l) -> s ch l", l=256)[:, :, 0:128])
                E.mark(f"ssm_t0_{grp}_{sub}")
                for j in range(8):
                    ch = sub * 8 + j
                    mch = m0 + ch
                    e2p = (f"ssm_emm_{grp}_{ch - 8}" if ch >= 8 else
                           (f"ssm_emm_{grp - 1}_{ch + 24}" if grp > 0
                            else None))
                    if e2p is not None:
                        E.wait("a", "t", e2p)
                    act(e2b[ch % 8][:], iotaL1_64[:], AF.Exp,
                        scale=dtAT_sb[:, mch:mch + 1])
                    E.mark(f"ssm_e2_{grp}_{ch}")
                for j in range(8):
                    ch = sub * 8 + j
                    E.wait("t", t0sem, f"ssm_t0_{grp}_{sub}")
                    if sub == 0 and j == 0:
                        E.wait("t", "v", f"ssm_scan_{grp}")
                        yev = (f"ssm_yev_{grp - 1}" if grp > 0 else None)
                        if yev is not None:
                            E.wait("t", "v", yev)
                    mm(pyb[:, ch * 64:(ch + 1) * 64],
                       ulnTt[sub % 2][:, j, :], usl[grp % 2][:, ch, :],
                       start=(j == 0), stop=False, inc=True)
                    E.mark(f"ssm_t0c_{grp}_{ch}")
                for j in range(8):
                    ch = sub * 8 + j
                    E.wait("t", "a", f"ssm_e2_{grp}_{ch}")
                    rhsH = h16_4[:, ch, :, 0:16]
                    mm(pyb[:, ch * 64:(ch + 1) * 64],
                       e2b[ch % 8][:], rhsH, start=False, stop=(j == 7),
                       inc=True)
                    E.mark(f"ssm_emm_{grp}_{ch}")
            E.wait("v", "t", f"ssm_emm_{grp}_31")
            if grp > 0:
                E.wait("v", "ys", f"ssm_yst_{grp - 1}")
            E.op("v", lambda e: e.tensor_copy(
                out=ystrip2.rearrange("p (j ch) -> p ch j", ch=32)
                if False else
                ystrip2.rearrange("p (j ch) -> p j ch", ch=32)
                .transpose((0, 2, 1)),
                in_=pyb.rearrange("p (ch j) -> p ch j", j=64)))
            E.mark(f"ssm_yev_{grp}")
            E.wait("gq", "v", f"ssm_yev_{grp}")
            E.dma("gq", "ys", ysp[grp].transpose((2, 0, 1, 3))
                  .rearrange("l b c ch -> l (b c) ch"),
                  ystrip2.rearrange("p (j ch) -> p j ch", ch=32))
            E.mark(f"ssm_yst_{grp}")
            E.mark(f"ssm_tdone_{grp}")
            if grp + 2 < NGRP:
                load_uslices(grp + 2)
        E.barrier("ssm")

        for i in range(ntile + 2):
            if i >= 2:
                ffn_back2("f2", i - 2)
            if 1 <= i <= ntile:
                ffn_back1("f2", i - 1)
            if i < ntile:
                ffn_front("f2", i)
            if i >= 2:
                ffn_back2b("f2", i - 2)
        E.wait("g", "os0", E.cnt["os0"])
        E.wait("g", "os1", E.cnt["os1"])
        E.wait("sq", "os0", E.cnt["os0"])
        E.wait("sq", "os1", E.cnt["os1"])

    marks = {}
    program(Ctx(nc, sems, None, marks))

    with nc.Block() as block:
        @block.sync
        def _(_e):
            program(Ctx(nc, sems, "sq", marks))

        @block.vector
        def _(_e):
            program(Ctx(nc, sems, "v", marks))

        @block.scalar
        def _(_e):
            program(Ctx(nc, sems, "a", marks))

        @block.tensor
        def _(_e):
            program(Ctx(nc, sems, "t", marks))

        @block.gpsimd
        def _(_e):
            program(Ctx(nc, sems, "g", marks))

    return nc


_NC_CACHE = []


def kernel(**inputs):
    ub = np.ascontiguousarray(inputs["L0_attn_u"], dtype=np.float32)
    shards = [ub[i * BC:(i + 1) * BC] for i in range(NCORES)]
    common = {
        "w1": np.ascontiguousarray(inputs["L0_ffn_w1"], dtype=np.float32),
        "b1": np.ascontiguousarray(inputs["L0_ffn_b1"], dtype=np.float32),
        "w2": np.ascontiguousarray(inputs["L0_ffn_w2"], dtype=np.float32),
        "b2": np.ascontiguousarray(inputs["L0_ffn_b2"], dtype=np.float32),
        "a_log": np.ascontiguousarray(inputs["L0_attn_A_log"],
                                      dtype=np.float32),
        "d_vec": np.ascontiguousarray(inputs["L0_attn_D"], dtype=np.float32),
        "c_re": np.ascontiguousarray(inputs["L0_attn_C_re"],
                                     dtype=np.float32),
        "log_dt": np.ascontiguousarray(inputs["L0_attn_log_dt"],
                                       dtype=np.float32),
    }
    if not _NC_CACHE:
        _NC_CACHE.append(build_program())
    nc = _NC_CACHE[0]
    in_maps = [dict(common, u=s) for s in shards]
    res = run_bass_kernel_spmd(nc, in_maps, core_ids=list(range(NCORES)))
    return np.concatenate([r["out"] for r in res.results], axis=0)


if __name__ == "__main__":
    nc = build_program()
    print("built ok")


# version 10 (final)
# speedup vs baseline: 1.1380x; 1.0166x over previous
import numpy as np
import ml_dtypes
import concourse.bass as bass
import concourse.mybir as mybir
from concourse.bass_utils import run_bass_kernel_spmd

F32 = mybir.dt.float32
BF16 = mybir.dt.bfloat16
FP8 = mybir.dt.float8e4
U8 = mybir.dt.uint8
AF = mybir.ActivationFunctionType
ALU = mybir.AluOpType
AX = mybir.AxisListType

B, L, DM, FF, NS = 32, 2048, 1024, 4096, 64
NCORES = 8
BC = B // NCORES
NTILE = BC * (L // 128)
NCH = DM
NCK = L // 128
EPS = 1e-5
MAGIC = 12582912.0
MAGICM128 = 12582784.0
SHP = 257
SHLEN = 128 * SHP + 256

ENG = {"v": "vector", "a": "scalar", "t": "tensor", "g": "gpsimd",
       "gq": "gpsimd", "sq": "sync", "cq": "scalar", "tq": "gpsimd"}
ALIASES = {"v": ("v",), "a": ("a", "cq"), "t": ("t",),
           "g": ("g", "gq", "tq"), "sq": ("sq",)}


class Ctx:

    def __init__(self, nc, sems, who, marks):
        self.nc = nc
        self.sems = sems
        self.who = who
        self.aliases = set() if who is None else set(ALIASES[who])
        self.record = who is None
        self.marks = marks
        self.cnt = {k: 0 for k in sems}

    def dma(self, queue, sem, out, in_, **kw):
        self.cnt[sem] += 16
        if queue in self.aliases:
            getattr(self.nc, ENG[queue]).dma_start(out=out, in_=in_, **kw) \
                .then_inc(self.sems[sem], 16)
        return self.cnt[sem]

    def dmat(self, sem, out, in_):
        self.cnt[sem] += 16
        if "sq" in self.aliases:
            self.nc.sync.dma_start_transpose(out=out, in_=in_).then_inc(
                self.sems[sem], 16)
        return self.cnt[sem]

    def op(self, engine, fn, inc=True):
        if inc:
            self.cnt[engine] += 1
        if engine in self.aliases:
            ins = fn(getattr(self.nc, ENG[engine]))
            if inc:
                ins.then_inc(self.sems[engine], 1)
        return self.cnt[engine]

    def mark(self, label):
        if self.record:
            self.marks[label] = dict(self.cnt)

    def val(self, label, sem):
        m = self.marks.get(label)
        if m is None:
            if not self.record:
                raise KeyError(f"mark {label} missing")
            return 0
        return m[sem]

    def wait(self, engine, sem, label_or_val):
        v = (self.val(label_or_val, sem) if isinstance(label_or_val, str)
             else label_or_val)
        if v <= 0:
            return
        if engine in self.aliases:
            getattr(self.nc, ENG[engine]).wait_ge(self.sems[sem], v)

    def barrier(self, label):
        self.mark("bar_" + label)
        snap = (dict(self.cnt) if self.record else self.marks["bar_" + label])
        if self.who is not None:
            e = getattr(self.nc, ENG[self.who])
            for sem, v in snap.items():
                if v > 0:
                    e.wait_ge(self.sems[sem], v)


def build_program(ntile=NTILE, ngrp=32, k16g=8, debug=False, phases='all'):
    nc = bass.Bass("TRN2", target_bir_lowering=False,
                   detect_race_conditions=False)

    u = nc.dram_tensor("u", [BC, L, DM], F32, kind="ExternalInput").ap()
    w1 = nc.dram_tensor("w1", [FF, DM], F32, kind="ExternalInput").ap()
    b1 = nc.dram_tensor("b1", [FF], F32, kind="ExternalInput").ap()
    w2 = nc.dram_tensor("w2", [DM, FF], F32, kind="ExternalInput").ap()
    b2 = nc.dram_tensor("b2", [DM], F32, kind="ExternalInput").ap()
    a_log = nc.dram_tensor("a_log", [NS], F32, kind="ExternalInput").ap()
    d_vec = nc.dram_tensor("d_vec", [DM], F32, kind="ExternalInput").ap()
    c_re = nc.dram_tensor("c_re", [DM, NS], F32, kind="ExternalInput").ap()
    log_dt = nc.dram_tensor("log_dt", [DM], F32, kind="ExternalInput").ap()
    outh = nc.dram_tensor("out", [BC, L, DM], F32, kind="ExternalOutput").ap()

    dkind = {"kind": "ExternalOutput"} if debug else {}
    xsp = nc.dram_tensor("xsp", [NTILE, 128, DM], F32, **dkind).ap()
    ulnT = nc.dram_tensor("ulnT", [DM, BC, L], BF16, **dkind).ap()
    ysp = nc.dram_tensor("ysp", [NCH // 32, BC, NCK, 128, 32], F32, **dkind).ap()
    k16h = nc.dram_tensor("k16h", [NCH, 256], BF16, **dkind).ap()
    shb = nc.dram_tensor("shb", [NCH, SHLEN], BF16, **dkind).ap()
    dtAh = nc.dram_tensor("dtAh", [NCH, NS], F32, **dkind).ap()
    tmp64 = nc.dram_tensor("tmp64", [NS], F32).ap()
    tmp1 = nc.dram_tensor("tmp1", [4], F32).ap()

    T = nc.alloc_sbuf_tensor
    w1T = T("w1T", [128, 8, FF], FP8)
    w2T = T("w2T", [128, 32, DM], FP8)
    b2hrep = T("b2hrep", [128, DM], F32)
    dtA_sb = T("dtA_sb", [128, 8, NS], F32)
    cre_sb = T("cre_sb", [128, 8, NS], F32)
    d_sb = T("d_sb", [128, 8], F32)
    dtAT_sb = T("dtAT_sb", [64, DM], F32)
    rho_sb = T("rho_sb", [64, DM], F32)
    iotaL = T("iotaL", [128, 128], F32)
    iotaL1_64 = T("iotaL1_64", [64, 128], F32)
    revI = T("revI", [128, 1], F32)
    iotaP = T("iotaP", [128, 1], F32)
    ones_c = T("ones_c", [128, 1], F32)
    id128 = T("id128", [128, 128], F32)
    zt16 = T("zt16", [128, 128], BF16)
    invsw1 = T("invsw1", [128, 1], F32)
    invsw2 = T("invsw2", [128, 1], F32)
    ldt_sb = T("ldt_sb", [128, 8], F32)
    al_sb = T("al_sb", [1, NS], F32)
    narep = T("narep", [128, NS], F32)

    xt = [T(f"xt{i}", [128, DM], F32) for i in range(4)]
    sqdX = T("sqdX", [128, DM], F32)
    w1s = T("w1s", [128, 1], F32)
    B4h = T("B4h", [2, FF], BF16)
    B4l = T("B4l", [2, FF], BF16)
    id16 = T("id16", [128, 128], BF16)
    sxp = T("sxp", [128, 16], BF16)
    sxT = T("sxT", [2, 4, 128], BF16)
    qu8x = T("qu8x", [128, DM], U8)
    qx16 = T("qx16", [128, DM], BF16)
    qxT = [T(f"qxT{i}", [128, 8, 128], BF16) for i in range(2)]
    hb = T("hb", [128, 1024], F32)
    sqd = T("sqd", [128, 1024], F32)
    t2b = T("t2b", [128, DM], F32)
    gbuf = T("gbuf", [128, FF], F32)
    qu8 = T("qu8", [128, FF], U8)
    qg16 = T("qg16", [128, FF], BF16)
    qgT = [T(f"qgT{i}", [128, 32, 128], BF16) for i in range(2)]
    xnew = [T(f"xnew{i}", [128, DM], F32) for i in range(2)]
    uln16 = T("uln16", [128, DM], BF16)
    ulnTt = [T(f"ulnTt{i}", [128, 8, 128], BF16) for i in range(2)]
    cols = [T(f"cols{i}", [128, 64], F32) for i in range(4)]

    hstash = T("hstash", [64, 2176], F32)
    ppk = [qgT[i][:].rearrange("p a b -> p (a b)")[:, 0:512].bitcast(BF16)
           if False else qgT[i][:, 0:4, :].rearrange("p a b -> p (a b)")
           for i in range(2)]
    usl = [xnew[i][:].bitcast(BF16).rearrange("p (a b) -> p a b", b=64)
           for i in range(2)]
    t0t = [ulnTt[0][:, j, :] for j in range(8)]
    rrep = t2b[0:64, 128:256]
    scantmp = t2b[0:64, 0:128]
    e2b = [qu8[:].bitcast(BF16)[0:64, 128 * j:128 * (j + 1)] for j in range(8)]
    pexp = sqd[0:128, 0:512]
    dtarep = [xt[0][:, 0:512], xt[0][:, 512:1024]]
    crerep = [xt[1][:, 0:512], xt[1][:, 512:1024]]
    sstash = gbuf[0:64, 0:2048]
    hst16 = qg16[0:64, 0:2176]
    ystrip2 = gbuf[:, 2048:4096]
    k16t = hb[:, 0:128]
    en2 = hb[:, 128:256]
    en3 = hb[:, 256:384]
    k16b = uln16[:, 0:128]

    pAll = nc.alloc_psum_tensor("pAll", [128, 4096], F32)
    pq = [pAll[:, 0:1024], pAll[:, 1024:2048]]
    p2 = pAll[:, 2048:3072]
    pst = pAll[0:64, 0:512]
    ptr = pAll[0:64, 3072:3584]
    pyb = pAll[:, 2048:4096]
    pmean = pAll[0:1, 3072:3073]

    if debug:
        dbg_w1T = nc.dram_tensor("dbg_w1T", [128, 8, FF], U8,
                                 kind="ExternalOutput").ap()
        dbg_w2T = nc.dram_tensor("dbg_w2T", [128, 32, DM], U8,
                                 kind="ExternalOutput").ap()
        dbg_qx = nc.dram_tensor("dbg_qx", [128, DM], BF16,
                                kind="ExternalOutput").ap()
        dbg_qg = nc.dram_tensor("dbg_qg", [128, FF], BF16,
                                kind="ExternalOutput").ap()
        dbg_g = nc.dram_tensor("dbg_g", [128, FF], F32,
                               kind="ExternalOutput").ap()
        dbg_cols = nc.dram_tensor("dbg_cols", [128, 96], F32,
                                  kind="ExternalOutput").ap()
        dbg_xt0 = nc.dram_tensor("dbg_xt0", [128, DM], F32,
                                 kind="ExternalOutput").ap()
        dbg_xt1 = nc.dram_tensor("dbg_xt1", [128, DM], F32,
                                 kind="ExternalOutput").ap()
    sems = {k: nc.alloc_semaphore("sem_" + k)
            for k in ("v", "a", "t", "g",
                      "xt0", "xt1", "os0", "os1", "ul0", "ul1", "k16s",
                      "shw", "ys", "tx", "tg", "tu", "us0", "us1",
                      "t0a", "t0b", "t0c", "t0d", "t0e", "t0f", "t0g", "t0h", "rp0", "rp1", "p0l",
                      "gb", "cqm")}

    CN = ["s1", "s2", "mx", "mn", "m", "ms", "msq", "var", "vpe", "rec", "inv",
          "ra", "ra2", "rec2", "rmsc", "d1", "d2", "mdev", "mr", "mr2", "mrc",
          "recm", "sx", "al", "tm", "beta", "rsx", "c1", "g2s", "gmax", "rme",
          "rec3", "rmsi", "gm2", "gmc", "recg", "sh", "alh", "rsh", "c2",
          "c2h", "sc0", "sc1", "sc2", "sc3", "acc", "tmpc", "mean",
          "s1u", "s2u", "mu", "msu", "msqu", "varu", "vpeu", "recu", "invu"]
    COL = {n: i for i, n in enumerate(CN)}
    assert len(CN) <= 64

    def program(E: Ctx):
        SD = AX.X

        def ts(en, o, i0, s1, s2, o0, o1=None, acc=None, inc=True):
            kw = {}
            if o1 is not None:
                kw["op1"] = o1
            if acc is not None:
                kw["accum_out"] = acc
            return E.op(en, lambda e: e.tensor_scalar(
                out=o, in0=i0, scalar1=s1, scalar2=s2, op0=o0, **kw),
                inc=True)

        def tt(en, o, i0, i1, o0, inc=True):
            return E.op(en, lambda e: e.tensor_tensor(
                out=o, in0=i0, in1=i1, op=o0), inc=True)

        def VD():
            E.op("v", lambda e: e.tensor_scalar(
                out=cols[0][:, 63:64], in0=ones_c[:], scalar1=0.0,
                scalar2=None, op0=ALU.mult), inc=True)

        def act(o, i0, fn, scale=None, bias=None, acc=None, inc=True):
            kw = {}
            if scale is not None:
                kw["scale"] = scale
            if bias is not None:
                kw["bias"] = bias
            if acc is not None:
                kw["accum_out"] = acc
            return E.op("a", lambda e: e.activation(
                out=o, in_=i0, func=fn, **kw), inc=inc)

        def cp(en, o, i0, inc=True):
            return E.op(en, lambda e: e.tensor_copy(out=o, in_=i0), inc=True)

        def rcp(o, i0, inc=True):
            return E.op("v", lambda e: e.reciprocal(out=o, in_=i0), inc=True)

        def mm(o, lhsT, rhs, start, stop, inc=False):
            return E.op("t", lambda e: e.matmul(
                o, lhsT, rhs, start=start, stop=stop, skip_group_check=True),
                inc=inc)

        def col(name, buf=0):
            return cols[buf][:, COL[name]:COL[name] + 1]

        E.op("g", lambda e: e.iota(iotaL[:], pattern=[[1, 128]], base=0,
                                   channel_multiplier=0,
                                   allow_small_or_imprecise_dtypes=True))
        E.op("g", lambda e: e.iota(iotaL1_64[:], pattern=[[1, 128]], base=1,
                                   channel_multiplier=0,
                                   allow_small_or_imprecise_dtypes=True))
        E.op("g", lambda e: e.iota(revI[:], pattern=[[0, 1]], base=127,
                                   channel_multiplier=-1,
                                   allow_small_or_imprecise_dtypes=True))
        E.op("g", lambda e: e.iota(iotaP[:], pattern=[[0, 1]], base=0,
                                   channel_multiplier=1,
                                   allow_small_or_imprecise_dtypes=True))
        E.op("g", lambda e: e.iota(ones_c[:], pattern=[[0, 1]], base=1,
                                   channel_multiplier=0,
                                   allow_small_or_imprecise_dtypes=True))
        E.mark("p0_iota")

        E.dma("gq", "p0l", xt[0][:], b2.unsqueeze(0).broadcast_to((128, DM)))
        E.dma("gq", "p0l", ldt_sb[:], log_dt.rearrange("(g p) -> p g", p=128), allow_slow_non_contiguous=True)
        E.dma("gq", "p0l", al_sb[:], a_log.unsqueeze(0))
        E.dma("gq", "p0l", d_sb[:], d_vec.rearrange("(g p) -> p g", p=128), allow_slow_non_contiguous=True)
        E.dma("gq", "p0l", cre_sb[:], c_re.rearrange("(g p) n -> p g n", p=128))
        E.mark("p0_loads")

        E.wait("v", "g", "p0_iota")
        E.wait("v", "p0l", "p0_loads")
        ts("v", id128[:], iotaL[:], iotaP[:], None, ALU.is_equal)
        VD()
        cp("v", id16[:], id128[:])
        ts("v", b2hrep[:], xt[0][:], 0.5, None, ALU.mult)
        E.op("v", lambda e: e.memset(zt16[:], 0.0))
        E.mark("p0_v1")
        E.wait("a", "p0l", "p0_loads")
        dtl = act(ldt_sb[:], ldt_sb[:], AF.Exp)
        aexp = act(al_sb[:], al_sb[:], AF.Exp)
        E.wait("v", "a", aexp)
        ts("v", al_sb[:], al_sb[:], -1.0, None, ALU.mult)
        E.mark("p0_na")
        E.wait("cq", "v", "p0_na")
        w64 = E.dma("cq", "cqm", tmp64.unsqueeze(0), al_sb[:])
        E.wait("cq", "cqm", w64)
        E.dma("cq", "cqm", narep[:], tmp64.unsqueeze(0).broadcast_to((128, NS)))
        E.mark("p0_narep")
        E.wait("v", "cqm", "p0_narep")
        E.wait("v", "a", dtl)
        for gi in range(8):
            ts("v", dtA_sb[:, gi, :], narep[:], ldt_sb[:, gi:gi + 1], None,
               ALU.mult, inc=(gi == 7))
        E.mark("p0_dta")
        E.wait("cq", "v", "p0_dta")
        E.dma("cq", "cqm", dtAh.rearrange("(g p) n -> p g n", p=128), dtA_sb[:])
        E.wait("t", "v", "p0_dta")
        for gi in range(8):
            if gi > 0:
                E.wait("t", "v", f"p0_dtat_{gi - 1}")
            tr = E.op("t", lambda e, gi=gi: e.transpose(
                ptr[0:64, 0:128], dtA_sb[:, gi, :], id128[:]), inc=True)
            E.wait("v", "t", tr)
            cp("v", dtAT_sb[:, gi * 128:(gi + 1) * 128], ptr[0:64, 0:128])
            E.mark(f"p0_dtat_{gi}")
        E.mark("p0_dtat")
        E.wait("a", "v", "p0_dtat")
        act(rho_sb[:], dtAT_sb[:], AF.Exp, scale=128.0)
        E.mark("p0_rho")

        def wprep(wap, nslab, sldm, wT, invsw, label):
            bufs = ([gbuf, gbuf] if sldm == 4096 else xt)
            qb = qg16[:, 0:sldm] if sldm == 4096 else uln16[:]
            stg = qgT[0] if sldm == 4096 else ulnTt[0]
            acc_col = col("acc", 1)
            tmpc = col("tmpc", 1)
            E.wait("gq", "v", "p0_v1")
            if label == "w2p":
                E.wait("gq", "v", "b4_done")
            ts("v", acc_col, ones_c[:], 0.0, None, ALU.mult)
            for sl in range(nslab):
                buf = bufs[sl % 2][:, 0:sldm]
                bsem = "gb" if sldm == 4096 else f"xt{sl % 2}"
                back = 1 if sldm == 4096 else 2
                if sl >= back:
                    E.wait("gq", "v", f"{label}_rd_{sl - back}")
                ld = E.dma("gq", bsem, buf, wap[sl * 128:(sl + 1) * 128, :])
                E.wait("v", bsem, ld)
                E.op("v", lambda e, b=buf: e.reduce_sum(
                    out=tmpc, in_=b, axis=SD, apply_absolute_value=True),
                    inc=False)
                VD()
                tt("v", acc_col, acc_col, tmpc, ALU.add)
                E.mark(f"{label}_rd_{sl}")
            E.wait("t", "v", f"{label}_rd_{nslab - 1}")
            mmv = E.op("t", lambda e: e.matmul(
                pmean, ones_c[:], acc_col, start=True, stop=True), inc=True)
            E.wait("v", "t", mmv)
            ts("v", cols[1][0:1, COL["mean"]:COL["mean"] + 1], pmean,
               1.0 / (FF * DM), 1e-5, ALU.mult, ALU.max)
            E.mark(f"{label}_mcl")
            E.wait("cq", "v", f"{label}_mcl")
            wt1 = E.dma("cq", "cqm", tmp1[0:1].unsqueeze(0),
                        cols[1][0:1, COL["mean"]:COL["mean"] + 1])
            E.wait("cq", "cqm", wt1)
            rb = E.dma("cq", "cqm", invsw[:],
                       tmp1[0:1].unsqueeze(0).broadcast_to((128, 1)))
            E.wait("v", "cqm", rb)
            rcp(tmpc, invsw[:])
            VD()
            if label == "w1p":
                cp("v", w1s[:], tmpc)
            E.mark(f"{label}_scale")
            for sl in range(nslab):
                buf = bufs[sl % 2][:, 0:sldm]
                bsem = "gb" if sldm == 4096 else f"xt{sl % 2}"
                if sl > 0:
                    E.wait("gq", "v", f"{label}_q_{sl - 1}")
                ld = E.dma("gq", bsem, buf, wap[sl * 128:(sl + 1) * 128, :])
                E.wait("v", bsem, ld)
                if sl > 0:
                    E.wait("v", "tx", f"{label}_tp_{sl - 1}")
                ts("v", buf, buf, tmpc, 1.4, ALU.mult, ALU.min, inc=False)
                ts("v", buf, buf, -1.4, MAGIC, ALU.max, ALU.add, inc=False)
                ts("v", qu8[:, 0:sldm], buf, MAGICM128, None, ALU.subtract,
                   inc=False)
                ts("v", qb, qu8[:, 0:sldm], -128.0, None, ALU.add)
                E.mark(f"{label}_q_{sl}")
                E.wait("sq", "v", f"{label}_q_{sl}")
                if sl > 0:
                    E.wait("sq", "v", f"{label}_cast_{sl - 1}")
                E.dmat("tx", stg[:], qb)
                E.mark(f"{label}_tp_{sl}")
                E.wait("v", "tx", f"{label}_tp_{sl}")
                cp("v", wT[:, :, sl * 128:(sl + 1) * 128], stg[:])
                E.mark(f"{label}_cast_{sl}")

        wprep(w1, 32, 1024, w1T, invsw1, "w1p")
        stage2 = gbuf[0:2, 0:4096]
        b1ld = E.dma("gq", "p0l", stage2,
                     b1.unsqueeze(0).broadcast_to((2, FF)))
        E.wait("v", "p0l", b1ld)
        ts("v", stage2, stage2, w1s[0:2, 0:1], None, ALU.mult)
        VD()
        cp("v", B4h[:], stage2)
        VD()
        tt("v", stage2, stage2, B4h[:], ALU.subtract)
        VD()
        cp("v", B4l[:], stage2)
        E.mark("b4_done")
        wprep(w2, 8, 4096, w2T, invsw2, "w2p")
        E.barrier("p0")

        def ffn_front(ph, i):
            bse, cse = i // NCK, i % NCK
            pf = f"{ph}_{i}"
            cI = lambda n: col(n, i % 4)
            if i >= 4:
                E.wait("v", "a", f"{ph}_ev2_{i - 4}")
            if i >= 4:
                E.wait("gq", "v", f"{ph}_xtfree_{i - 4}")
            xsem = f"xt{i % 2}"
            if ph == "f1":
                ld = E.dma("gq", xsem, xt[i % 4][:],
                           u[bse, cse * 128:(cse + 1) * 128, :])
            else:
                pl = E.dma("gq", xsem, xt[i % 4][:], xsp[i, :, :])
                E.wait("gq", xsem, pl)
                ld = E.dma("gq", xsem,
                           xt[i % 4][:].rearrange("p (g ch) -> p g ch", ch=32),
                           ysp.transpose((3, 0, 4, 1, 2))[:, :, :, bse, cse],
                           accum_op=ALU.add)
            E.mark(f"{pf}_xt")
            E.wait("v", xsem, ld)
            E.op("v", lambda e: e.reduce_sum(out=cI("s1"), in_=xt[i % 4][:],
                                             axis=SD), inc=False)
            E.op("v", lambda e: e.reduce_max(out=cI("mx"), in_=xt[i % 4][:],
                                             axis=SD), inc=False)
            E.op("v", lambda e: e.tensor_reduce(
                out=cI("mn"), in_=xt[i % 4][:], axis=SD, op=ALU.min))
            E.mark(f"{pf}_vst")
            E.wait("a", xsem, ld)
            if i >= 1:
                E.wait("a", "v", f"{ph}_sqdXfree_{i - 1}")
            act(sqdX[:], xt[i % 4][:], AF.Square, acc=cI("s2"))
            E.mark(f"{pf}_sqa")
            E.wait("v", "a", f"{pf}_sqa")
            ts("v", cI("m"), cI("s1"), 1.0 / DM, None, ALU.mult, inc=False)
            ts("v", cI("ms"), cI("s2"), 1.0 / DM, None, ALU.mult, inc=False)
            tt("v", cI("msq"), cI("m"), cI("m"), ALU.mult, inc=False)
            VD()
            tt("v", cI("var"), cI("ms"), cI("msq"), ALU.subtract, inc=False)
            VD()
            ts("v", cI("vpe"), cI("var"), EPS, None, ALU.add, inc=False)
            VD()
            rcp(cI("rec"), cI("vpe"))
            E.mark(f"{pf}_r1")
            VD()
            tt("v", cI("ra"), cI("var"), cI("rec"), ALU.mult, inc=False)
            VD()
            ts("v", cI("ra2"), cI("ra"), EPS, None, ALU.add, inc=False)
            VD()
            rcp(cI("rec2"), cI("ra2"))
            E.mark(f"{pf}_r2")
            E.wait("a", "v", f"{pf}_r2")
            act(cI("inv"), cI("rec"), AF.Sqrt, inc=False)
            act(cI("rmsc"), cI("rec2"), AF.Sqrt)
            E.mark(f"{pf}_sq2")
            E.wait("v", "a", f"{pf}_sq2")
            tt("v", cI("d1"), cI("mx"), cI("m"), ALU.subtract, inc=False)
            tt("v", cI("d2"), cI("m"), cI("mn"), ALU.subtract, inc=False)
            VD()
            tt("v", cI("mdev"), cI("d1"), cI("d2"), ALU.max, inc=False)
            VD()
            tt("v", cI("mr"), cI("mdev"), cI("inv"), ALU.mult, inc=False)
            VD()
            tt("v", cI("mr2"), cI("mr"), cI("rmsc"), ALU.mult, inc=False)
            VD()
            ts("v", cI("mrc"), cI("mr2"), 1e-5, None, ALU.max, inc=False)
            VD()
            rcp(cI("recm"), cI("mrc"), inc=False)
            VD()
            ts("v", cI("sx"), cI("recm"), 127.0, None, ALU.mult, inc=False)
            VD()
            tt("v", cI("al"), cI("inv"), cI("rmsc"), ALU.mult, inc=False)
            VD()
            tt("v", cI("al"), cI("al"), cI("sx"), ALU.mult, inc=False)
            VD()
            tt("v", cI("tm"), cI("m"), cI("al"), ALU.mult, inc=False)
            VD()
            ts("v", cI("beta"), cI("tm"), -1.0, None, ALU.mult, inc=False)
            rcp(cI("rsx"), cI("sx"), inc=False)
            VD()
            tt("v", cI("c1"), cI("rsx"), invsw1[:], ALU.mult)
            sl0 = (i % 4) * 4
            cp("v", sxp[:, sl0:sl0 + 1], cI("sx"))
            VD()
            tt("v", cI("tm"), cI("sx"), sxp[:, sl0:sl0 + 1], ALU.subtract)
            VD()
            cp("v", sxp[:, sl0 + 1:sl0 + 2], cI("tm"))
            E.mark(f"{pf}_cols")

        def ffn_front_tr(ph, i):
            pf = f"{ph}_{i}"
            cI = lambda n: col(n, i % 4)
            sl0 = (i % 4) * 4
            E.wait("t", "v", f"{pf}_cols")
            if i >= 1:
                E.wait("t", "v", f"{ph}_sxT_{i - 1}")
            E.op("t", lambda e: e.transpose(
                pAll[0:2, 3072:3136].bitcast(BF16),
                sxp[:, sl0:sl0 + 2], id16[:]), inc=True)
            E.mark(f"{pf}_sxtr")
            E.wait("v", "t", f"{pf}_sxtr")
            cp("v", sxT[:, i % 4, :],
               pAll[0:2, 3072:3136].bitcast(BF16)[:, 0:128])
            E.mark(f"{ph}_sxT_{i}")
            if i >= 1:
                E.wait("v", "tx", f"{ph}_qxT_{i - 1}")
            ts("v", sqdX[:], xt[i % 4][:], cI("al"), cI("beta"), ALU.mult,
               ALU.add, inc=False)
            ts("v", qu8x[:], sqdX[:], MAGIC, MAGICM128, ALU.add,
               ALU.subtract, inc=False)
            ts("v", qx16[:], qu8x[:], -128.0, None, ALU.add)
            E.mark(f"{ph}_sqdXfree_{i}")
            E.wait("sq", "v", f"{ph}_sqdXfree_{i}")
            if i >= 2:
                E.wait("sq", "t", f"{ph}_mm1d_{i - 2}")
            E.dmat("tx", qxT[i % 2][:], qx16[:])
            E.mark(f"{ph}_qxT_{i}")

        def ffn_back1(ph, i):
            bse, cse = i // NCK, i % NCK
            pf = f"{ph}_{i}"
            cI = lambda n: col(n, i % 4)
            E.wait("t", "tx", f"{ph}_qxT_{i}")
            E.wait("t", "v", f"{pf}_cols")
            E.wait("t", "v", f"{ph}_sxT_{i}")
            for q in range(4):
                prev = (f"{ph}_ev1_{i}_{q - 2}" if q >= 2 else
                        (f"{ph}_ev1_{i - 1}_{q + 2}" if i >= 1 else None))
                if prev is not None:
                    E.wait("t", "a", prev)
                for dt_ in range(8):
                    for hh in range(2):
                        mm(pq[q % 2][:, hh * 512:(hh + 1) * 512],
                           qxT[i % 2][:, dt_, :],
                           w1T[:, dt_, q * 1024 + hh * 512:
                               q * 1024 + (hh + 1) * 512],
                           start=(dt_ == 0), stop=False, inc=False)
                for hh in range(2):
                    mm(pq[q % 2][:, hh * 512:(hh + 1) * 512],
                       sxT[:, i % 4, :],
                       B4h[:, q * 1024 + hh * 512:q * 1024 + (hh + 1) * 512],
                       start=False, stop=False, inc=False)
                    mm(pq[q % 2][:, hh * 512:(hh + 1) * 512],
                       sxT[:, i % 4, :],
                       B4l[:, q * 1024 + hh * 512:q * 1024 + (hh + 1) * 512],
                       start=False, stop=True, inc=(hh == 1))
                E.mark(f"{ph}_mm1_{i}_{q}")
                E.wait("a", "t", f"{ph}_mm1_{i}_{q}")
                E.wait("a", "v", f"{pf}_cols")
                if q == 0 and i >= 1:
                    E.wait("a", "v", f"{ph}_qgq_{i - 1}")
                act(gbuf[:, q * 1024:(q + 1) * 1024], pq[q % 2][:], AF.Gelu,
                    scale=cI("c1"))
                E.mark(f"{ph}_ev1_{i}_{q}")
                act(sqd[:], gbuf[:, q * 1024:(q + 1) * 1024], AF.Square,
                    acc=cI("sc%d" % q))
                E.mark(f"{ph}_gsq_{i}_{q}")
                if q == 3:
                    E.mark(f"{ph}_mm1d_{i}")
            E.mark(f"{ph}_back1_{i}")

        def ffn_back2(ph, i):
            bse, cse = i // NCK, i % NCK
            pf = f"{ph}_{i}"
            cI = lambda n: col(n, i % 4)
            E.wait("v", "a", f"{ph}_gsq_{i}_3")
            tt("v", cI("g2s"), cI("sc0"), cI("sc1"), ALU.add, inc=False)
            tt("v", cI("tm"), cI("sc2"), cI("sc3"), ALU.add, inc=False)
            VD()
            tt("v", cI("g2s"), cI("g2s"), cI("tm"), ALU.add, inc=False)
            E.op("v", lambda e: e.reduce_max(
                out=cI("gmax"), in_=gbuf[:], axis=SD,
                apply_absolute_value=True), inc=False)
            ts("v", cI("rme"), cI("g2s"), 1.0 / FF, EPS, ALU.mult, ALU.add,
               inc=False)
            VD()
            rcp(cI("rec3"), cI("rme"))
            E.mark(f"{pf}_r3")
            E.wait("a", "v", f"{pf}_r3")
            act(cI("rmsi"), cI("rec3"), AF.Sqrt)
            E.mark(f"{pf}_s3")
            E.wait("v", "a", f"{pf}_s3")
            tt("v", cI("gm2"), cI("gmax"), cI("rmsi"), ALU.mult, inc=False)
            VD()
            ts("v", cI("gmc"), cI("gm2"), 1e-5, None, ALU.max, inc=False)
            VD()
            rcp(cI("recg"), cI("gmc"), inc=False)
            VD()
            ts("v", cI("sh"), cI("recg"), 127.0, None, ALU.mult, inc=False)
            VD()
            tt("v", cI("alh"), cI("rmsi"), cI("sh"), ALU.mult, inc=False)
            rcp(cI("rsh"), cI("sh"), inc=False)
            VD()
            tt("v", cI("c2"), cI("rsh"), invsw2[:], ALU.mult, inc=False)
            VD()
            ts("v", cI("c2h"), cI("c2"), 0.5, None, ALU.mult)
            E.mark(f"{pf}_cols2")
            if i >= 1:
                E.wait("v", "tg", f"{ph}_qgT_{i - 1}")
            ts("v", gbuf[:], gbuf[:], cI("alh"), MAGIC, ALU.mult, ALU.add,
               inc=False)
            ts("v", qu8[:], gbuf[:], MAGICM128, None, ALU.subtract,
               inc=False)
            ts("v", qg16[:], qu8[:], -128.0, None, ALU.add)
            E.mark(f"{ph}_qgq_{i}")
            E.wait("sq", "v", f"{ph}_qgq_{i}")
            if i >= 2:
                E.wait("sq", "t", f"{ph}_mm2_{i - 2}")
            E.dmat("tg", qgT[i % 2][:], qg16[:])
            E.mark(f"{ph}_qgT_{i}")

        def ffn_back2b(ph, i):
            bse, cse = i // NCK, i % NCK
            pf = f"{ph}_{i}"
            cI = lambda n: col(n, i % 4)
            E.wait("t", "tg", f"{ph}_qgT_{i}")
            if i >= 1:
                E.wait("t", "a", f"{ph}_ev2_{i - 1}")
            for ft in range(32):
                for dc in range(2):
                    mm(p2[:, dc * 512:(dc + 1) * 512],
                       qgT[i % 2][:, ft, :],
                       w2T[:, ft, dc * 512:(dc + 1) * 512],
                       start=(ft == 0), stop=(ft == 31),
                       inc=(ft == 31 and dc == 1))
            E.mark(f"{ph}_mm2_{i}")
            E.wait("a", "t", f"{ph}_mm2_{i}")
            if i >= 1:
                E.wait("a", "v", f"{ph}_t3_{i - 1}")
            act(t2b[:], p2[:], AF.Identity, scale=cI("c2h"))
            E.mark(f"{ph}_ev2_{i}")
            E.wait("v", "a", f"{ph}_ev2_{i}")
            tt("v", t2b[:], t2b[:], b2hrep[:], ALU.add)
            E.mark(f"{ph}_t3_{i}")
            if i >= 2:
                E.wait("v", f"os{i % 2}", f"{ph}_ost_{i - 2}")
            tt("v", xnew[i % 2][:], xt[i % 4][:], t2b[:], ALU.add)
            E.mark(f"{ph}_xn_{i}")
            E.mark(f"{ph}_xtfree_{i}")
            E.wait("gq", "v", f"{ph}_xn_{i}")
            if ph == "f1":
                E.dma("gq", f"os{i % 2}", xsp[i, :, :], xnew[i % 2][:])
            else:
                E.dma("gq", f"os{i % 2}",
                      outh[bse, cse * 128:(cse + 1) * 128, :],
                      xnew[i % 2][:])
            E.mark(f"{ph}_ost_{i}")
            if ph == "f1":
                E.op("v", lambda e: e.reduce_sum(
                    out=cI("s1u"), in_=xnew[i % 2][:], axis=SD), inc=False)
                E.wait("a", "v", f"{ph}_xn_{i}")
                act(sqd[:], xnew[i % 2][:], AF.Square, acc=cI("s2u"))
                E.mark(f"{pf}_sqa2")
                E.wait("v", "a", f"{pf}_sqa2")
                ts("v", cI("mu"), cI("s1u"), 1.0 / DM, None, ALU.mult,
                   inc=False)
                ts("v", cI("msu"), cI("s2u"), 1.0 / DM, None, ALU.mult,
                   inc=False)
                tt("v", cI("msqu"), cI("mu"), cI("mu"), ALU.mult, inc=False)
                VD()
                tt("v", cI("varu"), cI("msu"), cI("msqu"), ALU.subtract,
                   inc=False)
                VD()
                ts("v", cI("vpeu"), cI("varu"), EPS, None, ALU.add, inc=False)
                VD()
                rcp(cI("recu"), cI("vpeu"))
                E.mark(f"{pf}_r4")
                E.wait("a", "v", f"{pf}_r4")
                act(cI("invu"), cI("recu"), AF.Sqrt)
                E.mark(f"{pf}_s4")
                E.wait("v", "a", f"{pf}_s4")
                if i >= 1:
                    E.wait("v", "tu", f"{ph}_ulT_{i - 1}")
                ts("v", uln16[:], xnew[i % 2][:], cI("mu"), cI("invu"),
                   ALU.subtract, ALU.mult)
                E.mark(f"{pf}_uln")
                E.wait("sq", "v", f"{pf}_uln")
                if i >= 2:
                    E.wait("sq", f"ul{i % 2}", f"{ph}_ulst_{i - 2}")
                E.dmat("tu", ulnTt[i % 2][:], uln16[:])
                E.mark(f"{ph}_ulT_{i}")
                E.wait("gq", "tu", f"{ph}_ulT_{i}")
                E.dma("gq", f"ul{i % 2}",
                      ulnT.rearrange("(dt dp) b (c s) -> dp dt b c s",
                                     dp=128, s=128)[:, :, bse, cse, :],
                      ulnTt[i % 2][:])
                E.mark(f"{ph}_ulst_{i}")
            E.mark(f"{ph}_done_{i}")

        for i in range(ntile + 2):
            if i >= 2:
                ffn_back2("f1", i - 2)
            if i < ntile:
                ffn_front("f1", i)
            if 1 <= i <= ntile:
                ffn_back1("f1", i - 1)
            if i < ntile:
                ffn_front_tr("f1", i)
            if i >= 2:
                ffn_back2b("f1", i - 2)
        E.barrier("f1")
        if phases == "f1":
            if debug:
                E.dma("gq", "ys", dbg_w1T.bitcast(FP8), w1T[:])
                E.dma("gq", "ys", dbg_w2T.bitcast(FP8), w2T[:])
                E.dma("gq", "ys", dbg_qx, qx16[:])
                E.dma("gq", "ys", dbg_qg, qg16[:])
                E.dma("gq", "ys", dbg_g, gbuf[:])
                E.dma("gq", "ys", dbg_cols[:, 0:64], cols[0][:])
                E.dma("gq", "ys", dbg_cols[:, 64:96], cols[1][:, 0:32])
                E.dma("gq", "ys", dbg_xt0, xt[0][:])
                E.dma("gq", "ys", dbg_xt1, xt[1][:])
            E.wait("g", "ys", E.cnt["ys"])
            E.wait("g", "os0", E.cnt["os0"])
            E.wait("g", "os1", E.cnt["os1"])
            return

        def k16_group(gi):
            E.op("v", lambda e: e.memset(k16t[:], 0.0))
            E.mark(f"k16_z_{gi}")
            E.wait("a", "v", f"k16_z_{gi}")
            en3s = [en3, hb[:, 384:512]]
            for n in range(NS):
                act(en2[:], iotaL[:], AF.Exp, scale=dtA_sb[:, gi, n:n + 1],
                    inc=True)
                E.mark(f"k16_e_{gi}_{n}")
                E.wait("v", "a", f"k16_e_{gi}_{n}")
                ts("v", en3s[n % 2][:], en2[:], cre_sb[:, gi, n:n + 1], None,
                   ALU.mult, inc=True)
                E.mark(f"k16_m_{gi}_{n}")
                if n >= 1:
                    tt("v", k16t[:], k16t[:], en3s[(n - 1) % 2][:], ALU.add,
                       inc=False)
                if n + 1 < NS:
                    E.wait("a", "v", f"k16_m_{gi}_{n}")
            tt("v", k16t[:], k16t[:], en3s[(NS - 1) % 2][:], ALU.add,
               inc=False)
            VD()
            tt("v", k16t[:, 0:1], k16t[:, 0:1], d_sb[:, gi:gi + 1], ALU.add,
               inc=False)
            VD()
            VD()
            cp("v", k16b[:], k16t[:])
            E.mark(f"k16_b_{gi}")
            E.wait("gq", "v", f"k16_b_{gi}")
            E.dma("gq", "k16s", k16h[gi * 128:(gi + 1) * 128, 128:256],
                  k16b[:])
            E.dma("gq", "k16s", k16h[gi * 128:(gi + 1) * 128, 0:128],
                  zt16[:])
            E.mark(f"k16_st_{gi}")
            E.wait("cq", "k16s", f"k16_st_{gi}")
            for sub in range(8):
                m0 = gi * 128 + sub * 16
                wv = shb[m0:m0 + 16, 0:128 * SHP].rearrange(
                    "ch (s i) -> ch s i", i=SHP)[:, :, 0:256]
                E.dma("cq", "shw", wv, k16h[m0:m0 + 16, :].unsqueeze(1)
                      .broadcast_to((16, 128, 256)))
            E.mark(f"k16_sh_{gi}")

        NGRP = ngrp

        def load_uslices(grp):
            if grp >= 2:
                E.wait("sq", "t", f"ssm_tdone_{grp - 2}")
            m0 = grp * 32
            E.dmat(f"us{grp % 2}",
                   usl[grp % 2][:].rearrange("p a b -> p (a b)"),
                   ulnT[m0:m0 + 32].rearrange("ch b (c s) -> (ch b c) s",
                                              s=128))
            E.mark(f"ssm_usl_{grp}")

        k16_group(0)
        if k16g > 1:
            k16_group(1)
        load_uslices(0)
        if NGRP > 1:
            load_uslices(1)

        def t0view(mch):
            return shb[mch, 128:128 + 128 * 256].rearrange(
                "(s l) -> s l", l=256)[:, 0:128]

        hs4 = hstash.rearrange("p (blk b sl) -> p blk b sl", b=4, sl=17)
        h16_4 = hst16.rearrange("p (blk b sl) -> p blk b sl", b=4, sl=17)
        ss4 = sstash.rearrange("p (blk b c) -> p blk b c", b=4, c=16)
        rrep3 = rrep.rearrange("p (blk b) -> p blk b", b=4)
        stmp3 = scantmp.rearrange("p (blk b) -> p blk b", b=4)

        for grp in range(NGRP):
            m0 = grp * 32
            if grp % 4 == 0 and grp // 4 + 2 < k16g:
                k16_group(grp // 4 + 2)
            for pk in range(4):
                if pk % 2 == 0:
                    pp = f"ssm_pexp_{grp}_{pk - 1}" if pk > 0 else \
                        (f"ssm_pexp_{grp - 1}_3" if grp > 0 else None)
                    if pp is not None:
                        E.wait("cq", "a", pp)
                        E.wait("cq", "v", f"ssm_ppk_{grp}_{pk - 1}" if pk > 0
                               else f"ssm_ppk_{grp - 1}_3")
                    E.dma("cq", "rp0",
                          xt[0][:, 0:1024],
                          dtAh[m0 + pk * 8:m0 + pk * 8 + 16, :]
                          .rearrange("a n -> (a n)").unsqueeze(0)
                          .broadcast_to((128, 1024)))
                    E.dma("cq", "rp1",
                          xt[1][:, 0:1024],
                          c_re[m0 + pk * 8:m0 + pk * 8 + 16, :]
                          .rearrange("a n -> (a n)").unsqueeze(0)
                          .broadcast_to((128, 1024)))
                    E.mark(f"ssm_rep_{grp}_{pk}")
                E.wait("a", "rp0", f"ssm_rep_{grp}_{pk - pk % 2}")
                ppv = f"ssm_ppk_{grp}_{pk - 1}" if pk > 0 else \
                    (f"ssm_ppk_{grp - 1}_3" if grp > 0 else None)
                if ppv is not None:
                    E.wait("a", "v", ppv)
                act(pexp[:], dtarep[pk % 2][:], AF.Exp, scale=revI[:])
                E.mark(f"ssm_pexp_{grp}_{pk}")
                E.wait("v", "a", f"ssm_pexp_{grp}_{pk}")
                smmp = f"ssm_smm_{grp}_{pk - 2}" if pk >= 2 else \
                    (f"ssm_smm_{grp - 1}_{pk + 2}" if grp > 0 else None)
                if smmp is not None:
                    E.wait("v", "t", smmp)
                E.wait("v", "rp1", f"ssm_rep_{grp}_{pk - pk % 2}")
                tt("v", ppk[pk % 2][:], pexp[:], crerep[pk % 2][:], ALU.mult)
                E.mark(f"ssm_ppk_{grp}_{pk}")
                E.wait("t", "v", f"ssm_ppk_{grp}_{pk}")
                E.wait("t", f"us{grp % 2}", f"ssm_usl_{grp}")
                scp = f"ssm_scp_{grp}_{pk - 1}" if pk > 0 else \
                    (f"ssm_scp_{grp - 1}_3" if grp > 0 else None)
                if scp is not None:
                    E.wait("t", "v", scp)
                for ch in range(8):
                    q = pk * 8 + ch
                    mm(pst[:, ch * 64:(ch + 1) * 64],
                       ppk[pk % 2][:, ch * 64:(ch + 1) * 64],
                       usl[grp % 2][:, q, :], start=True, stop=True,
                       inc=(ch == 7))
                E.mark(f"ssm_smm_{grp}_{pk}")
                E.wait("v", "t", f"ssm_smm_{grp}_{pk}")
                cp("v", sstash[:, pk * 512:(pk + 1) * 512], pst[:])
                E.mark(f"ssm_scp_{grp}_{pk}")
            if grp > 0:
                E.wait("v", "t", f"ssm_emm_{grp - 1}_31")
            E.op("v", lambda e: e.memset(hstash[:], 0.0), inc=False)
            for bb in range(4):
                cp("v", rrep3[:, :, bb], rho_sb[:, m0:m0 + 32],
                   inc=(bb == 3))
            VD()
            E.mark(f"ssm_rrep_{grp}")
            for c in range(NCK):
                tt("v", stmp3[:, 0:16, :], hs4[:, 0:16, :, c],
                   rrep3[:, 0:16, :], ALU.mult, inc=False)
                tt("v", stmp3[:, 16:32, :], hs4[:, 16:32, :, c],
                   rrep3[:, 16:32, :], ALU.mult, inc=False)
                tt("v", hs4[:, 0:16, :, c + 1], stmp3[:, 0:16, :],
                   ss4[:, 0:16, :, c], ALU.add, inc=False)
                tt("v", hs4[:, 16:32, :, c + 1], stmp3[:, 16:32, :],
                   ss4[:, 16:32, :, c], ALU.add, inc=(c == NCK - 1))
            VD()
            cp("v", hst16[:], hstash[:])
            E.mark(f"ssm_scan_{grp}")
            for sub in range(4):
                t0sem = ["t0a", "t0b"][sub % 2]
                t0prev = (f"ssm_t0c_{grp}_{(sub - 2) * 8 + 7}" if sub >= 2
                          else (f"ssm_t0c_{grp - 1}_{(sub + 2) * 8 + 7}"
                                if grp > 0 else None))
                if t0prev is not None:
                    E.wait("gq", "t", t0prev)
                E.wait("gq", "shw", f"k16_sh_{(m0 + sub * 8) // 128}")
                E.dma("gq", t0sem, ulnTt[sub % 2][:],
                      shb[m0 + sub * 8:m0 + sub * 8 + 8,
                          128:128 + 128 * 256]
                      .rearrange("ch (s l) -> s ch l", l=256)[:, :, 0:128])
                E.mark(f"ssm_t0_{grp}_{sub}")
                for j in range(8):
                    ch = sub * 8 + j
                    mch = m0 + ch
                    e2p = (f"ssm_emm_{grp}_{ch - 8}" if ch >= 8 else
                           (f"ssm_emm_{grp - 1}_{ch + 24}" if grp > 0
                            else None))
                    if e2p is not None:
                        E.wait("a", "t", e2p)
                    act(e2b[ch % 8][:], iotaL1_64[:], AF.Exp,
                        scale=dtAT_sb[:, mch:mch + 1])
                    E.mark(f"ssm_e2_{grp}_{ch}")
                for j in range(8):
                    ch = sub * 8 + j
                    E.wait("t", t0sem, f"ssm_t0_{grp}_{sub}")
                    if sub == 0 and j == 0:
                        E.wait("t", "v", f"ssm_scan_{grp}")
                        yev = (f"ssm_yev_{grp - 1}" if grp > 0 else None)
                        if yev is not None:
                            E.wait("t", "v", yev)
                    mm(pyb[:, ch * 64:(ch + 1) * 64],
                       ulnTt[sub % 2][:, j, :], usl[grp % 2][:, ch, :],
                       start=(j == 0), stop=False, inc=True)
                    E.mark(f"ssm_t0c_{grp}_{ch}")
                for j in range(8):
                    ch = sub * 8 + j
                    E.wait("t", "a", f"ssm_e2_{grp}_{ch}")
                    rhsH = h16_4[:, ch, :, 0:16]
                    mm(pyb[:, ch * 64:(ch + 1) * 64],
                       e2b[ch % 8][:], rhsH, start=False, stop=(j == 7),
                       inc=True)
                    E.mark(f"ssm_emm_{grp}_{ch}")
            E.wait("v", "t", f"ssm_emm_{grp}_31")
            if grp > 0:
                E.wait("v", "ys", f"ssm_yst_{grp - 1}")
            E.op("v", lambda e: e.tensor_copy(
                out=ystrip2.rearrange("p (j ch) -> p ch j", ch=32)
                if False else
                ystrip2.rearrange("p (j ch) -> p j ch", ch=32)
                .transpose((0, 2, 1)),
                in_=pyb.rearrange("p (ch j) -> p ch j", j=64)))
            E.mark(f"ssm_yev_{grp}")
            E.wait("gq", "v", f"ssm_yev_{grp}")
            E.dma("gq", "ys", ysp[grp].transpose((2, 0, 1, 3))
                  .rearrange("l b c ch -> l (b c) ch"),
                  ystrip2.rearrange("p (j ch) -> p j ch", ch=32))
            E.mark(f"ssm_yst_{grp}")
            E.mark(f"ssm_tdone_{grp}")
            if grp + 2 < NGRP:
                load_uslices(grp + 2)
        E.barrier("ssm")

        for i in range(ntile + 2):
            if i >= 2:
                ffn_back2("f2", i - 2)
            if i < ntile:
                ffn_front("f2", i)
            if 1 <= i <= ntile:
                ffn_back1("f2", i - 1)
            if i < ntile:
                ffn_front_tr("f2", i)
            if i >= 2:
                ffn_back2b("f2", i - 2)
        E.wait("g", "os0", E.cnt["os0"])
        E.wait("g", "os1", E.cnt["os1"])
        E.wait("sq", "os0", E.cnt["os0"])
        E.wait("sq", "os1", E.cnt["os1"])

    marks = {}
    program(Ctx(nc, sems, None, marks))

    with nc.Block() as block:
        @block.sync
        def _(_e):
            program(Ctx(nc, sems, "sq", marks))

        @block.vector
        def _(_e):
            program(Ctx(nc, sems, "v", marks))

        @block.scalar
        def _(_e):
            program(Ctx(nc, sems, "a", marks))

        @block.tensor
        def _(_e):
            program(Ctx(nc, sems, "t", marks))

        @block.gpsimd
        def _(_e):
            program(Ctx(nc, sems, "g", marks))

    return nc


_NC_CACHE = []


def kernel(**inputs):
    ub = np.ascontiguousarray(inputs["L0_attn_u"], dtype=np.float32)
    shards = [ub[i * BC:(i + 1) * BC] for i in range(NCORES)]
    common = {
        "w1": np.ascontiguousarray(inputs["L0_ffn_w1"], dtype=np.float32),
        "b1": np.ascontiguousarray(inputs["L0_ffn_b1"], dtype=np.float32),
        "w2": np.ascontiguousarray(inputs["L0_ffn_w2"], dtype=np.float32),
        "b2": np.ascontiguousarray(inputs["L0_ffn_b2"], dtype=np.float32),
        "a_log": np.ascontiguousarray(inputs["L0_attn_A_log"],
                                      dtype=np.float32),
        "d_vec": np.ascontiguousarray(inputs["L0_attn_D"], dtype=np.float32),
        "c_re": np.ascontiguousarray(inputs["L0_attn_C_re"],
                                     dtype=np.float32),
        "log_dt": np.ascontiguousarray(inputs["L0_attn_log_dt"],
                                       dtype=np.float32),
    }
    if not _NC_CACHE:
        _NC_CACHE.append(build_program())
    nc = _NC_CACHE[0]
    in_maps = [dict(common, u=s) for s in shards]
    res = run_bass_kernel_spmd(nc, in_maps, core_ids=list(range(NCORES)))
    return np.concatenate([r["out"] for r in res.results], axis=0)


if __name__ == "__main__":
    nc = build_program()
    print("built ok")


# version 11
# speedup vs baseline: 1.1518x; 1.0121x over previous
import numpy as np
import ml_dtypes
import concourse.bass as bass
import concourse.mybir as mybir
from concourse.bass_utils import run_bass_kernel_spmd

F32 = mybir.dt.float32
BF16 = mybir.dt.bfloat16
FP8 = mybir.dt.float8e4
U8 = mybir.dt.uint8
AF = mybir.ActivationFunctionType
ALU = mybir.AluOpType
AX = mybir.AxisListType

B, L, DM, FF, NS = 32, 2048, 1024, 4096, 64
NCORES = 8
BC = B // NCORES
NTILE = BC * (L // 128)
NCH = DM
NCK = L // 128
EPS = 1e-5
MAGIC = 12582912.0
MAGICM128 = 12582784.0
SHP = 257
SHLEN = 128 * SHP + 256

ENG = {"v": "vector", "a": "scalar", "t": "tensor", "g": "gpsimd",
       "gq": "gpsimd", "sq": "sync", "cq": "scalar", "tq": "gpsimd"}
ALIASES = {"v": ("v",), "a": ("a", "cq"), "t": ("t",),
           "g": ("g", "gq", "tq"), "sq": ("sq",)}


class Ctx:

    def __init__(self, nc, sems, who, marks):
        self.nc = nc
        self.sems = sems
        self.who = who
        self.aliases = set() if who is None else set(ALIASES[who])
        self.record = who is None
        self.marks = marks
        self.cnt = {k: 0 for k in sems}

    def dma(self, queue, sem, out, in_, **kw):
        self.cnt[sem] += 16
        if queue in self.aliases:
            getattr(self.nc, ENG[queue]).dma_start(out=out, in_=in_, **kw) \
                .then_inc(self.sems[sem], 16)
        return self.cnt[sem]

    def dmat(self, sem, out, in_):
        self.cnt[sem] += 16
        if "sq" in self.aliases:
            self.nc.sync.dma_start_transpose(out=out, in_=in_).then_inc(
                self.sems[sem], 16)
        return self.cnt[sem]

    def op(self, engine, fn, inc=True):
        if inc:
            self.cnt[engine] += 1
        if engine in self.aliases:
            ins = fn(getattr(self.nc, ENG[engine]))
            if inc:
                ins.then_inc(self.sems[engine], 1)
        return self.cnt[engine]

    def mark(self, label):
        if self.record:
            self.marks[label] = dict(self.cnt)

    def val(self, label, sem):
        m = self.marks.get(label)
        if m is None:
            if not self.record:
                raise KeyError(f"mark {label} missing")
            return 0
        return m[sem]

    def wait(self, engine, sem, label_or_val):
        v = (self.val(label_or_val, sem) if isinstance(label_or_val, str)
             else label_or_val)
        if v <= 0:
            return
        if engine in self.aliases:
            getattr(self.nc, ENG[engine]).wait_ge(self.sems[sem], v)

    def barrier(self, label):
        self.mark("bar_" + label)
        snap = (dict(self.cnt) if self.record else self.marks["bar_" + label])
        if self.who is not None:
            e = getattr(self.nc, ENG[self.who])
            for sem, v in snap.items():
                if v > 0:
                    e.wait_ge(self.sems[sem], v)


def build_program(ntile=NTILE, ngrp=32, k16g=8, debug=False, phases='all'):
    nc = bass.Bass("TRN2", target_bir_lowering=False,
                   detect_race_conditions=False)

    u = nc.dram_tensor("u", [BC, L, DM], F32, kind="ExternalInput").ap()
    w1 = nc.dram_tensor("w1", [FF, DM], F32, kind="ExternalInput").ap()
    b1 = nc.dram_tensor("b1", [FF], F32, kind="ExternalInput").ap()
    w2 = nc.dram_tensor("w2", [DM, FF], F32, kind="ExternalInput").ap()
    b2 = nc.dram_tensor("b2", [DM], F32, kind="ExternalInput").ap()
    a_log = nc.dram_tensor("a_log", [NS], F32, kind="ExternalInput").ap()
    d_vec = nc.dram_tensor("d_vec", [DM], F32, kind="ExternalInput").ap()
    c_re = nc.dram_tensor("c_re", [DM, NS], F32, kind="ExternalInput").ap()
    log_dt = nc.dram_tensor("log_dt", [DM], F32, kind="ExternalInput").ap()
    outh = nc.dram_tensor("out", [BC, L, DM], F32, kind="ExternalOutput").ap()

    dkind = {"kind": "ExternalOutput"} if debug else {}
    xsp = nc.dram_tensor("xsp", [NTILE, 128, DM], F32, **dkind).ap()
    ulnT = nc.dram_tensor("ulnT", [DM, BC, L], BF16, **dkind).ap()
    ysp = nc.dram_tensor("ysp", [NCH // 32, BC, NCK, 128, 32], F32, **dkind).ap()
    k16h = nc.dram_tensor("k16h", [NCH, 256], BF16, **dkind).ap()
    shb = nc.dram_tensor("shb", [NCH, SHLEN], BF16, **dkind).ap()
    dtAh = nc.dram_tensor("dtAh", [NCH, NS], F32, **dkind).ap()
    tmp64 = nc.dram_tensor("tmp64", [NS], F32).ap()
    tmp1 = nc.dram_tensor("tmp1", [4], F32).ap()

    T = nc.alloc_sbuf_tensor
    w1T = T("w1T", [128, 8, FF], FP8)
    w2T = T("w2T", [128, 32, DM], FP8)
    b2hrep = T("b2hrep", [128, DM], F32)
    dtA_sb = T("dtA_sb", [128, 8, NS], F32)
    cre_sb = T("cre_sb", [128, 8, NS], F32)
    d_sb = T("d_sb", [128, 8], F32)
    dtAT_sb = T("dtAT_sb", [64, DM], F32)
    rho_sb = T("rho_sb", [64, DM], F32)
    iotaL = T("iotaL", [128, 128], F32)
    iotaL1_64 = T("iotaL1_64", [64, 128], F32)
    revI = T("revI", [128, 1], F32)
    iotaP = T("iotaP", [128, 1], F32)
    ones_c = T("ones_c", [128, 1], F32)
    magcol = T("magcol", [128, 1], F32)
    m128col = T("m128col", [128, 1], F32)
    id128 = T("id128", [128, 128], F32)
    zt16 = T("zt16", [128, 128], BF16)
    invsw1 = T("invsw1", [128, 1], F32)
    invsw2 = T("invsw2", [128, 1], F32)
    ldt_sb = T("ldt_sb", [128, 8], F32)
    al_sb = T("al_sb", [1, NS], F32)
    narep = T("narep", [128, NS], F32)

    xt = [T(f"xt{i}", [128, DM], F32) for i in range(4)]
    sqdX = T("sqdX", [128, DM], F32)
    w1s = T("w1s", [128, 1], F32)
    B4h = T("B4h", [2, FF], BF16)
    B4l = T("B4l", [2, FF], BF16)
    id16 = T("id16", [128, 128], BF16)
    sxp = T("sxp", [128, 16], BF16)
    sxT = T("sxT", [2, 4, 128], BF16)
    qu8x = T("qu8x", [128, DM], U8)
    qx16 = T("qx16", [128, DM], BF16)
    qxT = [T(f"qxT{i}", [128, 8, 128], BF16) for i in range(2)]
    hb = T("hb", [128, 1024], F32)
    sqd = T("sqd", [128, 1024], F32)
    t2b = T("t2b", [128, DM], F32)
    gbuf = T("gbuf", [128, FF], F32)
    qu8 = T("qu8", [128, FF], U8)
    qg16 = T("qg16", [128, FF], BF16)
    qgT = [T(f"qgT{i}", [128, 32, 128], BF16) for i in range(2)]
    xnew = [T(f"xnew{i}", [128, DM], F32) for i in range(2)]
    uln16 = T("uln16", [128, DM], BF16)
    ulnTt = [T(f"ulnTt{i}", [128, 8, 128], BF16) for i in range(2)]
    cols = [T(f"cols{i}", [128, 64], F32) for i in range(4)]

    hstash = T("hstash", [64, 2176], F32)
    ppk = [qgT[i][:].rearrange("p a b -> p (a b)")[:, 0:512].bitcast(BF16)
           if False else qgT[i][:, 0:4, :].rearrange("p a b -> p (a b)")
           for i in range(2)]
    usl = [xnew[i][:].bitcast(BF16).rearrange("p (a b) -> p a b", b=64)
           for i in range(2)]
    t0t = [ulnTt[0][:, j, :] for j in range(8)]
    rrep = t2b[0:64, 128:256]
    scantmp = t2b[0:64, 0:128]
    e2b = [qu8[:].bitcast(BF16)[0:64, 128 * j:128 * (j + 1)] for j in range(8)]
    pexp = sqd[0:128, 0:512]
    dtarep = [xt[0][:, 0:512], xt[0][:, 512:1024]]
    crerep = [xt[1][:, 0:512], xt[1][:, 512:1024]]
    sstash = gbuf[0:64, 0:2048]
    hst16 = qg16[0:64, 0:2176]
    ystrip2 = gbuf[:, 2048:4096]
    k16t = hb[:, 0:128]
    en2 = hb[:, 128:256]
    en3 = hb[:, 256:384]
    k16b = uln16[:, 0:128]

    pAll = nc.alloc_psum_tensor("pAll", [128, 4096], F32)
    pq = [pAll[:, 0:1024], pAll[:, 1024:2048]]
    p2 = pAll[:, 2048:3072]
    pst = pAll[0:64, 0:512]
    ptr = pAll[0:64, 3072:3584]
    pyb = pAll[:, 2048:4096]
    pmean = pAll[0:1, 3072:3073]

    if debug:
        dbg_w1T = nc.dram_tensor("dbg_w1T", [128, 8, FF], U8,
                                 kind="ExternalOutput").ap()
        dbg_w2T = nc.dram_tensor("dbg_w2T", [128, 32, DM], U8,
                                 kind="ExternalOutput").ap()
        dbg_qx = nc.dram_tensor("dbg_qx", [128, DM], BF16,
                                kind="ExternalOutput").ap()
        dbg_qg = nc.dram_tensor("dbg_qg", [128, FF], BF16,
                                kind="ExternalOutput").ap()
        dbg_g = nc.dram_tensor("dbg_g", [128, FF], F32,
                               kind="ExternalOutput").ap()
        dbg_cols = nc.dram_tensor("dbg_cols", [128, 96], F32,
                                  kind="ExternalOutput").ap()
        dbg_xt0 = nc.dram_tensor("dbg_xt0", [128, DM], F32,
                                 kind="ExternalOutput").ap()
        dbg_xt1 = nc.dram_tensor("dbg_xt1", [128, DM], F32,
                                 kind="ExternalOutput").ap()
    sems = {k: nc.alloc_semaphore("sem_" + k)
            for k in ("v", "a", "t", "g",
                      "xt0", "xt1", "os0", "os1", "ul0", "ul1", "k16s",
                      "shw", "ys", "tx", "tg", "tu", "us0", "us1",
                      "t0a", "t0b", "t0c", "t0d", "t0e", "t0f", "t0g", "t0h", "rp0", "rp1", "p0l",
                      "gb", "cqm")}

    CN = ["s1", "s2", "mx", "mn", "m", "ms", "msq", "var", "vpe", "rec", "inv",
          "ra", "ra2", "rec2", "rmsc", "d1", "d2", "mdev", "mr", "mr2", "mrc",
          "recm", "sx", "al", "tm", "beta", "rsx", "c1", "g2s", "gmax", "rme",
          "rec3", "rmsi", "gm2", "gmc", "recg", "sh", "alh", "rsh", "c2",
          "c2h", "sc0", "sc1", "sc2", "sc3", "acc", "tmpc", "mean",
          "s1u", "s2u", "mu", "msu", "msqu", "varu", "vpeu", "recu", "invu"]
    COL = {n: i for i, n in enumerate(CN)}
    assert len(CN) <= 64

    def program(E: Ctx):
        SD = AX.X

        def ts(en, o, i0, s1, s2, o0, o1=None, acc=None, inc=True):
            kw = {}
            if o1 is not None:
                kw["op1"] = o1
            if acc is not None:
                kw["accum_out"] = acc
            return E.op(en, lambda e: e.tensor_scalar(
                out=o, in0=i0, scalar1=s1, scalar2=s2, op0=o0, **kw),
                inc=True)

        def tt(en, o, i0, i1, o0, inc=True):
            return E.op(en, lambda e: e.tensor_tensor(
                out=o, in0=i0, in1=i1, op=o0), inc=True)

        def VD():
            E.op("v", lambda e: e.tensor_scalar(
                out=cols[0][:, 63:64], in0=ones_c[:], scalar1=0.0,
                scalar2=None, op0=ALU.mult), inc=True)

        def act(o, i0, fn, scale=None, bias=None, acc=None, inc=True):
            kw = {}
            if scale is not None:
                kw["scale"] = scale
            if bias is not None:
                kw["bias"] = bias
            if acc is not None:
                kw["accum_out"] = acc
            return E.op("a", lambda e: e.activation(
                out=o, in_=i0, func=fn, **kw), inc=inc)

        def cp(en, o, i0, inc=True):
            return E.op(en, lambda e: e.tensor_copy(out=o, in_=i0), inc=True)

        def rcp(o, i0, inc=True):
            return E.op("v", lambda e: e.reciprocal(out=o, in_=i0), inc=True)

        def mm(o, lhsT, rhs, start, stop, inc=False):
            return E.op("t", lambda e: e.matmul(
                o, lhsT, rhs, start=start, stop=stop, skip_group_check=True),
                inc=inc)

        def col(name, buf=0):
            return cols[buf][:, COL[name]:COL[name] + 1]

        E.op("g", lambda e: e.iota(iotaL[:], pattern=[[1, 128]], base=0,
                                   channel_multiplier=0,
                                   allow_small_or_imprecise_dtypes=True))
        E.op("g", lambda e: e.iota(iotaL1_64[:], pattern=[[1, 128]], base=1,
                                   channel_multiplier=0,
                                   allow_small_or_imprecise_dtypes=True))
        E.op("g", lambda e: e.iota(revI[:], pattern=[[0, 1]], base=127,
                                   channel_multiplier=-1,
                                   allow_small_or_imprecise_dtypes=True))
        E.op("g", lambda e: e.iota(iotaP[:], pattern=[[0, 1]], base=0,
                                   channel_multiplier=1,
                                   allow_small_or_imprecise_dtypes=True))
        E.op("g", lambda e: e.iota(ones_c[:], pattern=[[0, 1]], base=1,
                                   channel_multiplier=0,
                                   allow_small_or_imprecise_dtypes=True))
        E.mark("p0_iota")

        E.dma("gq", "p0l", xt[0][:], b2.unsqueeze(0).broadcast_to((128, DM)))
        E.dma("gq", "p0l", ldt_sb[:], log_dt.rearrange("(g p) -> p g", p=128), allow_slow_non_contiguous=True)
        E.dma("gq", "p0l", al_sb[:], a_log.unsqueeze(0))
        E.dma("gq", "p0l", d_sb[:], d_vec.rearrange("(g p) -> p g", p=128), allow_slow_non_contiguous=True)
        E.dma("gq", "p0l", cre_sb[:], c_re.rearrange("(g p) n -> p g n", p=128))
        E.mark("p0_loads")

        E.wait("v", "g", "p0_iota")
        E.wait("v", "p0l", "p0_loads")
        ts("v", id128[:], iotaL[:], iotaP[:], None, ALU.is_equal)
        VD()
        cp("v", id16[:], id128[:])
        ts("v", b2hrep[:], xt[0][:], 0.5, None, ALU.mult)
        ts("v", magcol[:], ones_c[:], MAGIC, None, ALU.mult, inc=False)
        ts("v", m128col[:], ones_c[:], -128.0, None, ALU.mult, inc=False)
        E.op("v", lambda e: e.memset(zt16[:], 0.0))
        E.mark("p0_v1")
        E.wait("a", "p0l", "p0_loads")
        dtl = act(ldt_sb[:], ldt_sb[:], AF.Exp)
        aexp = act(al_sb[:], al_sb[:], AF.Exp)
        E.wait("v", "a", aexp)
        ts("v", al_sb[:], al_sb[:], -1.0, None, ALU.mult)
        E.mark("p0_na")
        E.wait("cq", "v", "p0_na")
        w64 = E.dma("cq", "cqm", tmp64.unsqueeze(0), al_sb[:])
        E.wait("cq", "cqm", w64)
        E.dma("cq", "cqm", narep[:], tmp64.unsqueeze(0).broadcast_to((128, NS)))
        E.mark("p0_narep")
        E.wait("v", "cqm", "p0_narep")
        E.wait("v", "a", dtl)
        for gi in range(8):
            ts("v", dtA_sb[:, gi, :], narep[:], ldt_sb[:, gi:gi + 1], None,
               ALU.mult, inc=(gi == 7))
        E.mark("p0_dta")
        E.wait("cq", "v", "p0_dta")
        E.dma("cq", "cqm", dtAh.rearrange("(g p) n -> p g n", p=128), dtA_sb[:])
        E.wait("t", "v", "p0_dta")
        for gi in range(8):
            if gi > 0:
                E.wait("t", "v", f"p0_dtat_{gi - 1}")
            tr = E.op("t", lambda e, gi=gi: e.transpose(
                ptr[0:64, 0:128], dtA_sb[:, gi, :], id128[:]), inc=True)
            E.wait("v", "t", tr)
            cp("v", dtAT_sb[:, gi * 128:(gi + 1) * 128], ptr[0:64, 0:128])
            E.mark(f"p0_dtat_{gi}")
        E.mark("p0_dtat")
        E.wait("a", "v", "p0_dtat")
        act(rho_sb[:], dtAT_sb[:], AF.Exp, scale=128.0)
        E.mark("p0_rho")

        def wprep(wap, nslab, sldm, wT, invsw, label):
            bufs = ([gbuf, gbuf] if sldm == 4096 else xt)
            qb = qg16[:, 0:sldm] if sldm == 4096 else uln16[:]
            stg = qgT[0] if sldm == 4096 else ulnTt[0]
            acc_col = col("acc", 1)
            tmpc = col("tmpc", 1)
            E.wait("gq", "v", "p0_v1")
            if label == "w2p":
                E.wait("gq", "v", "b4_done")
            ts("v", acc_col, ones_c[:], 0.0, None, ALU.mult)
            for sl in range(nslab):
                buf = bufs[sl % 2][:, 0:sldm]
                bsem = "gb" if sldm == 4096 else f"xt{sl % 2}"
                back = 1 if sldm == 4096 else 2
                if sl >= back:
                    E.wait("gq", "v", f"{label}_rd_{sl - back}")
                ld = E.dma("gq", bsem, buf, wap[sl * 128:(sl + 1) * 128, :])
                E.wait("v", bsem, ld)
                E.op("v", lambda e, b=buf: e.reduce_sum(
                    out=tmpc, in_=b, axis=SD, apply_absolute_value=True),
                    inc=False)
                VD()
                tt("v", acc_col, acc_col, tmpc, ALU.add)
                E.mark(f"{label}_rd_{sl}")
            E.wait("t", "v", f"{label}_rd_{nslab - 1}")
            mmv = E.op("t", lambda e: e.matmul(
                pmean, ones_c[:], acc_col, start=True, stop=True), inc=True)
            E.wait("v", "t", mmv)
            ts("v", cols[1][0:1, COL["mean"]:COL["mean"] + 1], pmean,
               1.0 / (FF * DM), 1e-5, ALU.mult, ALU.max)
            E.mark(f"{label}_mcl")
            E.wait("cq", "v", f"{label}_mcl")
            wt1 = E.dma("cq", "cqm", tmp1[0:1].unsqueeze(0),
                        cols[1][0:1, COL["mean"]:COL["mean"] + 1])
            E.wait("cq", "cqm", wt1)
            rb = E.dma("cq", "cqm", invsw[:],
                       tmp1[0:1].unsqueeze(0).broadcast_to((128, 1)))
            E.wait("v", "cqm", rb)
            rcp(tmpc, invsw[:])
            VD()
            if label == "w1p":
                cp("v", w1s[:], tmpc)
            E.mark(f"{label}_scale")
            for sl in range(nslab):
                buf = bufs[sl % 2][:, 0:sldm]
                bsem = "gb" if sldm == 4096 else f"xt{sl % 2}"
                if sl > 0:
                    E.wait("gq", "v", f"{label}_q_{sl - 1}")
                ld = E.dma("gq", bsem, buf, wap[sl * 128:(sl + 1) * 128, :])
                E.wait("v", bsem, ld)
                if sl > 0:
                    E.wait("v", "tx", f"{label}_tp_{sl - 1}")
                ts("v", buf, buf, tmpc, 1.4, ALU.mult, ALU.min, inc=False)
                ts("v", buf, buf, -1.4, MAGIC, ALU.max, ALU.add, inc=False)
                ts("v", qu8[:, 0:sldm], buf, MAGICM128, None, ALU.subtract,
                   inc=False)
                ts("v", qb, qu8[:, 0:sldm], -128.0, None, ALU.add)
                E.mark(f"{label}_q_{sl}")
                E.wait("sq", "v", f"{label}_q_{sl}")
                if sl > 0:
                    E.wait("sq", "v", f"{label}_cast_{sl - 1}")
                E.dmat("tx", stg[:], qb)
                E.mark(f"{label}_tp_{sl}")
                E.wait("v", "tx", f"{label}_tp_{sl}")
                cp("v", wT[:, :, sl * 128:(sl + 1) * 128], stg[:])
                E.mark(f"{label}_cast_{sl}")

        wprep(w1, 32, 1024, w1T, invsw1, "w1p")
        stage2 = gbuf[0:2, 0:4096]
        b1ld = E.dma("gq", "p0l", stage2,
                     b1.unsqueeze(0).broadcast_to((2, FF)))
        E.wait("v", "p0l", b1ld)
        ts("v", stage2, stage2, w1s[0:2, 0:1], None, ALU.mult)
        VD()
        cp("v", B4h[:], stage2)
        VD()
        tt("v", stage2, stage2, B4h[:], ALU.subtract)
        VD()
        cp("v", B4l[:], stage2)
        E.mark("b4_done")
        wprep(w2, 8, 4096, w2T, invsw2, "w2p")
        E.barrier("p0")

        def ffn_front(ph, i):
            bse, cse = i // NCK, i % NCK
            pf = f"{ph}_{i}"
            cI = lambda n: col(n, i % 4)
            if i >= 4:
                E.wait("v", "a", f"{ph}_ev2_{i - 4}")
            if i >= 4:
                E.wait("gq", "v", f"{ph}_xtfree_{i - 4}")
            xsem = f"xt{i % 2}"
            if ph == "f1":
                ld = E.dma("gq", xsem, xt[i % 4][:],
                           u[bse, cse * 128:(cse + 1) * 128, :])
            else:
                pl = E.dma("gq", xsem, xt[i % 4][:], xsp[i, :, :])
                E.wait("gq", xsem, pl)
                ld = E.dma("gq", xsem,
                           xt[i % 4][:].rearrange("p (g ch) -> p g ch", ch=32),
                           ysp.transpose((3, 0, 4, 1, 2))[:, :, :, bse, cse],
                           accum_op=ALU.add)
            E.mark(f"{pf}_xt")
            E.wait("v", xsem, ld)
            E.op("v", lambda e: e.reduce_sum(out=cI("s1"), in_=xt[i % 4][:],
                                             axis=SD), inc=False)
            E.op("v", lambda e: e.reduce_max(out=cI("mx"), in_=xt[i % 4][:],
                                             axis=SD), inc=False)
            E.op("v", lambda e: e.tensor_reduce(
                out=cI("mn"), in_=xt[i % 4][:], axis=SD, op=ALU.min))
            E.mark(f"{pf}_vst")
            E.wait("a", xsem, ld)
            if i >= 1:
                E.wait("a", "v", f"{ph}_sqdXfree_{i - 1}")
            act(sqdX[:], xt[i % 4][:], AF.Square, acc=cI("s2"))
            E.mark(f"{pf}_sqa")
            E.wait("v", "a", f"{pf}_sqa")
            ts("v", cI("m"), cI("s1"), 1.0 / DM, None, ALU.mult, inc=False)
            ts("v", cI("ms"), cI("s2"), 1.0 / DM, None, ALU.mult, inc=False)
            tt("v", cI("msq"), cI("m"), cI("m"), ALU.mult, inc=False)
            VD()
            tt("v", cI("var"), cI("ms"), cI("msq"), ALU.subtract, inc=False)
            VD()
            ts("v", cI("vpe"), cI("var"), EPS, None, ALU.add, inc=False)
            VD()
            rcp(cI("rec"), cI("vpe"))
            E.mark(f"{pf}_r1")
            VD()
            tt("v", cI("ra"), cI("var"), cI("rec"), ALU.mult, inc=False)
            VD()
            ts("v", cI("ra2"), cI("ra"), EPS, None, ALU.add, inc=False)
            VD()
            rcp(cI("rec2"), cI("ra2"))
            E.mark(f"{pf}_r2")
            E.wait("a", "v", f"{pf}_r2")
            act(cI("inv"), cI("rec"), AF.Sqrt, inc=False)
            act(cI("rmsc"), cI("rec2"), AF.Sqrt)
            E.mark(f"{pf}_sq2")
            E.wait("v", "a", f"{pf}_sq2")
            tt("v", cI("d1"), cI("mx"), cI("m"), ALU.subtract, inc=False)
            tt("v", cI("d2"), cI("m"), cI("mn"), ALU.subtract, inc=False)
            VD()
            tt("v", cI("mdev"), cI("d1"), cI("d2"), ALU.max, inc=False)
            VD()
            tt("v", cI("mr"), cI("mdev"), cI("inv"), ALU.mult, inc=False)
            VD()
            tt("v", cI("mr2"), cI("mr"), cI("rmsc"), ALU.mult, inc=False)
            VD()
            ts("v", cI("mrc"), cI("mr2"), 1e-5, None, ALU.max, inc=False)
            VD()
            rcp(cI("recm"), cI("mrc"), inc=False)
            VD()
            ts("v", cI("sx"), cI("recm"), 127.0, None, ALU.mult, inc=False)
            VD()
            tt("v", cI("al"), cI("inv"), cI("rmsc"), ALU.mult, inc=False)
            VD()
            tt("v", cI("al"), cI("al"), cI("sx"), ALU.mult, inc=False)
            VD()
            tt("v", cI("tm"), cI("m"), cI("al"), ALU.mult, inc=False)
            VD()
            ts("v", cI("beta"), cI("tm"), -1.0, None, ALU.mult, inc=False)
            rcp(cI("rsx"), cI("sx"), inc=False)
            VD()
            tt("v", cI("c1"), cI("rsx"), invsw1[:], ALU.mult)
            sl0 = (i % 4) * 4
            cp("v", sxp[:, sl0:sl0 + 1], cI("sx"))
            VD()
            tt("v", cI("tm"), cI("sx"), sxp[:, sl0:sl0 + 1], ALU.subtract)
            VD()
            cp("v", sxp[:, sl0 + 1:sl0 + 2], cI("tm"))
            E.mark(f"{pf}_cols")

        def ffn_front_tr(ph, i):
            pf = f"{ph}_{i}"
            cI = lambda n: col(n, i % 4)
            sl0 = (i % 4) * 4
            E.wait("t", "v", f"{pf}_cols")
            if i >= 1:
                E.wait("t", "v", f"{ph}_sxT_{i - 1}")
            E.op("t", lambda e: e.transpose(
                pAll[0:2, 3072:3136].bitcast(BF16),
                sxp[:, sl0:sl0 + 2], id16[:]), inc=True)
            E.mark(f"{pf}_sxtr")
            E.wait("v", "t", f"{pf}_sxtr")
            cp("v", sxT[:, i % 4, :],
               pAll[0:2, 3072:3136].bitcast(BF16)[:, 0:128])
            E.mark(f"{ph}_sxT_{i}")
            if i >= 1:
                E.wait("v", "tx", f"{ph}_qxT_{i - 1}")
            ts("v", sqdX[:], xt[i % 4][:], cI("al"), cI("beta"), ALU.mult,
               ALU.add, inc=False)
            ts("v", qu8x[:], sqdX[:], MAGIC, MAGICM128, ALU.add,
               ALU.subtract, inc=False)
            ts("v", qx16[:], qu8x[:], -128.0, None, ALU.add)
            E.mark(f"{ph}_sqdXfree_{i}")
            E.wait("sq", "v", f"{ph}_sqdXfree_{i}")
            if i >= 2:
                E.wait("sq", "t", f"{ph}_mm1d_{i - 2}")
            E.dmat("tx", qxT[i % 2][:], qx16[:])
            E.mark(f"{ph}_qxT_{i}")

        def ffn_back1(ph, i):
            bse, cse = i // NCK, i % NCK
            pf = f"{ph}_{i}"
            cI = lambda n: col(n, i % 4)
            E.wait("t", "tx", f"{ph}_qxT_{i}")
            E.wait("t", "v", f"{pf}_cols")
            E.wait("t", "v", f"{ph}_sxT_{i}")
            for q in range(4):
                prev = (f"{ph}_ev1_{i}_{q - 2}" if q >= 2 else
                        (f"{ph}_ev1_{i - 1}_{q + 2}" if i >= 1 else None))
                if prev is not None:
                    E.wait("t", "a", prev)
                for dt_ in range(8):
                    for hh in range(2):
                        mm(pq[q % 2][:, hh * 512:(hh + 1) * 512],
                           qxT[i % 2][:, dt_, :],
                           w1T[:, dt_, q * 1024 + hh * 512:
                               q * 1024 + (hh + 1) * 512],
                           start=(dt_ == 0), stop=False, inc=False)
                for hh in range(2):
                    mm(pq[q % 2][:, hh * 512:(hh + 1) * 512],
                       sxT[:, i % 4, :],
                       B4h[:, q * 1024 + hh * 512:q * 1024 + (hh + 1) * 512],
                       start=False, stop=False, inc=False)
                    mm(pq[q % 2][:, hh * 512:(hh + 1) * 512],
                       sxT[:, i % 4, :],
                       B4l[:, q * 1024 + hh * 512:q * 1024 + (hh + 1) * 512],
                       start=False, stop=True, inc=(hh == 1))
                E.mark(f"{ph}_mm1_{i}_{q}")
                E.wait("a", "t", f"{ph}_mm1_{i}_{q}")
                E.wait("a", "v", f"{pf}_cols")
                if q == 0 and i >= 1:
                    E.wait("a", "v", f"{ph}_qgq_{i - 1}")
                act(gbuf[:, q * 1024:(q + 1) * 1024], pq[q % 2][:], AF.Gelu,
                    scale=cI("c1"))
                E.mark(f"{ph}_ev1_{i}_{q}")
                act(sqd[:], gbuf[:, q * 1024:(q + 1) * 1024], AF.Square,
                    acc=cI("sc%d" % q))
                E.mark(f"{ph}_gsq_{i}_{q}")
                if q == 3:
                    E.mark(f"{ph}_mm1d_{i}")
            E.mark(f"{ph}_back1_{i}")

        def ffn_back2(ph, i):
            bse, cse = i // NCK, i % NCK
            pf = f"{ph}_{i}"
            cI = lambda n: col(n, i % 4)
            E.wait("v", "a", f"{ph}_gsq_{i}_3")
            tt("v", cI("g2s"), cI("sc0"), cI("sc1"), ALU.add, inc=False)
            tt("v", cI("tm"), cI("sc2"), cI("sc3"), ALU.add, inc=False)
            VD()
            tt("v", cI("g2s"), cI("g2s"), cI("tm"), ALU.add, inc=False)
            E.op("v", lambda e: e.reduce_max(
                out=cI("gmax"), in_=gbuf[:], axis=SD,
                apply_absolute_value=True), inc=False)
            ts("v", cI("rme"), cI("g2s"), 1.0 / FF, EPS, ALU.mult, ALU.add,
               inc=False)
            VD()
            rcp(cI("rec3"), cI("rme"))
            E.mark(f"{pf}_r3")
            E.wait("a", "v", f"{pf}_r3")
            act(cI("rmsi"), cI("rec3"), AF.Sqrt)
            E.mark(f"{pf}_s3")
            E.wait("v", "a", f"{pf}_s3")
            tt("v", cI("gm2"), cI("gmax"), cI("rmsi"), ALU.mult, inc=False)
            VD()
            ts("v", cI("gmc"), cI("gm2"), 1e-5, None, ALU.max, inc=False)
            VD()
            rcp(cI("recg"), cI("gmc"), inc=False)
            VD()
            ts("v", cI("sh"), cI("recg"), 127.0, None, ALU.mult, inc=False)
            VD()
            tt("v", cI("alh"), cI("rmsi"), cI("sh"), ALU.mult, inc=False)
            rcp(cI("rsh"), cI("sh"), inc=False)
            VD()
            tt("v", cI("c2"), cI("rsh"), invsw2[:], ALU.mult, inc=False)
            VD()
            ts("v", cI("c2h"), cI("c2"), 0.5, None, ALU.mult)
            E.mark(f"{pf}_cols2")
            E.wait("a", "v", f"{pf}_cols2")
            if i >= 1:
                E.wait("a", "tg", f"{ph}_qgT_{i - 1}")
            act(gbuf[:], gbuf[:], AF.Identity, scale=cI("alh"), bias=magcol[:])
            E.mark(f"{pf}_qgA_{i}")
            E.wait("v", "a", f"{pf}_qgA_{i}")
            ts("v", qu8[:], gbuf[:], MAGICM128, None, ALU.subtract)
            E.mark(f"{ph}_qgq_{i}")
            E.wait("a", "v", f"{ph}_qgq_{i}")
            act(qg16[:], qu8[:], AF.Identity, bias=m128col[:])
            E.mark(f"{ph}_qg3_{i}")
            E.wait("sq", "a", f"{ph}_qg3_{i}")
            if i >= 2:
                E.wait("sq", "t", f"{ph}_mm2_{i - 2}")
            E.dmat("tg", qgT[i % 2][:], qg16[:])
            E.mark(f"{ph}_qgT_{i}")

        def ffn_back2b(ph, i):
            bse, cse = i // NCK, i % NCK
            pf = f"{ph}_{i}"
            cI = lambda n: col(n, i % 4)
            E.wait("t", "tg", f"{ph}_qgT_{i}")
            if i >= 1:
                E.wait("t", "a", f"{ph}_ev2_{i - 1}")
            for ft in range(32):
                for dc in range(2):
                    mm(p2[:, dc * 512:(dc + 1) * 512],
                       qgT[i % 2][:, ft, :],
                       w2T[:, ft, dc * 512:(dc + 1) * 512],
                       start=(ft == 0), stop=(ft == 31),
                       inc=(ft == 31 and dc == 1))
            E.mark(f"{ph}_mm2_{i}")
            E.wait("a", "t", f"{ph}_mm2_{i}")
            if i >= 1:
                E.wait("a", "v", f"{ph}_t3_{i - 1}")
            act(t2b[:], p2[:], AF.Identity, scale=cI("c2h"))
            E.mark(f"{ph}_ev2_{i}")
            E.wait("v", "a", f"{ph}_ev2_{i}")
            tt("v", t2b[:], t2b[:], b2hrep[:], ALU.add)
            E.mark(f"{ph}_t3_{i}")
            if i >= 2:
                E.wait("v", f"os{i % 2}", f"{ph}_ost_{i - 2}")
            tt("v", xnew[i % 2][:], xt[i % 4][:], t2b[:], ALU.add)
            E.mark(f"{ph}_xn_{i}")
            E.mark(f"{ph}_xtfree_{i}")
            E.wait("gq", "v", f"{ph}_xn_{i}")
            if ph == "f1":
                E.dma("gq", f"os{i % 2}", xsp[i, :, :], xnew[i % 2][:])
            else:
                E.dma("gq", f"os{i % 2}",
                      outh[bse, cse * 128:(cse + 1) * 128, :],
                      xnew[i % 2][:])
            E.mark(f"{ph}_ost_{i}")
            if ph == "f1":
                E.op("v", lambda e: e.reduce_sum(
                    out=cI("s1u"), in_=xnew[i % 2][:], axis=SD), inc=False)
                E.wait("a", "v", f"{ph}_xn_{i}")
                act(sqd[:], xnew[i % 2][:], AF.Square, acc=cI("s2u"))
                E.mark(f"{pf}_sqa2")
                E.wait("v", "a", f"{pf}_sqa2")
                ts("v", cI("mu"), cI("s1u"), 1.0 / DM, None, ALU.mult,
                   inc=False)
                ts("v", cI("msu"), cI("s2u"), 1.0 / DM, None, ALU.mult,
                   inc=False)
                tt("v", cI("msqu"), cI("mu"), cI("mu"), ALU.mult, inc=False)
                VD()
                tt("v", cI("varu"), cI("msu"), cI("msqu"), ALU.subtract,
                   inc=False)
                VD()
                ts("v", cI("vpeu"), cI("varu"), EPS, None, ALU.add, inc=False)
                VD()
                rcp(cI("recu"), cI("vpeu"))
                E.mark(f"{pf}_r4")
                E.wait("a", "v", f"{pf}_r4")
                act(cI("invu"), cI("recu"), AF.Sqrt)
                E.mark(f"{pf}_s4")
                E.wait("v", "a", f"{pf}_s4")
                if i >= 1:
                    E.wait("v", "tu", f"{ph}_ulT_{i - 1}")
                ts("v", uln16[:], xnew[i % 2][:], cI("mu"), cI("invu"),
                   ALU.subtract, ALU.mult)
                E.mark(f"{pf}_uln")
                E.wait("sq", "v", f"{pf}_uln")
                if i >= 2:
                    E.wait("sq", f"ul{i % 2}", f"{ph}_ulst_{i - 2}")
                E.dmat("tu", ulnTt[i % 2][:], uln16[:])
                E.mark(f"{ph}_ulT_{i}")
                E.wait("gq", "tu", f"{ph}_ulT_{i}")
                E.dma("gq", f"ul{i % 2}",
                      ulnT.rearrange("(dt dp) b (c s) -> dp dt b c s",
                                     dp=128, s=128)[:, :, bse, cse, :],
                      ulnTt[i % 2][:])
                E.mark(f"{ph}_ulst_{i}")
            E.mark(f"{ph}_done_{i}")

        for i in range(ntile + 2):
            if i >= 2:
                ffn_back2("f1", i - 2)
            if i < ntile:
                ffn_front("f1", i)
            if 1 <= i <= ntile:
                ffn_back1("f1", i - 1)
            if i < ntile:
                ffn_front_tr("f1", i)
            if i >= 2:
                ffn_back2b("f1", i - 2)
        E.barrier("f1")
        if phases == "f1":
            if debug:
                E.dma("gq", "ys", dbg_w1T.bitcast(FP8), w1T[:])
                E.dma("gq", "ys", dbg_w2T.bitcast(FP8), w2T[:])
                E.dma("gq", "ys", dbg_qx, qx16[:])
                E.dma("gq", "ys", dbg_qg, qg16[:])
                E.dma("gq", "ys", dbg_g, gbuf[:])
                E.dma("gq", "ys", dbg_cols[:, 0:64], cols[0][:])
                E.dma("gq", "ys", dbg_cols[:, 64:96], cols[1][:, 0:32])
                E.dma("gq", "ys", dbg_xt0, xt[0][:])
                E.dma("gq", "ys", dbg_xt1, xt[1][:])
            E.wait("g", "ys", E.cnt["ys"])
            E.wait("g", "os0", E.cnt["os0"])
            E.wait("g", "os1", E.cnt["os1"])
            return

        def k16_group(gi):
            E.op("v", lambda e: e.memset(k16t[:], 0.0))
            E.mark(f"k16_z_{gi}")
            E.wait("a", "v", f"k16_z_{gi}")
            en3s = [en3, hb[:, 384:512]]
            for n in range(NS):
                act(en2[:], iotaL[:], AF.Exp, scale=dtA_sb[:, gi, n:n + 1],
                    inc=True)
                E.mark(f"k16_e_{gi}_{n}")
                E.wait("v", "a", f"k16_e_{gi}_{n}")
                ts("v", en3s[n % 2][:], en2[:], cre_sb[:, gi, n:n + 1], None,
                   ALU.mult, inc=True)
                E.mark(f"k16_m_{gi}_{n}")
                if n >= 1:
                    tt("v", k16t[:], k16t[:], en3s[(n - 1) % 2][:], ALU.add,
                       inc=False)
                if n + 1 < NS:
                    E.wait("a", "v", f"k16_m_{gi}_{n}")
            tt("v", k16t[:], k16t[:], en3s[(NS - 1) % 2][:], ALU.add,
               inc=False)
            VD()
            tt("v", k16t[:, 0:1], k16t[:, 0:1], d_sb[:, gi:gi + 1], ALU.add,
               inc=False)
            VD()
            VD()
            cp("v", k16b[:], k16t[:])
            E.mark(f"k16_b_{gi}")
            E.wait("gq", "v", f"k16_b_{gi}")
            E.dma("gq", "k16s", k16h[gi * 128:(gi + 1) * 128, 128:256],
                  k16b[:])
            E.dma("gq", "k16s", k16h[gi * 128:(gi + 1) * 128, 0:128],
                  zt16[:])
            E.mark(f"k16_st_{gi}")
            E.wait("cq", "k16s", f"k16_st_{gi}")
            for sub in range(8):
                m0 = gi * 128 + sub * 16
                wv = shb[m0:m0 + 16, 0:128 * SHP].rearrange(
                    "ch (s i) -> ch s i", i=SHP)[:, :, 0:256]
                E.dma("cq", "shw", wv, k16h[m0:m0 + 16, :].unsqueeze(1)
                      .broadcast_to((16, 128, 256)))
            E.mark(f"k16_sh_{gi}")

        NGRP = ngrp

        def load_uslices(grp):
            if grp >= 2:
                E.wait("sq", "t", f"ssm_tdone_{grp - 2}")
            m0 = grp * 32
            E.dmat(f"us{grp % 2}",
                   usl[grp % 2][:].rearrange("p a b -> p (a b)"),
                   ulnT[m0:m0 + 32].rearrange("ch b (c s) -> (ch b c) s",
                                              s=128))
            E.mark(f"ssm_usl_{grp}")

        k16_group(0)
        if k16g > 1:
            k16_group(1)
        load_uslices(0)
        if NGRP > 1:
            load_uslices(1)

        def t0view(mch):
            return shb[mch, 128:128 + 128 * 256].rearrange(
                "(s l) -> s l", l=256)[:, 0:128]

        hs4 = hstash.rearrange("p (blk b sl) -> p blk b sl", b=4, sl=17)
        h16_4 = hst16.rearrange("p (blk b sl) -> p blk b sl", b=4, sl=17)
        ss4 = sstash.rearrange("p (blk b c) -> p blk b c", b=4, c=16)
        rrep3 = rrep.rearrange("p (blk b) -> p blk b", b=4)
        stmp3 = scantmp.rearrange("p (blk b) -> p blk b", b=4)

        for grp in range(NGRP):
            m0 = grp * 32
            if grp % 4 == 0 and grp // 4 + 2 < k16g:
                k16_group(grp // 4 + 2)
            for pk in range(4):
                if pk % 2 == 0:
                    pp = f"ssm_pexp_{grp}_{pk - 1}" if pk > 0 else \
                        (f"ssm_pexp_{grp - 1}_3" if grp > 0 else None)
                    if pp is not None:
                        E.wait("cq", "a", pp)
                        E.wait("cq", "v", f"ssm_ppk_{grp}_{pk - 1}" if pk > 0
                               else f"ssm_ppk_{grp - 1}_3")
                    E.dma("cq", "rp0",
                          xt[0][:, 0:1024],
                          dtAh[m0 + pk * 8:m0 + pk * 8 + 16, :]
                          .rearrange("a n -> (a n)").unsqueeze(0)
                          .broadcast_to((128, 1024)))
                    E.dma("cq", "rp1",
                          xt[1][:, 0:1024],
                          c_re[m0 + pk * 8:m0 + pk * 8 + 16, :]
                          .rearrange("a n -> (a n)").unsqueeze(0)
                          .broadcast_to((128, 1024)))
                    E.mark(f"ssm_rep_{grp}_{pk}")
                E.wait("a", "rp0", f"ssm_rep_{grp}_{pk - pk % 2}")
                ppv = f"ssm_ppk_{grp}_{pk - 1}" if pk > 0 else \
                    (f"ssm_ppk_{grp - 1}_3" if grp > 0 else None)
                if ppv is not None:
                    E.wait("a", "v", ppv)
                act(pexp[:], dtarep[pk % 2][:], AF.Exp, scale=revI[:])
                E.mark(f"ssm_pexp_{grp}_{pk}")
                E.wait("v", "a", f"ssm_pexp_{grp}_{pk}")
                smmp = f"ssm_smm_{grp}_{pk - 2}" if pk >= 2 else \
                    (f"ssm_smm_{grp - 1}_{pk + 2}" if grp > 0 else None)
                if smmp is not None:
                    E.wait("v", "t", smmp)
                E.wait("v", "rp1", f"ssm_rep_{grp}_{pk - pk % 2}")
                tt("v", ppk[pk % 2][:], pexp[:], crerep[pk % 2][:], ALU.mult)
                E.mark(f"ssm_ppk_{grp}_{pk}")
                E.wait("t", "v", f"ssm_ppk_{grp}_{pk}")
                E.wait("t", f"us{grp % 2}", f"ssm_usl_{grp}")
                scp = f"ssm_scp_{grp}_{pk - 1}" if pk > 0 else \
                    (f"ssm_scp_{grp - 1}_3" if grp > 0 else None)
                if scp is not None:
                    E.wait("t", "v", scp)
                for ch in range(8):
                    q = pk * 8 + ch
                    mm(pst[:, ch * 64:(ch + 1) * 64],
                       ppk[pk % 2][:, ch * 64:(ch + 1) * 64],
                       usl[grp % 2][:, q, :], start=True, stop=True,
                       inc=(ch == 7))
                E.mark(f"ssm_smm_{grp}_{pk}")
                E.wait("v", "t", f"ssm_smm_{grp}_{pk}")
                cp("v", sstash[:, pk * 512:(pk + 1) * 512], pst[:])
                E.mark(f"ssm_scp_{grp}_{pk}")
            if grp > 0:
                E.wait("v", "t", f"ssm_emm_{grp - 1}_31")
            E.op("v", lambda e: e.memset(hstash[:], 0.0), inc=False)
            for bb in range(4):
                cp("v", rrep3[:, :, bb], rho_sb[:, m0:m0 + 32],
                   inc=(bb == 3))
            VD()
            E.mark(f"ssm_rrep_{grp}")
            for c in range(NCK):
                tt("v", stmp3[:, 0:16, :], hs4[:, 0:16, :, c],
                   rrep3[:, 0:16, :], ALU.mult, inc=False)
                tt("v", stmp3[:, 16:32, :], hs4[:, 16:32, :, c],
                   rrep3[:, 16:32, :], ALU.mult, inc=False)
                tt("v", hs4[:, 0:16, :, c + 1], stmp3[:, 0:16, :],
                   ss4[:, 0:16, :, c], ALU.add, inc=False)
                tt("v", hs4[:, 16:32, :, c + 1], stmp3[:, 16:32, :],
                   ss4[:, 16:32, :, c], ALU.add, inc=(c == NCK - 1))
            VD()
            cp("v", hst16[:], hstash[:])
            E.mark(f"ssm_scan_{grp}")
            for sub in range(4):
                t0sem = ["t0a", "t0b"][sub % 2]
                t0prev = (f"ssm_t0c_{grp}_{(sub - 2) * 8 + 7}" if sub >= 2
                          else (f"ssm_t0c_{grp - 1}_{(sub + 2) * 8 + 7}"
                                if grp > 0 else None))
                if t0prev is not None:
                    E.wait("gq", "t", t0prev)
                E.wait("gq", "shw", f"k16_sh_{(m0 + sub * 8) // 128}")
                E.dma("gq", t0sem, ulnTt[sub % 2][:],
                      shb[m0 + sub * 8:m0 + sub * 8 + 8,
                          128:128 + 128 * 256]
                      .rearrange("ch (s l) -> s ch l", l=256)[:, :, 0:128])
                E.mark(f"ssm_t0_{grp}_{sub}")
                for j in range(8):
                    ch = sub * 8 + j
                    mch = m0 + ch
                    e2p = (f"ssm_emm_{grp}_{ch - 8}" if ch >= 8 else
                           (f"ssm_emm_{grp - 1}_{ch + 24}" if grp > 0
                            else None))
                    if e2p is not None:
                        E.wait("a", "t", e2p)
                    act(e2b[ch % 8][:], iotaL1_64[:], AF.Exp,
                        scale=dtAT_sb[:, mch:mch + 1])
                    E.mark(f"ssm_e2_{grp}_{ch}")
                for j in range(8):
                    ch = sub * 8 + j
                    E.wait("t", t0sem, f"ssm_t0_{grp}_{sub}")
                    if sub == 0 and j == 0:
                        E.wait("t", "v", f"ssm_scan_{grp}")
                        yev = (f"ssm_yev_{grp - 1}" if grp > 0 else None)
                        if yev is not None:
                            E.wait("t", "v", yev)
                    mm(pyb[:, ch * 64:(ch + 1) * 64],
                       ulnTt[sub % 2][:, j, :], usl[grp % 2][:, ch, :],
                       start=(j == 0), stop=False, inc=True)
                    E.mark(f"ssm_t0c_{grp}_{ch}")
                for j in range(8):
                    ch = sub * 8 + j
                    E.wait("t", "a", f"ssm_e2_{grp}_{ch}")
                    rhsH = h16_4[:, ch, :, 0:16]
                    mm(pyb[:, ch * 64:(ch + 1) * 64],
                       e2b[ch % 8][:], rhsH, start=False, stop=(j == 7),
                       inc=True)
                    E.mark(f"ssm_emm_{grp}_{ch}")
            E.wait("v", "t", f"ssm_emm_{grp}_31")
            if grp > 0:
                E.wait("v", "ys", f"ssm_yst_{grp - 1}")
            E.op("v", lambda e: e.tensor_copy(
                out=ystrip2.rearrange("p (j ch) -> p ch j", ch=32)
                if False else
                ystrip2.rearrange("p (j ch) -> p j ch", ch=32)
                .transpose((0, 2, 1)),
                in_=pyb.rearrange("p (ch j) -> p ch j", j=64)))
            E.mark(f"ssm_yev_{grp}")
            E.wait("gq", "v", f"ssm_yev_{grp}")
            E.dma("gq", "ys", ysp[grp].transpose((2, 0, 1, 3))
                  .rearrange("l b c ch -> l (b c) ch"),
                  ystrip2.rearrange("p (j ch) -> p j ch", ch=32))
            E.mark(f"ssm_yst_{grp}")
            E.mark(f"ssm_tdone_{grp}")
            if grp + 2 < NGRP:
                load_uslices(grp + 2)
        E.barrier("ssm")

        for i in range(ntile + 2):
            if i >= 2:
                ffn_back2("f2", i - 2)
            if i < ntile:
                ffn_front("f2", i)
            if 1 <= i <= ntile:
                ffn_back1("f2", i - 1)
            if i < ntile:
                ffn_front_tr("f2", i)
            if i >= 2:
                ffn_back2b("f2", i - 2)
        E.wait("g", "os0", E.cnt["os0"])
        E.wait("g", "os1", E.cnt["os1"])
        E.wait("sq", "os0", E.cnt["os0"])
        E.wait("sq", "os1", E.cnt["os1"])

    marks = {}
    program(Ctx(nc, sems, None, marks))

    with nc.Block() as block:
        @block.sync
        def _(_e):
            program(Ctx(nc, sems, "sq", marks))

        @block.vector
        def _(_e):
            program(Ctx(nc, sems, "v", marks))

        @block.scalar
        def _(_e):
            program(Ctx(nc, sems, "a", marks))

        @block.tensor
        def _(_e):
            program(Ctx(nc, sems, "t", marks))

        @block.gpsimd
        def _(_e):
            program(Ctx(nc, sems, "g", marks))

    return nc


_NC_CACHE = []


def kernel(**inputs):
    ub = np.ascontiguousarray(inputs["L0_attn_u"], dtype=np.float32)
    shards = [ub[i * BC:(i + 1) * BC] for i in range(NCORES)]
    common = {
        "w1": np.ascontiguousarray(inputs["L0_ffn_w1"], dtype=np.float32),
        "b1": np.ascontiguousarray(inputs["L0_ffn_b1"], dtype=np.float32),
        "w2": np.ascontiguousarray(inputs["L0_ffn_w2"], dtype=np.float32),
        "b2": np.ascontiguousarray(inputs["L0_ffn_b2"], dtype=np.float32),
        "a_log": np.ascontiguousarray(inputs["L0_attn_A_log"],
                                      dtype=np.float32),
        "d_vec": np.ascontiguousarray(inputs["L0_attn_D"], dtype=np.float32),
        "c_re": np.ascontiguousarray(inputs["L0_attn_C_re"],
                                     dtype=np.float32),
        "log_dt": np.ascontiguousarray(inputs["L0_attn_log_dt"],
                                       dtype=np.float32),
    }
    if not _NC_CACHE:
        _NC_CACHE.append(build_program())
    nc = _NC_CACHE[0]
    in_maps = [dict(common, u=s) for s in shards]
    res = run_bass_kernel_spmd(nc, in_maps, core_ids=list(range(NCORES)))
    return np.concatenate([r["out"] for r in res.results], axis=0)


if __name__ == "__main__":
    nc = build_program()
    print("built ok")
